# version 08
# baseline (speedup 1.0000x reference)
import numpy as np
import ml_dtypes
import concourse.bass as bass
import concourse.mybir as mybir
from concourse.bass_utils import run_bass_kernel_spmd

F32 = mybir.dt.float32
BF16 = mybir.dt.bfloat16
I8 = mybir.dt.int8
AF = mybir.ActivationFunctionType
ALU = mybir.AluOpType
AX = mybir.AxisListType

D = 2048
H = 16
HD = 128
INTER = 8192
CH = 512
NPOS = 9
TKV = NPOS * CH
TOWN = 1024
NCC = D // 128
NII = INTER // 128
SCALE = 1.0 / float(np.sqrt(HD))
EPS = 1e-5

DEBUG = False
DEBUG_ROWS = False

ENGS = ("sync", "gpsimd", "tensor", "vector", "scalar")
DMA_ENGS = ("sync", "gpsimd")


class Em:

    def __init__(self):
        self.ops = []

    def op(self, eng, fn, waits=()):
        tok = len(self.ops)
        ws = []
        for w in waits:
            if w is None:
                continue
            assert 0 <= w < tok, f"wait on future token {w} at {tok}"
            ws.append(w)
        self.ops.append([eng, fn, ws, False])
        return tok

    def emit(self, block, sems):
        ops = self.ops
        for o in ops:
            if o[0] in DMA_ENGS:
                o[3] = True
        for i, (eng, fn, ws, _) in enumerate(ops):
            for w in ws:
                weng = ops[w][0]
                if weng == eng and weng == "tensor":
                    continue
                ops[w][3] = True
        val_after = [0] * len(ops)
        cnt = {e: 0 for e in ENGS}
        for i, (eng, fn, ws, need) in enumerate(ops):
            if need:
                cnt[eng] += 16 if eng in DMA_ENGS else 1
            val_after[i] = cnt[eng]
        finals = dict(cnt)
        per = {e: [] for e in ENGS}
        for i, (eng, fn, ws, need) in enumerate(ops):
            rw = {}
            for w in ws:
                weng = ops[w][0]
                if weng == eng and weng == "tensor":
                    continue
                v = val_after[w]
                rw[weng] = max(rw.get(weng, 0), v)
            per[eng].append((rw, fn, need))

        def mk(engname):
            def f(engine):
                hi = {e: 0 for e in ENGS}
                for rw, fn, need in per[engname]:
                    for weng, v in rw.items():
                        if v > hi[weng]:
                            engine.wait_ge(sems[weng], v)
                            hi[weng] = v
                    inst = fn(engine)
                    if need:
                        inst.then_inc(sems[engname],
                                      16 if engname in DMA_ENGS else 1)
                for e in ENGS:
                    if finals[e] > hi[e]:
                        engine.wait_ge(sems[e], finals[e])
            return f
        block.sync(mk("sync"))
        block.gpsimd(mk("gpsimd"))
        block.tensor(mk("tensor"))
        block.vector(mk("vector"))
        block.scalar(mk("scalar"))


def build_nc():
    nc = bass.Bass("TRN2", target_bir_lowering=False)

    i_xkvT = nc.dram_tensor("xkvT", [D, TKV], F32, kind="ExternalInput").ap()
    i_xown = nc.dram_tensor("xown", [TOWN, D], F32, kind="ExternalInput").ap()
    i_wq = nc.dram_tensor("wq", [H, D, HD], F32, kind="ExternalInput").ap()
    i_wk = nc.dram_tensor("wk", [H, D, HD], F32, kind="ExternalInput").ap()
    i_wvhd = nc.dram_tensor("wvhd", [D, D], F32, kind="ExternalInput").ap()
    i_wo = nc.dram_tensor("wo", [D, D], F32, kind="ExternalInput").ap()
    i_bqT = nc.dram_tensor("bqT", [128, H], F32, kind="ExternalInput").ap()
    i_bkT = nc.dram_tensor("bkT", [128, H], F32, kind="ExternalInput").ap()
    i_bvr = nc.dram_tensor("bvr", [1, D], F32, kind="ExternalInput").ap()
    i_bor = nc.dram_tensor("bor", [1, D], F32, kind="ExternalInput").ap()
    i_wg = nc.dram_tensor("wg", [D, INTER], I8, kind="ExternalInput").ap()
    i_wu = nc.dram_tensor("wu", [D, INTER], I8, kind="ExternalInput").ap()
    i_wd = nc.dram_tensor("wd", [INTER, D], I8, kind="ExternalInput").ap()
    i_sg = nc.dram_tensor("sgpm", [128, NII], F32, kind="ExternalInput").ap()
    i_su = nc.dram_tensor("supm", [128, NII], F32, kind="ExternalInput").ap()
    i_sdr = nc.dram_tensor("sdr", [1, D], F32, kind="ExternalInput").ap()
    i_bigoff = nc.dram_tensor("bigoff", [128, 2 * NPOS], F32,
                              kind="ExternalInput").ap()
    i_dmask = nc.dram_tensor("dmask", [128, 4 * CH], BF16,
                             kind="ExternalInput").ap()

    o_out = nc.dram_tensor("out", [TOWN, D], F32, kind="ExternalOutput").ap()

    ik = "ExternalOutput" if DEBUG else "Internal"
    d_xnT = nc.dram_tensor("xnT_d", [D, TKV], BF16, kind=ik).ap()
    d_k = nc.dram_tensor("k_d", [D, TKV], BF16, kind=ik).ap()
    d_v = nc.dram_tensor("v_d", [TKV, D], BF16, kind=ik).ap()
    d_g = nc.dram_tensor("g_d", [TOWN, D], BF16, kind=ik).ap()
    if DEBUG:
        o_QT = nc.dram_tensor("dbg_QT", [128, H * TOWN], BF16,
                              kind="ExternalOutput").ap()
        o_yT = nc.dram_tensor("dbg_yT", [128, H * TOWN], BF16,
                              kind="ExternalOutput").ap()
        o_l = nc.dram_tensor("dbg_l", [2 * H, 512], F32,
                             kind="ExternalOutput").ap()
        o_c = nc.dram_tensor("dbg_c", [2 * H, 512], F32,
                             kind="ExternalOutput").ap()
        o_rl = nc.dram_tensor("dbg_rl", [2 * H, 512], F32,
                              kind="ExternalOutput").ap()

    BASE = 16512
    TOPLIM = 229344
    P0 = BASE
    M0 = BASE + 16384
    Z1 = BASE + 45056
    Z2 = BASE + 110592
    Z3 = BASE + 143360
    Z4 = BASE + 176128
    assert Z4 + 32768 <= TOPLIM

    uid = [0]

    def sb(name, shape, dtype, off):
        uid[0] += 1
        return nc.alloc_sbuf_tensor_at(f"{name}_{uid[0]}", shape, dtype,
                                       offset=off)

    bigoff = sb("bigoff", [128, 2 * NPOS], F32, P0 + 0)
    bkT = sb("bkT", [128, H], F32, P0 + 128)
    bqT = sb("bqT", [128, H], F32, P0 + 192)
    sgpm = sb("sgpm", [128, NII], F32, P0 + 256)
    supm = sb("supm", [128, NII], F32, P0 + 512)
    dmask = sb("dmask", [128, 4 * CH], BF16, P0 + 768)
    bvb = sb("bvb", [128, D], F32, P0 + 4864)
    onesb = sb("onesb", [128, 1], BF16, P0 + 13056)
    ones1b = sb("ones1b", [1, 128], BF16, P0 + 13120)
    ones1f = sb("ones1f", [1, 128], F32, P0 + 13376)
    nones1 = sb("nones1", [1, 128], BF16, P0 + 14016)
    maxk2 = sb("maxk2", [1, H], F32, P0 + 13888)
    epsc = sb("epsc", [128, 1], F32, P0 + 13952)

    em = Em()
    op = em.op

    from contextlib import ExitStack
    with ExitStack() as ctx:
        ps = [ctx.enter_context(nc.psum_tensor(f"ps{k}", [128, 512], F32))
              for k in range(8)]
        sems = {e: ctx.enter_context(nc.semaphore(f"S_{e}")) for e in ENGS}
        block = ctx.enter_context(nc.Block())

        ps_last = {k: None for k in range(8)}

        t_bigoff = op("sync", lambda e: e.dma_start(out=bigoff[:], in_=i_bigoff))
        t_bkT = op("sync", lambda e: e.dma_start(out=bkT[:], in_=i_bkT))
        t_bqT = op("sync", lambda e: e.dma_start(out=bqT[:], in_=i_bqT))
        t_sg = op("sync", lambda e: e.dma_start(out=sgpm[:], in_=i_sg))
        t_su = op("sync", lambda e: e.dma_start(out=supm[:], in_=i_su))
        t_dmask = op("sync", lambda e: e.dma_start(out=dmask[:], in_=i_dmask))
        t_ones_b = op("vector", lambda e: e.memset(onesb[:], 1.0))
        t_ones_1b = op("vector", lambda e: e.memset(ones1b[:], 1.0))
        t_ones_1f = op("vector", lambda e: e.memset(ones1f[:], 1.0))
        op("vector", lambda e: e.memset(maxk2[:], 0.0))
        t_eps = op("vector", lambda e: e.memset(epsc[:], float(EPS)))
        t_nones = op("vector", lambda e: e.memset(nones1[:], -1.0))

        bvrow = sb("bvrow", [1, D], F32, M0)
        tv = op("sync", lambda e: e.dma_start(out=bvrow[:], in_=i_bvr))
        tok_bvb = None
        bv_mm_last = None
        for ec in range(4):
            mmb = op("tensor", lambda e, ec=ec: e.matmul(
                ps[7][:], ones1f[:], bvrow[0:1, ec * 512:(ec + 1) * 512],
                start=True, stop=True), waits=[tv, t_ones_1f, ps_last[7]])
            bv_mm_last = mmb
            tok_bvb = op("vector", lambda e, ec=ec: e.tensor_copy(
                bvb[:, ec * 512:(ec + 1) * 512], ps[7][:]), waits=[mmb])
            ps_last[7] = tok_bvb

        xbcc = sb("xbcc", [128, 1024], BF16, M0 + 8192)
        sqcc = sb("sqcc", [128, 1024], BF16, M0 + 10240)
        r_m = sb("r_m", [1, 512], F32, M0 + 12288)
        r_msq = sb("r_msq", [1, 512], F32, M0 + 14336)
        r_var = sb("r_var", [1, 512], F32, M0 + 16384)
        r_std = sb("r_std", [1, 512], F32, M0 + 18432)
        r_rstd = sb("r_rstd", [1, 512], F32, M0 + 20480)
        r_mr = sb("r_mr", [1, 512], F32, M0 + 22528)
        rstdb = sb("rstdb", [128, 512], F32, M0 + 24576)
        mrb = sb("mrb", [128, 512], F32, M0 + 26624)
        tmp1 = sb("tmp1", [128, 1024], F32, M0 + 0)

        xT = [sb("xT0", [128, NCC * 512], F32, Z1),
              sb("xT1", [128, NCC * 512], F32, Z1 + 32768)]
        xnc = [sb("xnc0", [128, NCC * 512], BF16, Z2),
               sb("xnc1", [128, NCC * 512], BF16, Z2 + 16384)]
        xnT_own = sb("xnT_own", [128, NCC * TOWN], BF16, Z3)

        xT_cons = [None, None]
        xnc_w = [None, None]
        xb_cons = [None, None]
        sq_cons = [None, None]
        t1_cons = [None, None]
        st_xnT = []
        own_copy_tok = []
        tmp1_guard = bv_mm_last

        for tc in range(NPOS):
            p = tc % 2
            ld = op("sync", lambda e, p=p, tc=tc: e.dma_start(
                out=xT[p][:].rearrange("p (cc t) -> p cc t", cc=NCC),
                in_=i_xkvT[:, tc * 512:(tc + 1) * 512]
                .rearrange("(cc cin) t -> cin cc t", cin=128)),
                waits=[xT_cons[p]])
            mmS = mmQ = None
            for cc in range(NCC):
                q = cc % 2
                cast = op("scalar", lambda e, p=p, cc=cc, q=q: e.copy(
                    xbcc[:, q * 512:(q + 1) * 512],
                    xT[p][:, cc * 512:(cc + 1) * 512]),
                    waits=[ld, xb_cons[q]])
                sq = op("scalar", lambda e, q=q: e.activation(
                    sqcc[:, q * 512:(q + 1) * 512],
                    xbcc[:, q * 512:(q + 1) * 512], AF.Square),
                    waits=[cast, sq_cons[q]])
                mmS = op("tensor", lambda e, q=q, cc=cc: e.matmul(
                    ps[6][0:1, :], onesb[:], xbcc[:, q * 512:(q + 1) * 512],
                    start=(cc == 0), stop=(cc == NCC - 1)),
                    waits=[cast, t_ones_b] + ([ps_last[6]] if cc == 0 else []))
                mmQ = op("tensor", lambda e, q=q, cc=cc: e.matmul(
                    ps[5][0:1, :], onesb[:], sqcc[:, q * 512:(q + 1) * 512],
                    start=(cc == 0), stop=(cc == NCC - 1)),
                    waits=[sq] + ([ps_last[5]] if cc == 0 else []))
                xb_cons[q] = mmS
                sq_cons[q] = mmQ
            v1 = op("vector", lambda e: e.tensor_scalar_mul(
                r_m[:], ps[6][0:1, :], 1.0 / D), waits=[mmS])
            v2 = op("vector", lambda e: e.tensor_scalar_mul(
                r_msq[:], ps[5][0:1, :], 1.0 / D), waits=[mmQ])
            ps_last[6] = v1
            ps_last[5] = v2
            v3 = op("vector", lambda e: e.tensor_mul(
                r_var[:], r_m[:], r_m[:]), waits=[v1])
            v4 = op("vector", lambda e: e.tensor_sub(
                r_var[:], r_msq[:], r_var[:]), waits=[v2, v3])
            a1 = op("scalar", lambda e: e.activation(
                r_std[:], r_var[:], AF.Sqrt, bias=epsc[0:1, 0:1], scale=1.0),
                waits=[v4, t_eps])
            v5 = op("vector", lambda e: e.reciprocal(
                r_rstd[:], r_std[:]), waits=[a1])
            v6 = op("vector", lambda e: e.tensor_mul(
                r_mr[:], r_m[:], r_rstd[:]), waits=[v5, v1])
            mB1 = op("tensor", lambda e: e.matmul(
                ps[4][:], ones1f[:], r_rstd[:], start=True, stop=True),
                waits=[v5, t_ones_1f, ps_last[4]])
            vB1 = op("vector", lambda e: e.tensor_copy(rstdb[:], ps[4][:]),
                     waits=[mB1])
            ps_last[4] = vB1
            mB2 = op("tensor", lambda e: e.matmul(
                ps[3][:], ones1f[:], r_mr[:], start=True, stop=True),
                waits=[v6, ps_last[3]])
            vB2 = op("vector", lambda e: e.tensor_copy(mrb[:], ps[3][:]),
                     waits=[mB2])
            ps_last[3] = vB2
            last_apply = None
            for cc in range(NCC):
                q = cc % 2
                va = op("vector", lambda e, p=p, cc=cc, q=q: e.tensor_mul(
                    tmp1[:, q * 512:(q + 1) * 512],
                    xT[p][:, cc * 512:(cc + 1) * 512], rstdb[:]),
                    waits=[vB1, t1_cons[q], tmp1_guard,
                           xnc_w[p] if cc == 0 else None])
                vb = op("vector", lambda e, p=p, cc=cc, q=q: e.tensor_sub(
                    xnc[p][:, cc * 512:(cc + 1) * 512],
                    tmp1[:, q * 512:(q + 1) * 512], mrb[:]),
                    waits=[vB2, va])
                t1_cons[q] = vb
                last_apply = vb
            tmp1_guard = None
            xT_cons[p] = last_apply
            deps = [last_apply]
            if tc < 2:
                oc = op("vector", lambda e, p=p, tc=tc: e.tensor_copy(
                    xnT_own[:].rearrange("p (cc t) -> p cc t", cc=NCC)
                    [:, :, tc * 512:(tc + 1) * 512],
                    xnc[p][:].rearrange("p (cc t) -> p cc t", cc=NCC)),
                    waits=[last_apply])
                own_copy_tok.append(oc)
                deps.append(oc)
            st = op("sync", lambda e, p=p, tc=tc: e.dma_start(
                out=d_xnT.rearrange("(cc cin) t -> cin cc t", cin=128)
                [:, :, tc * 512:(tc + 1) * 512],
                in_=xnc[p][:].rearrange("p (cc t) -> p cc t", cc=NCC)),
                waits=deps)
            st_xnT.append(st)
            xnc_w[p] = st

        kbuf = sb("kbuf", [128, 1024], BF16, M0 + 8192)
        sqb = sb("sqb", [128, 1024], BF16, M0 + 10240)
        r_nrm = sb("r_nrm", [1, 512], F32, M0 + 12288)

        wk_sb = sb("wk", [128, H * NCC * 128], BF16, Z1)
        ld_wk = None
        for hq in range(4):
            ld_wk = op("gpsimd", lambda e, hq=hq: e.dma_start(
                out=wk_sb[:].rearrange("p (h cc d) -> p h cc d",
                                       h=H, cc=NCC)[:, hq * 4:(hq + 1) * 4],
                in_=i_wk.rearrange("h (cc cin) d -> cin h cc d", cin=128)
                [:, hq * 4:(hq + 1) * 4]),
                waits=[xT_cons[0], xT_cons[1]] if hq == 0 else [])

        xnc2_cons = [None, None]
        kb_cons = [[], []]
        st_k_last = None
        nrm_tok = None
        ph1_m_guard = [t1_cons[0], t1_cons[1]]
        for tc in range(NPOS):
            p = tc % 2
            ld = op("sync", lambda e, p=p, tc=tc: e.dma_start(
                out=xnc[p][:].rearrange("p (cc t) -> p cc t", cc=NCC),
                in_=d_xnT.rearrange("(cc cin) t -> cin cc t", cin=128)
                [:, :, tc * 512:(tc + 1) * 512]),
                waits=[st_xnT[tc], st_xnT[-1], st_xnT[-2], xnc2_cons[p]])
            last_mm_chunk = None

            def norm_block(h):
                nonlocal nrm_tok
                kp = h % 2
                sqo = op("vector", lambda e, kp=kp: e.tensor_mul(
                    sqb[:, kp * 512:(kp + 1) * 512],
                    kbuf[:, kp * 512:(kp + 1) * 512],
                    kbuf[:, kp * 512:(kp + 1) * 512]),
                    waits=[ev_toks[kp], nrm_tok])
                mmn = op("tensor", lambda e, kp=kp: e.matmul(
                    ps[6][0:1, :], onesb[:],
                    sqb[:, kp * 512:(kp + 1) * 512], start=True, stop=True),
                    waits=[sqo, ps_last[6]])
                prev_nrm = nrm_tok
                mx1 = op("vector", lambda e: e.reduce_max(
                    r_nrm[0:1, 0:1], ps[6][0:1, :], AX.X), waits=[mmn])
                ps_last[6] = mx1
                nrm_tok = op("vector", lambda e, h=h: e.tensor_max(
                    maxk2[0:1, h:h + 1], maxk2[0:1, h:h + 1],
                    r_nrm[0:1, 0:1]), waits=[mx1, prev_nrm])
                kb_cons[h % 2].append(mmn)

            ev_toks = [None, None]
            for h in range(H):
                kp = h % 2
                bank = kp
                mm = None
                for cc in range(NCC):
                    mm = op("tensor",
                            lambda e, p=p, h=h, cc=cc, bank=bank: e.matmul(
                                ps[bank][:],
                                wk_sb[:, (h * NCC + cc) * 128:
                                      (h * NCC + cc + 1) * 128],
                                xnc[p][:, cc * 512:(cc + 1) * 512],
                                start=(cc == 0), stop=(cc == NCC - 1)),
                            waits=[ld, ld_wk, ps_last[bank]]
                            if cc == 0 else [])
                last_mm_chunk = mm
                ev = op("vector",
                        lambda e, h=h, kp=kp, bank=bank: e.tensor_single_scalar(
                            kbuf[:, kp * 512:(kp + 1) * 512], ps[bank][:],
                            bkT[:, h:h + 1], ALU.add),
                        waits=[mm, t_bkT] + kb_cons[kp] + ph1_m_guard)
                ph1_m_guard = []
                ev_toks[kp] = ev
                ps_last[bank] = ev
                stk = op("sync", lambda e, h=h, tc=tc, kp=kp: e.dma_start(
                    out=d_k[h * 128:(h + 1) * 128, tc * 512:(tc + 1) * 512],
                    in_=kbuf[:, kp * 512:(kp + 1) * 512]),
                    waits=[ev])
                st_k_last = stk
                kb_cons[kp] = [stk]
                if h > 0:
                    norm_block(h - 1)
            norm_block(H - 1)
            xnc2_cons[p] = last_mm_chunk

        vstage = sb("vstage", [128, 1024], BF16, M0 + 14336)
        wv_sb = sb("wv", [128, NCC * D], BF16, Z1)
        ld_wv = op("gpsimd", lambda e: e.dma_start(
            out=wv_sb[:].rearrange("p (cc e) -> p cc e", cc=NCC),
            in_=i_wvhd.rearrange("(cc cin) e -> cin cc e", cin=128)),
            waits=[xnc2_cons[0], xnc2_cons[1], ld_wk])

        vs_cons = [[], []]
        st_v_last = None
        xncv_cons = [None, None]
        for tc in range(NPOS):
            p = tc % 2
            ld = op("sync", lambda e, p=p, tc=tc: e.dma_start(
                out=xnc[p][:].rearrange("p (cc t) -> p cc t", cc=NCC),
                in_=d_xnT.rearrange("(cc cin) t -> cin cc t", cin=128)
                [:, :, tc * 512:(tc + 1) * 512]),
                waits=[st_xnT[tc], xncv_cons[p],
                       xnc2_cons[p] if tc < 2 else None])
            last_mm_chunk = None
            for hg in range(4):
                for tt in range(4):
                    vp = (hg * 4 + tt) % 2
                    bank = vp
                    mm = None
                    for cc in range(NCC):
                        mm = op("tensor",
                                lambda e, p=p, hg=hg, tt=tt, cc=cc, bank=bank:
                                e.matmul(
                                    ps[bank][:],
                                    xnc[p][:, cc * 512 + tt * 128:
                                           cc * 512 + (tt + 1) * 128],
                                    wv_sb[:, cc * D + hg * 512:
                                          cc * D + (hg + 1) * 512],
                                    start=(cc == 0), stop=(cc == NCC - 1)),
                                waits=[ld, ld_wv, ps_last[bank]]
                                if cc == 0 else [])
                    last_mm_chunk = mm
                    ev = op("vector",
                            lambda e, hg=hg, vp=vp, bank=bank: e.tensor_add(
                                vstage[:, vp * 512:(vp + 1) * 512],
                                ps[bank][:],
                                bvb[:, hg * 512:(hg + 1) * 512]),
                            waits=[mm, tok_bvb] + vs_cons[vp])
                    ps_last[bank] = ev
                    stv = op("sync",
                             lambda e, tc=tc, tt=tt, hg=hg, vp=vp: e.dma_start(
                                 out=d_v[tc * 512 + tt * 128:
                                         tc * 512 + (tt + 1) * 128,
                                         hg * 512:(hg + 1) * 512],
                                 in_=vstage[:, vp * 512:(vp + 1) * 512]),
                             waits=[ev])
                    st_v_last = stv
                    vs_cons[vp] = [stv]
            xncv_cons[p] = last_mm_chunk

        wq_sb = sb("wq", [128, H * NCC * 128], BF16, Z1)
        ld_wq = None
        for hq in range(4):
            ld_wq = op("gpsimd", lambda e, hq=hq: e.dma_start(
                out=wq_sb[:].rearrange("p (h cc d) -> p h cc d",
                                       h=H, cc=NCC)[:, hq * 4:(hq + 1) * 4],
                in_=i_wq.rearrange("h (cc cin) d -> cin h cc d", cin=128)
                [:, hq * 4:(hq + 1) * 4]),
                waits=[xncv_cons[0], xncv_cons[1], ld_wv] if hq == 0 else [])
        QT = sb("QT", [128, H * TOWN], BF16, Z4)
        q_ev_last = None
        q_mm_last = None
        for h in range(H):
            for sl in range(2):
                qp = (h * 2 + sl) % 2
                bank = qp
                mm = None
                for cc in range(NCC):
                    mm = op("tensor",
                            lambda e, h=h, sl=sl, cc=cc, bank=bank: e.matmul(
                                ps[bank][:],
                                wq_sb[:, (h * NCC + cc) * 128:
                                      (h * NCC + cc + 1) * 128],
                                xnT_own[:, cc * TOWN + sl * 512:
                                        cc * TOWN + (sl + 1) * 512],
                                start=(cc == 0), stop=(cc == NCC - 1)),
                            waits=([ld_wq, ps_last[bank]] + own_copy_tok)
                            if cc == 0 else [])
                q_mm_last = mm
                q_ev_last = op("vector",
                               lambda e, h=h, sl=sl, bank=bank: e.tensor_scalar(
                                   QT[:, h * TOWN + sl * 512:
                                      h * TOWN + (sl + 1) * 512],
                                   ps[bank][:], bqT[:, h:h + 1], float(SCALE),
                                   ALU.add, ALU.mult),
                               waits=[mm, t_bqT])
                ps_last[bank] = q_ev_last

        sd_row = sb("sd_row", [1, D], F32, M0 + 0)
        sdownb = sb("sdownb", [128, D], F32, M0 + 8192)
        gsil = sb("gsil", [128, 1024], BF16, M0 + 16384)
        tmp3 = sb("tmp3", [128, 1024], F32, M0 + 18432)

        tsd = op("sync", lambda e: e.dma_start(out=sd_row[:], in_=i_sdr),
                 waits=[t1_cons[0], t1_cons[1]])
        sd_toks = []
        for ec in range(4):
            mmb = op("tensor", lambda e, ec=ec: e.matmul(
                ps[7][:], ones1f[:], sd_row[0:1, ec * 512:(ec + 1) * 512],
                start=True, stop=True), waits=[tsd, ps_last[7]])
            tok = op("vector", lambda e, ec=ec: e.tensor_copy(
                sdownb[:, ec * 512:(ec + 1) * 512], ps[7][:]),
                waits=[mmb] + kb_cons[0] + kb_cons[1])
            ps_last[7] = tok
            sd_toks.append(tok)

        fuseT = sb("fuseT", [128, NII * 512], BF16, Z1)
        slab = [sb("slab0", [128, NCC * 512], BF16, Z2),
                sb("slab1", [128, NCC * 512], BF16, Z2 + 16384)]
        fuse_toks = [None] * NII
        st_g_last = None
        ds0 = ds1 = None
        gs_cons = [[], []]
        t3_cons = [None, None]
        for fc in range(2):
            gate_prev, up_prev = ds0, ds1
            for grp in range(NII // 4):
                extra_g = [xncv_cons[0], xncv_cons[1]] if (fc == 0 and grp == 0) else []
                ld_g = op("gpsimd", lambda e, grp=grp: e.dma_start(
                    out=slab[0][:].rearrange("p (cc i) -> p cc i", cc=NCC),
                    in_=i_wg.rearrange("(cc cin) i -> cin cc i", cin=128)
                    [:, :, grp * 512:(grp + 1) * 512]),
                    waits=[gate_prev] + extra_g)
                ld_u = op("gpsimd", lambda e, grp=grp: e.dma_start(
                    out=slab[1][:].rearrange("p (cc i) -> p cc i", cc=NCC),
                    in_=i_wu.rearrange("(cc cin) i -> cin cc i", cin=128)
                    [:, :, grp * 512:(grp + 1) * 512]),
                    waits=[up_prev] + extra_g)
                for isub in range(4):
                    ii = grp * 4 + isub
                    gp = ii % 2
                    bank = gp
                    bank2 = 2 + gp
                    mm = None
                    for cc in range(NCC):
                        mm = op("tensor",
                                lambda e, isub=isub, cc=cc, fc=fc, bank=bank:
                                e.matmul(
                                    ps[bank][:],
                                    slab[0][:, cc * 512 + isub * 128:
                                            cc * 512 + (isub + 1) * 128],
                                    xnT_own[:, cc * TOWN + fc * 512:
                                            cc * TOWN + (fc + 1) * 512],
                                    start=(cc == 0), stop=(cc == NCC - 1)),
                                waits=[ld_g, ps_last[bank], q_mm_last]
                                if cc == 0 else [])
                    gate_prev = mm
                    sil = op("scalar",
                             lambda e, ii=ii, gp=gp, bank=bank: e.activation(
                                 gsil[:, gp * 512:(gp + 1) * 512], ps[bank][:],
                                 AF.Silu, scale=sgpm[:, ii:ii + 1]),
                             waits=[mm, t_sg] + gs_cons[gp])
                    ps_last[bank] = sil
                    mmu = None
                    for cc in range(NCC):
                        mmu = op("tensor",
                                 lambda e, isub=isub, cc=cc, fc=fc, bank2=bank2:
                                 e.matmul(
                                     ps[bank2][:],
                                     slab[1][:, cc * 512 + isub * 128:
                                             cc * 512 + (isub + 1) * 128],
                                     xnT_own[:, cc * TOWN + fc * 512:
                                             cc * TOWN + (fc + 1) * 512],
                                     start=(cc == 0), stop=(cc == NCC - 1)),
                                 waits=[ld_u, ps_last[bank2]]
                                 if cc == 0 else [])
                    up_prev = mmu
                    fu = op("vector",
                            lambda e, ii=ii, gp=gp, bank2=bank2:
                            e.scalar_tensor_tensor(
                                fuseT[:, ii * 512:(ii + 1) * 512],
                                ps[bank2][:], supm[:, ii:ii + 1],
                                gsil[:, gp * 512:(gp + 1) * 512],
                                ALU.mult, ALU.mult),
                            waits=[mmu, sil, t_su, fuse_toks[ii]])
                    ps_last[bank2] = fu
                    gs_cons[gp] = [fu]
                    fuse_toks[ii] = fu
            ds = [gate_prev, up_prev]
            for ec in range(4):
                mm_last = None
                for half in range(4):
                    dp = half % 2
                    ldd = op("gpsimd",
                             lambda e, half=half, ec=ec, dp=dp: e.dma_start(
                                 out=slab[dp][:].rearrange(
                                     "p (k i) -> p k i", k=NCC),
                                 in_=i_wd.rearrange(
                                     "(ii iin) e -> iin ii e", iin=128)
                                 [:, half * 16:(half + 1) * 16,
                                  ec * 512:(ec + 1) * 512]),
                             waits=[ds[dp]])
                    for k in range(16):
                        ii = half * 16 + k
                        for tt in range(4):
                            bank = 4 + tt
                            w = []
                            if k == 0 and tt == 0:
                                w.append(ldd)
                            if tt == 0:
                                w.append(fuse_toks[ii])
                            if ii == 0:
                                w.append(ps_last[bank])
                            mm_last = op(
                                "tensor",
                                lambda e, ii=ii, k=k, tt=tt, dp=dp, bank=bank:
                                e.matmul(
                                    ps[bank][:],
                                    fuseT[:, ii * 512 + tt * 128:
                                          ii * 512 + (tt + 1) * 128],
                                    slab[dp][:, k * 512:(k + 1) * 512],
                                    start=(ii == 0), stop=(ii == NII - 1)),
                                waits=w)
                    ds[dp] = mm_last
                for tt in range(4):
                    bank = 4 + tt
                    tp = tt % 2
                    vm = op("vector",
                            lambda e, tt=tt, ec=ec, tp=tp, bank=bank:
                            e.tensor_mul(
                                tmp3[:, tp * 512:(tp + 1) * 512], ps[bank][:],
                                sdownb[:, ec * 512:(ec + 1) * 512]),
                            waits=[mm_last, t3_cons[tp]])
                    ps_last[bank] = vm
                    si = op("scalar", lambda e, tp=tp: e.activation(
                        gsil[:, tp * 512:(tp + 1) * 512],
                        tmp3[:, tp * 512:(tp + 1) * 512], AF.Silu),
                        waits=[vm] + gs_cons[tp])
                    stg = op("sync",
                             lambda e, fc=fc, tt=tt, ec=ec, tp=tp: e.dma_start(
                                 out=d_g[fc * 512 + tt * 128:
                                         fc * 512 + (tt + 1) * 128,
                                         ec * 512:(ec + 1) * 512],
                                 in_=gsil[:, tp * 512:(tp + 1) * 512]),
                             waits=[si])
                    st_g_last = stg
                    gs_cons[tp] = [stg]
                    t3_cons[tp] = si
            ds0, ds1 = ds[0], ds[1]

        pbuf = sb("pbuf", [128, 8 * 512], BF16, M0 + 0)
        rlb = sb("rlb", [128, 512], F32, M0 + 8192)
        sqb5 = sb("sqb5", [128, 512], BF16, M0 + 10240)
        R1 = sb("R1", [128, 512], F32, M0 + 11264)
        R2 = sb("R2", [128, 512], F32, M0 + 13312)
        C1 = sb("C1", [1, 512], BF16, M0 + 15360)
        C2 = sb("C2", [1, 512], BF16, M0 + 16384)
        r_c = R1[32:33, :]
        r_ln = R1[64:65, :]
        r_rl = R2[0:1, :]

        ktb = [sb("kt0", [128, TKV], BF16, Z1),
               sb("kt1", [128, TKV], BF16, Z1 + TKV * 2)]
        vtb = [sb("vt0", [128, TKV], BF16, Z1 + TKV * 4),
               sb("vt1", [128, TKV], BF16, Z1 + TKV * 6)]
        yT = sb("yT", [128, H * TOWN], BF16, Z3)

        wos = [sb("wos0", [128, NCC * 512], BF16, Z2),
               sb("wos1", [128, NCC * 512], BF16, Z2 + 16384)]

        def ld_wo_op(ec, waits):
            return op("gpsimd", lambda e, ec=ec: e.dma_start(
                out=wos[ec % 2][:].rearrange("p (dc e) -> p dc e", dc=NCC),
                in_=i_wo.rearrange("(dc din) e -> din dc e", din=128)
                [:, :, ec * 512:(ec + 1) * 512]), waits=waits)

        wo_tok = [None] * 4
        wo_tok[0] = ld_wo_op(0, [ds0, ds1])
        wo_tok[1] = ld_wo_op(1, [ds0, ds1])

        kt_cons = [None, None]
        vt_cons = [None, None]
        pb_cons = [None] * 8
        t5_cons = [None] * 3
        C_cons = [None, None]
        sq5_cons = None
        rlb_cons = st_g_last
        mrb_prev = None
        y_done = None
        DELAY = 6
        NPB = 8
        NT5 = 3
        sbank = (0, 1, 4, 5)
        npair = NPOS * 4

        def lb_of(h, sl):
            return 3 if (h * 2 + sl) % 2 == 0 else 7

        def c_chain(h, sl):
            nonlocal sq5_cons
            lb = lb_of(h, sl)
            crow = C1 if (h * 2 + sl) % 2 == 0 else C2
            sqo = op("vector", lambda e, h=h, sl=sl: e.tensor_mul(
                sqb5[:],
                QT[:, h * TOWN + sl * 512: h * TOWN + (sl + 1) * 512],
                QT[:, h * TOWN + sl * 512: h * TOWN + (sl + 1) * 512]),
                waits=[q_ev_last, sq5_cons, st_v_last, st_g_last])
            mmc = op("tensor", lambda e, lb=lb: e.matmul(
                ps[lb][0:1, :], onesb[:], sqb5[:], start=True, stop=True),
                waits=[sqo, ps_last[lb]])
            sq5_cons = mmc
            vc1 = op("vector", lambda e, h=h, lb=lb: e.tensor_scalar_mul(
                r_c, ps[lb][0:1, :], maxk2[0:1, h:h + 1]),
                waits=[mmc, st_g_last])
            acl = op("scalar", lambda e: e.activation(
                r_ln, r_c, AF.Ln), waits=[vc1])
            ac = op("scalar", lambda e, crow=crow: e.activation(
                crow[0:1, :], r_ln, AF.Exp, scale=0.5),
                waits=[acl, C_cons[(h * 2 + sl) % 2]])
            return (crow, ac)

        pend = {(0, 0): c_chain(0, 0)}
        first5 = [st_k_last, st_v_last, st_g_last, ds0, ds1]
        for h in range(H):
            kp = h % 2
            ldk = op("sync", lambda e, h=h, kp=kp: e.dma_start(
                out=ktb[kp][:], in_=d_k[h * 128:(h + 1) * 128, :]),
                waits=[kt_cons[kp]] + (first5 if h < 2 else []))
            ldvs = []
            for pos in range(NPOS):
                ldvs.append(op("sync",
                               lambda e, h=h, kp=kp, pos=pos: e.dma_start(
                                   out=vtb[kp][:].rearrange(
                                       "p (pos tl d) -> p pos tl d",
                                       pos=NPOS, tl=4)[:, pos],
                                   in_=d_v[pos * 512:(pos + 1) * 512,
                                           h * 128:(h + 1) * 128]
                                   .rearrange("(tl s) d -> s tl d", s=128)),
                               waits=[vt_cons[kp]] if pos == 0 else []))
            for sl in range(2):
                yb = 2 if (h * 2 + sl) % 2 == 0 else 6
                lb = lb_of(h, sl)
                crow, vcb = pend.pop((h, sl))
                mmy_last = None
                mml_last = None

                def consume(j, first, last):
                    nonlocal mml_last, mmy_last
                    pb = j % NPB
                    pt = pt_toks[j]
                    mml_last = op("tensor",
                                  lambda e, j=j, pb=pb, lb=lb, first=first,
                                  last=last: e.matmul(
                                      ps[lb][0:1, :], onesb[:],
                                      pbuf[:, pb * 512:(pb + 1) * 512],
                                      start=first, stop=last),
                                  waits=[pt] + ([vcb] if first else []))
                    mmy_last = op("tensor",
                                  lambda e, j=j, pb=pb, kp=kp, yb=yb,
                                  first=first, last=last: e.matmul(
                                      ps[yb][:],
                                      vtb[kp][:, j * 128:(j + 1) * 128],
                                      pbuf[:, pb * 512:(pb + 1) * 512],
                                      start=first, stop=last),
                                  waits=[ldvs[j // 4]] + ([ps_last[yb]]
                                                          if first else []))
                    pb_cons[pb] = mmy_last

                pt_toks = {}
                for pos in range(NPOS):
                    for st in range(4):
                        j = pos * 4 + st
                        pb = j % NPB
                        bank = sbank[j % 4]
                        mms = op("tensor",
                                 lambda e, h=h, sl=sl, pos=pos, st=st, kp=kp,
                                 bank=bank: e.matmul(
                                     ps[bank][:],
                                     ktb[kp][:, pos * 512 + st * 128:
                                             pos * 512 + (st + 1) * 128],
                                     QT[:, h * TOWN + sl * 512:
                                        h * TOWN + (sl + 1) * 512],
                                     start=True, stop=False),
                                 waits=[ldk, ps_last[bank]])
                        mmn2 = op("tensor",
                                  lambda e, bank=bank, crow=crow: e.matmul(
                                      ps[bank][:], nones1[:], crow[0:1, :],
                                      start=False, stop=True),
                                  waits=[vcb, t_nones])
                        ex = op("scalar",
                                lambda e, pb=pb, sl=sl, pos=pos, bank=bank:
                                e.activation(
                                    pbuf[:, pb * 512:(pb + 1) * 512],
                                    ps[bank][:], AF.Exp,
                                    bias=bigoff[:, sl * NPOS + pos:
                                                sl * NPOS + pos + 1]),
                                waits=[mmn2, t_bigoff, pb_cons[pb]])
                        ps_last[bank] = ex
                        ex_last = ex
                        pt = ex
                        if pos == sl:
                            pt = op("vector",
                                    lambda e, pb=pb, st=st: e.tensor_mul(
                                        pbuf[:, pb * 512:(pb + 1) * 512],
                                        pbuf[:, pb * 512:(pb + 1) * 512],
                                        dmask[:, st * 512:(st + 1) * 512]),
                                    waits=[ex, t_dmask])
                        pt_toks[j] = pt
                        mmn_last_for_c = mmn2
                        if j >= DELAY:
                            consume(j - DELAY, j - DELAY == 0, False)
                        if j == 20:
                            nxt = (h, 1) if sl == 0 else (h + 1, 0)
                            if nxt[0] < H:
                                pend[nxt] = c_chain(*nxt)
                for j in range(npair - DELAY, npair):
                    consume(j, False, j == npair - 1)
                ps_last[lb] = mml_last
                if DEBUG and DEBUG_ROWS:
                    vl = op("vector", lambda e, lb=lb: e.tensor_copy(
                        R2[32:33, :], ps[lb][0:1, :]), waits=[mml_last])
                    ps_last[lb] = vl
                    op("sync", lambda e, h=h, sl=sl: e.dma_start(
                        out=o_l[h * 2 + sl: h * 2 + sl + 1, :],
                        in_=R2[32:33, :]), waits=[vl])
                    op("sync", lambda e, h=h, sl=sl, crow=crow: e.dma_start(
                        out=o_c[h * 2 + sl: h * 2 + sl + 1, :],
                        in_=crow[0:1, :]), waits=[])
                vrec = op("vector", lambda e, lb=lb: e.reciprocal(
                    r_rl, ps[lb][0:1, :]), waits=[mml_last, mrb_prev])
                ps_last[lb] = vrec
                bb = sbank[3]
                mrb5 = op("tensor", lambda e, bb=bb: e.matmul(
                    ps[bb][:], ones1f[:], r_rl, start=True, stop=True),
                    waits=[vrec, ps_last[bb]])
                mrb_prev = mrb5
                vrb = op("vector", lambda e, bb=bb: e.tensor_copy(
                    rlb[:], ps[bb][:]), waits=[mrb5, rlb_cons])
                ps_last[bb] = vrb
                if DEBUG and DEBUG_ROWS:
                    op("sync", lambda e, h=h, sl=sl: e.dma_start(
                        out=o_rl[h * 2 + sl: h * 2 + sl + 1, :],
                        in_=rlb[0:1, :]), waits=[vrb])
                y_done = op("vector",
                            lambda e, h=h, sl=sl, yb=yb:
                            e.scalar_tensor_tensor(
                                yT[:, h * TOWN + sl * 512:
                                   h * TOWN + (sl + 1) * 512],
                                ps[yb][:], 1.0, rlb[:], ALU.mult, ALU.mult),
                            waits=[mmy_last, vrb])
                ps_last[yb] = y_done
                rlb_cons = y_done
                C_cons[(h * 2 + sl) % 2] = mmn_last_for_c
            kt_cons[kp] = mmy_last
            vt_cons[kp] = mmy_last

        tmp6 = sb("tmp6", [128, 1024], F32, M0 + 0)
        xtile = sb("xtile", [128, 1024], F32, M0 + 4096)
        gtile = sb("gtile", [128, 1024], F32, M0 + 8192)
        bob = sb("bob", [128, D], F32, M0 + 12288)
        bo_row = sb("bo_row", [1, D], F32, M0 + 20480)
        t5g = ex_last

        tbo = op("sync", lambda e: e.dma_start(out=bo_row[:], in_=i_bor),
                 waits=[y_done, t5g])
        bob_toks = []
        for ec in range(4):
            mmb = op("tensor", lambda e, ec=ec: e.matmul(
                ps[6][:], ones1f[:], bo_row[0:1, ec * 512:(ec + 1) * 512],
                start=True, stop=True), waits=[tbo, ps_last[6]])
            tok = op("vector", lambda e, ec=ec: e.tensor_copy(
                bob[:, ec * 512:(ec + 1) * 512], ps[6][:]),
                waits=[mmb, y_done, t5g])
            ps_last[6] = tok
            bob_toks.append(tok)

        t6_cons = [None, None]
        xt_cons = [y_done, y_done]
        gt_cons = [y_done, y_done]
        wo_rd = [None, None]
        st_last = None
        for ec in range(4):
            if ec >= 2:
                wo_tok[ec] = ld_wo_op(ec, [wo_rd[ec % 2]])
            for tt in range(8):
                opar = tt % 2
                sl, tl = tt // 4, tt % 4
                bank = opar
                mm = None
                for hh in range(H):
                    mm = op("tensor",
                            lambda e, hh=hh, sl=sl, tl=tl, ec=ec, bank=bank:
                            e.matmul(
                                ps[bank][:],
                                yT[:, hh * TOWN + sl * 512 + tl * 128:
                                   hh * TOWN + sl * 512 + (tl + 1) * 128],
                                wos[ec % 2][:, hh * 512:(hh + 1) * 512],
                                start=(hh == 0), stop=(hh == H - 1)),
                            waits=[wo_tok[ec], y_done, ps_last[bank]]
                            if hh == 0 else [])
                wo_rd[ec % 2] = mm
                ldx = op("sync", lambda e, tt=tt, ec=ec, opar=opar: e.dma_start(
                    out=xtile[:, opar * 512:(opar + 1) * 512],
                    in_=i_xown[tt * 128:(tt + 1) * 128,
                               ec * 512:(ec + 1) * 512]),
                    waits=[xt_cons[opar]])
                ldg = op("gpsimd", lambda e, tt=tt, ec=ec, opar=opar: e.dma_start(
                    out=gtile[:, opar * 512:(opar + 1) * 512],
                    in_=d_g[tt * 128:(tt + 1) * 128,
                            ec * 512:(ec + 1) * 512]),
                    waits=[gt_cons[opar], st_g_last])
                e1 = op("vector", lambda e, opar=opar, ec=ec, bank=bank: e.tensor_add(
                    tmp6[:, opar * 512:(opar + 1) * 512], ps[bank][:],
                    bob[:, ec * 512:(ec + 1) * 512]),
                    waits=[mm, t6_cons[opar]] + (bob_toks if ec == 0 else []))
                ps_last[bank] = e1
                e2 = op("vector", lambda e, opar=opar: e.tensor_mul(
                    tmp6[:, opar * 512:(opar + 1) * 512],
                    tmp6[:, opar * 512:(opar + 1) * 512],
                    gtile[:, opar * 512:(opar + 1) * 512]), waits=[ldg, e1])
                e3 = op("vector", lambda e, opar=opar: e.tensor_add(
                    tmp6[:, opar * 512:(opar + 1) * 512],
                    tmp6[:, opar * 512:(opar + 1) * 512],
                    xtile[:, opar * 512:(opar + 1) * 512]), waits=[ldx, e2])
                xt_cons[opar] = e3
                gt_cons[opar] = e3
                st_last = op("sync",
                             lambda e, tt=tt, ec=ec, opar=opar: e.dma_start(
                                 out=o_out[tt * 128:(tt + 1) * 128,
                                           ec * 512:(ec + 1) * 512],
                                 in_=tmp6[:, opar * 512:(opar + 1) * 512]),
                             waits=[e3])
                t6_cons[opar] = st_last

        if DEBUG:
            op("sync", lambda e: e.dma_start(out=o_QT, in_=QT[:]),
               waits=[st_last])
            op("sync", lambda e: e.dma_start(out=o_yT, in_=yT[:]),
               waits=[st_last])
        em.emit(block, sems)
    return nc


def kernel(L0_attn_x, L0_attn_Wq, L0_attn_bq, L0_attn_Wk, L0_attn_bk,
           L0_attn_Wv, L0_attn_bv, L0_attn_Wo, L0_attn_bo,
           L0_ffn_w_gate, L0_ffn_w_up, L0_ffn_w_down,
           L0_ffn_s_gate, L0_ffn_s_up, L0_ffn_s_down):
    f32 = np.float32
    x = np.asarray(L0_attn_x, dtype=f32)
    Bsz, T, _ = x.shape

    wvhd = np.ascontiguousarray(
        np.asarray(L0_attn_Wv, f32).transpose(1, 0, 2).reshape(D, D))
    bqT = np.ascontiguousarray(
        np.broadcast_to(np.asarray(L0_attn_bq, f32).T, (128, H)))
    bkT = np.ascontiguousarray(
        np.broadcast_to(np.asarray(L0_attn_bk, f32).T, (128, H)))
    bvr = np.ascontiguousarray(np.asarray(L0_attn_bv, f32).reshape(1, D))
    bor = np.ascontiguousarray(np.asarray(L0_attn_bo, f32).reshape(1, D))
    sgpm = np.ascontiguousarray(
        np.asarray(L0_ffn_s_gate, f32).reshape(NII, 128).T)
    supm = np.ascontiguousarray(
        np.asarray(L0_ffn_s_up, f32).reshape(NII, 128).T)
    sdr = np.ascontiguousarray(np.asarray(L0_ffn_s_down, f32).reshape(1, D))
    dm = np.zeros((128, 4, CH), dtype=ml_dtypes.bfloat16)
    srow = np.arange(128)[:, None]
    tcol = np.arange(CH)[None, :]
    for k in range(4):
        dm[:, k, :] = ((k * 128 + srow) <= tcol).astype(ml_dtypes.bfloat16)
    dm = np.ascontiguousarray(dm.reshape(128, 4 * CH))

    shared = dict(
        wq=np.ascontiguousarray(np.asarray(L0_attn_Wq, f32)),
        wk=np.ascontiguousarray(np.asarray(L0_attn_Wk, f32)),
        wvhd=wvhd,
        wo=np.ascontiguousarray(np.asarray(L0_attn_Wo, f32)),
        bqT=bqT, bkT=bkT, bvr=bvr, bor=bor,
        wg=np.ascontiguousarray(np.asarray(L0_ffn_w_gate, np.int8)),
        wu=np.ascontiguousarray(np.asarray(L0_ffn_w_up, np.int8)),
        wd=np.ascontiguousarray(np.asarray(L0_ffn_w_down, np.int8)),
        sgpm=sgpm, supm=supm, sdr=sdr, dmask=dm,
    )

    in_maps = []
    metas = []
    for i in range(8):
        a, b = i, 7 - i
        rows = [x[0, a * CH:(a + 1) * CH], x[1, b * CH:(b + 1) * CH]]
        rows += [x[0, j * CH:(j + 1) * CH] for j in range(a)]
        rows += [x[1, j * CH:(j + 1) * CH] for j in range(b)]
        xkv = np.concatenate(rows, axis=0)
        big = np.full((2 * NPOS,), -1e9, dtype=f32)
        big[0] = 0.0
        big[2:2 + a] = 0.0
        big[NPOS + 1] = 0.0
        big[NPOS + 2 + a: NPOS + 2 + a + b] = 0.0
        m = dict(shared)
        m["xkvT"] = np.ascontiguousarray(xkv.T)
        m["xown"] = np.ascontiguousarray(xkv[:TOWN])
        m["bigoff"] = np.ascontiguousarray(
            np.broadcast_to(big[None, :], (128, 2 * NPOS)))
        in_maps.append(m)
        metas.append((a, b))

    nc = build_nc()
    res = run_bass_kernel_spmd(nc, in_maps, core_ids=list(range(8)))

    out = np.empty((2, T, D), dtype=f32)
    for i, (a, b) in enumerate(metas):
        oo = np.asarray(res.results[i]["out"], dtype=f32)
        out[0, a * CH:(a + 1) * CH] = oo[:CH]
        out[1, b * CH:(b + 1) * CH] = oo[CH:]
    return out


# version 10
# speedup vs baseline: 1.2353x; 1.2353x over previous
import numpy as np
import ml_dtypes
import concourse.bass as bass
import concourse.mybir as mybir
from concourse.bass_utils import run_bass_kernel_spmd

F32 = mybir.dt.float32
BF16 = mybir.dt.bfloat16
I8 = mybir.dt.int8
AF = mybir.ActivationFunctionType
ALU = mybir.AluOpType
AX = mybir.AxisListType

D = 2048
H = 16
HD = 128
INTER = 8192
CH = 512
NPOS = 9
TKV = NPOS * CH
TOWN = 1024
NCC = D // 128
NII = INTER // 128
SCALE = 1.0 / float(np.sqrt(HD))
EPS = 1e-5

DEBUG = False
DEBUG_ROWS = False

ENGS = ("sync", "gpsimd", "tensor", "vector", "scalar")
DMA_ENGS = ("sync", "gpsimd")


class Em:

    def __init__(self):
        self.ops = []

    def op(self, eng, fn, waits=()):
        tok = len(self.ops)
        ws = []
        for w in waits:
            if w is None:
                continue
            assert 0 <= w < tok, f"wait on future token {w} at {tok}"
            ws.append(w)
        self.ops.append([eng, fn, ws, False])
        return tok

    def emit(self, block, sems):
        ops = self.ops
        for o in ops:
            if o[0] in DMA_ENGS:
                o[3] = True
        for i, (eng, fn, ws, _) in enumerate(ops):
            for w in ws:
                weng = ops[w][0]
                if weng == eng and weng == "tensor":
                    continue
                ops[w][3] = True
        val_after = [0] * len(ops)
        cnt = {e: 0 for e in ENGS}
        for i, (eng, fn, ws, need) in enumerate(ops):
            if need:
                cnt[eng] += 16 if eng in DMA_ENGS else 1
            val_after[i] = cnt[eng]
        finals = dict(cnt)
        per = {e: [] for e in ENGS}
        for i, (eng, fn, ws, need) in enumerate(ops):
            rw = {}
            for w in ws:
                weng = ops[w][0]
                if weng == eng and weng == "tensor":
                    continue
                v = val_after[w]
                rw[weng] = max(rw.get(weng, 0), v)
            per[eng].append((rw, fn, need))

        def mk(engname):
            def f(engine):
                hi = {e: 0 for e in ENGS}
                for rw, fn, need in per[engname]:
                    for weng, v in rw.items():
                        if v > hi[weng]:
                            engine.wait_ge(sems[weng], v)
                            hi[weng] = v
                    inst = fn(engine)
                    if need:
                        inst.then_inc(sems[engname],
                                      16 if engname in DMA_ENGS else 1)
                for e in ENGS:
                    if finals[e] > hi[e]:
                        engine.wait_ge(sems[e], finals[e])
            return f
        block.sync(mk("sync"))
        block.gpsimd(mk("gpsimd"))
        block.tensor(mk("tensor"))
        block.vector(mk("vector"))
        block.scalar(mk("scalar"))


def build_nc():
    nc = bass.Bass("TRN2", target_bir_lowering=False)

    i_xkvT = nc.dram_tensor("xkvT", [D, TKV], F32, kind="ExternalInput").ap()
    i_xown = nc.dram_tensor("xown", [TOWN, D], F32, kind="ExternalInput").ap()
    i_wq = nc.dram_tensor("wq", [H, D, HD], F32, kind="ExternalInput").ap()
    i_wk = nc.dram_tensor("wk", [H, D, HD], F32, kind="ExternalInput").ap()
    i_wvhd = nc.dram_tensor("wvhd", [D, D], F32, kind="ExternalInput").ap()
    i_wo = nc.dram_tensor("wo", [D, D], F32, kind="ExternalInput").ap()
    i_bqT = nc.dram_tensor("bqT", [128, H], F32, kind="ExternalInput").ap()
    i_bkT = nc.dram_tensor("bkT", [128, H], F32, kind="ExternalInput").ap()
    i_bvr = nc.dram_tensor("bvr", [1, D], F32, kind="ExternalInput").ap()
    i_bor = nc.dram_tensor("bor", [1, D], F32, kind="ExternalInput").ap()
    i_wg = nc.dram_tensor("wg", [D, INTER], I8, kind="ExternalInput").ap()
    i_wu = nc.dram_tensor("wu", [D, INTER], I8, kind="ExternalInput").ap()
    i_wd = nc.dram_tensor("wd", [INTER, D], I8, kind="ExternalInput").ap()
    i_sg = nc.dram_tensor("sgpm", [128, NII], F32, kind="ExternalInput").ap()
    i_su = nc.dram_tensor("supm", [128, NII], F32, kind="ExternalInput").ap()
    i_sdr = nc.dram_tensor("sdr", [1, D], F32, kind="ExternalInput").ap()
    i_bigoff = nc.dram_tensor("bigoff", [128, 2 * NPOS], F32,
                              kind="ExternalInput").ap()
    i_dmask = nc.dram_tensor("dmask", [128, 4 * CH], BF16,
                             kind="ExternalInput").ap()

    o_out = nc.dram_tensor("out", [TOWN, D], F32, kind="ExternalOutput").ap()

    ik = "ExternalOutput" if DEBUG else "Internal"
    d_xnT = nc.dram_tensor("xnT_d", [D, TKV], BF16, kind=ik).ap()
    d_k = nc.dram_tensor("k_d", [D, TKV], BF16, kind=ik).ap()
    d_v = nc.dram_tensor("v_d", [TKV, D], BF16, kind=ik).ap()
    d_g = nc.dram_tensor("g_d", [TOWN, D], BF16, kind=ik).ap()
    if DEBUG:
        o_QT = nc.dram_tensor("dbg_QT", [128, H * TOWN], BF16,
                              kind="ExternalOutput").ap()
        o_yT = nc.dram_tensor("dbg_yT", [128, H * TOWN], BF16,
                              kind="ExternalOutput").ap()
        o_l = nc.dram_tensor("dbg_l", [2 * H, 512], F32,
                             kind="ExternalOutput").ap()
        o_c = nc.dram_tensor("dbg_c", [2 * H, 512], F32,
                             kind="ExternalOutput").ap()
        o_rl = nc.dram_tensor("dbg_rl", [2 * H, 512], F32,
                              kind="ExternalOutput").ap()

    BASE = 16512
    TOPLIM = 229344
    P0 = BASE
    M0 = BASE + 16384
    Z1 = BASE + 45056
    Z2 = BASE + 110592
    Z3 = BASE + 143360
    Z4 = BASE + 176128
    assert Z4 + 32768 <= TOPLIM

    uid = [0]

    def sb(name, shape, dtype, off):
        uid[0] += 1
        return nc.alloc_sbuf_tensor_at(f"{name}_{uid[0]}", shape, dtype,
                                       offset=off)

    bigoff = sb("bigoff", [128, 2 * NPOS], F32, P0 + 0)
    bkT = sb("bkT", [128, H], F32, P0 + 128)
    bqT = sb("bqT", [128, H], F32, P0 + 192)
    sgpm = sb("sgpm", [128, NII], F32, P0 + 256)
    supm = sb("supm", [128, NII], F32, P0 + 512)
    dmask = sb("dmask", [128, 4 * CH], BF16, P0 + 768)
    bvb = sb("bvb", [128, D], F32, P0 + 4864)
    onesb = sb("onesb", [128, 1], BF16, P0 + 13056)
    ones1b = sb("ones1b", [1, 128], BF16, P0 + 13120)
    ones1f = sb("ones1f", [1, 128], F32, P0 + 13376)
    maxk2 = sb("maxk2", [1, H], F32, P0 + 13888)
    epsc = sb("epsc", [128, 1], F32, P0 + 13952)

    em = Em()
    op = em.op

    from contextlib import ExitStack
    with ExitStack() as ctx:
        ps = [ctx.enter_context(nc.psum_tensor(f"ps{k}", [128, 512], F32))
              for k in range(8)]
        sems = {e: ctx.enter_context(nc.semaphore(f"S_{e}")) for e in ENGS}
        block = ctx.enter_context(nc.Block())

        ps_last = {k: None for k in range(8)}

        t_bigoff = op("sync", lambda e: e.dma_start(out=bigoff[:], in_=i_bigoff))
        t_bkT = op("sync", lambda e: e.dma_start(out=bkT[:], in_=i_bkT))
        t_bqT = op("sync", lambda e: e.dma_start(out=bqT[:], in_=i_bqT))
        t_sg = op("sync", lambda e: e.dma_start(out=sgpm[:], in_=i_sg))
        t_su = op("sync", lambda e: e.dma_start(out=supm[:], in_=i_su))
        t_dmask = op("sync", lambda e: e.dma_start(out=dmask[:], in_=i_dmask))
        t_ones_b = op("vector", lambda e: e.memset(onesb[:], 1.0))
        t_ones_1b = op("vector", lambda e: e.memset(ones1b[:], 1.0))
        t_ones_1f = op("vector", lambda e: e.memset(ones1f[:], 1.0))
        op("vector", lambda e: e.memset(maxk2[:], 0.0))
        t_eps = op("vector", lambda e: e.memset(epsc[:], float(EPS)))

        bvrow = sb("bvrow", [1, D], F32, M0)
        tv = op("sync", lambda e: e.dma_start(out=bvrow[:], in_=i_bvr))
        tok_bvb = None
        bv_mm_last = None
        for ec in range(4):
            mmb = op("tensor", lambda e, ec=ec: e.matmul(
                ps[7][:], ones1f[:], bvrow[0:1, ec * 512:(ec + 1) * 512],
                start=True, stop=True), waits=[tv, t_ones_1f, ps_last[7]])
            bv_mm_last = mmb
            tok_bvb = op("vector", lambda e, ec=ec: e.tensor_copy(
                bvb[:, ec * 512:(ec + 1) * 512], ps[7][:]), waits=[mmb])
            ps_last[7] = tok_bvb

        tmp1 = sb("tmp1", [128, 1024], BF16, M0 + 0)
        sqcc = sb("sqcc", [128, 1024], BF16, M0 + 2048)
        kbuf = sb("kbuf", [128, 1024], BF16, M0 + 4096)
        sqb = sb("sqb", [128, 1024], BF16, M0 + 6144)
        r_m = sb("r_m", [1, 512], F32, M0 + 8192)[:]
        r_msq = sb("r_msq", [1, 512], F32, M0 + 10240)[:]
        r_m2 = sb("r_m2", [1, 512], F32, M0 + 12288)[:]
        r_var = sb("r_var", [1, 512], F32, M0 + 14336)[:]
        r_std = sb("r_std", [1, 512], F32, M0 + 16384)[:]
        r_rstd = sb("r_rstd", [1, 512], F32, M0 + 18432)[:]
        r_mr = sb("r_mr", [1, 512], F32, M0 + 20480)[:]
        r_nrm = sb("r_nrm", [1, 512], F32, M0 + 22528)
        rstdb = sb("rstdb", [128, 512], BF16, M0 + 24576)
        mrb = sb("mrb", [128, 512], BF16, M0 + 25600)

        xT = [sb("xT0", [128, NCC * 512], BF16, Z1),
              sb("xT1", [128, NCC * 512], BF16, Z1 + 16384)]
        wk_sb = sb("wkA", [128, 8 * NCC * 128], BF16, Z1 + 32768)
        wk_sbB = sb("wkB", [128, 8 * NCC * 128], BF16, Z4)
        xnc = [sb("xnc0", [128, NCC * 512], BF16, Z2),
               sb("xnc1", [128, NCC * 512], BF16, Z2 + 16384)]
        xnT_own = sb("xnT_own", [128, NCC * TOWN], BF16, Z3)

        def wk_ap(h, cc):
            if h < 8:
                return wk_sb[:, (h * NCC + cc) * 128:(h * NCC + cc + 1) * 128]
            return wk_sbB[:, ((h - 8) * NCC + cc) * 128:
                          ((h - 8) * NCC + cc + 1) * 128]

        ld_wk = None
        for hq in range(4):
            dst = wk_sb if hq < 2 else wk_sbB
            ld_wk = op("gpsimd", lambda e, hq=hq, dst=dst: e.dma_start(
                out=dst[:].rearrange("p (h cc d) -> p h cc d",
                                     h=8, cc=NCC)[:, (hq % 2) * 4:
                                                  (hq % 2) * 4 + 4],
                in_=i_wk.rearrange("h (cc cin) d -> cin h cc d", cin=128)
                [:, hq * 4:(hq + 1) * 4]))

        xT_cons = [None, None]
        xnc_w = [[], []]
        sq_cons = [None, None]
        t1_cons = [None, None]
        st_xnT = []
        own_copy_tok = []
        xnc2_cons = [None, None]
        kb_cons = [[bv_mm_last], [bv_mm_last]]
        st_k_last = None
        nrm_tok = None
        sqcc_guard = bv_mm_last
        tmp1_guard = bv_mm_last
        chunk_state = {}
        chunk_a1 = [None]

        def ln_chunk(tc):
            p = tc % 2
            ld = op("gpsimd", lambda e, p=p, tc=tc: e.dma_start(
                out=xT[p][:].rearrange("p (cc t) -> p cc t", cc=NCC),
                in_=i_xkvT[:, tc * 512:(tc + 1) * 512]
                .rearrange("(cc cin) t -> cin cc t", cin=128)),
                waits=[xT_cons[p]])
            mmS = mmQ = None
            nonlocal sqcc_guard, tmp1_guard
            for cc in range(NCC):
                q = cc % 2
                sq = op("vector", lambda e, p=p, cc=cc, q=q: e.tensor_mul(
                    sqcc[:, q * 512:(q + 1) * 512],
                    xT[p][:, cc * 512:(cc + 1) * 512],
                    xT[p][:, cc * 512:(cc + 1) * 512]),
                    waits=[ld, sq_cons[q], sqcc_guard])
                sqcc_guard = None
                mmS = op("tensor", lambda e, p=p, cc=cc: e.matmul(
                    ps[6][0:1, :], onesb[:],
                    xT[p][:, cc * 512:(cc + 1) * 512],
                    start=(cc == 0), stop=(cc == NCC - 1)),
                    waits=[ld, t_ones_b] + ([ps_last[6]] if cc == 0 else []))
                mmQ = op("tensor", lambda e, q=q, cc=cc: e.matmul(
                    ps[5][0:1, :], onesb[:], sqcc[:, q * 512:(q + 1) * 512],
                    start=(cc == 0), stop=(cc == NCC - 1)),
                    waits=[sq] + ([ps_last[5]] if cc == 0 else []))
                sq_cons[q] = mmQ
            v1 = op("vector", lambda e: e.tensor_scalar_mul(
                r_m, ps[6][0:1, :], 1.0 / D), waits=[mmS])
            v2 = op("vector", lambda e: e.tensor_scalar_mul(
                r_msq, ps[5][0:1, :], 1.0 / D), waits=[mmQ])
            ps_last[6] = v1
            ps_last[5] = v2
            v3 = op("vector", lambda e: e.tensor_mul(
                r_m2, r_m, r_m), waits=[v1])
            v4 = op("vector", lambda e: e.tensor_sub(
                r_var, r_msq, r_m2), waits=[v2, v3])
            a1 = op("scalar", lambda e: e.activation(
                r_std, r_var, AF.Sqrt, bias=epsc[0:1, 0:1], scale=1.0),
                waits=[v4, t_eps])
            chunk_a1[0] = a1
            v5 = op("vector", lambda e: e.reciprocal(r_rstd, r_std),
                    waits=[a1])
            v6 = op("vector", lambda e: e.tensor_mul(
                r_mr, r_m, r_rstd), waits=[v5, v1])
            mB1 = op("tensor", lambda e: e.matmul(
                ps[4][:], ones1f[:], r_rstd, start=True, stop=True),
                waits=[v5, t_ones_1f, ps_last[4]])
            vB1 = op("vector", lambda e: e.tensor_copy(rstdb[:], ps[4][:]),
                     waits=[mB1])
            ps_last[4] = vB1
            mB2 = op("tensor", lambda e: e.matmul(
                ps[3][:], ones1f[:], r_mr, start=True, stop=True),
                waits=[v6, ps_last[3]])
            vB2 = op("vector", lambda e: e.tensor_copy(mrb[:], ps[3][:]),
                     waits=[mB2])
            ps_last[3] = vB2
            last_apply = None
            for cc in range(NCC):
                q = cc % 2
                va = op("vector", lambda e, p=p, cc=cc, q=q: e.tensor_mul(
                    tmp1[:, q * 512:(q + 1) * 512],
                    xT[p][:, cc * 512:(cc + 1) * 512], rstdb[:]),
                    waits=[vB1, t1_cons[q], tmp1_guard] +
                    (xnc_w[p] if cc == 0 else []))
                tmp1_guard = None
                vb = op("vector", lambda e, p=p, cc=cc, q=q: e.tensor_sub(
                    xnc[p][:, cc * 512:(cc + 1) * 512],
                    tmp1[:, q * 512:(q + 1) * 512], mrb[:]),
                    waits=[vB2, va])
                t1_cons[q] = vb
                last_apply = vb
            xT_cons[p] = last_apply
            deps = [last_apply]
            if tc < 2:
                oc = op("vector", lambda e, p=p, tc=tc: e.tensor_copy(
                    xnT_own[:].rearrange("p (cc t) -> p cc t", cc=NCC)
                    [:, :, tc * 512:(tc + 1) * 512],
                    xnc[p][:].rearrange("p (cc t) -> p cc t", cc=NCC)),
                    waits=[last_apply])
                own_copy_tok.append(oc)
                deps.append(oc)
            st = op("sync", lambda e, p=p, tc=tc: e.dma_start(
                out=d_xnT.rearrange("(cc cin) t -> cin cc t", cin=128)
                [:, :, tc * 512:(tc + 1) * 512],
                in_=xnc[p][:].rearrange("p (cc t) -> p cc t", cc=NCC)),
                waits=deps)
            st_xnT.append(st)
            chunk_state[tc] = (p, last_apply, st)

        def kproj_chunk(tc):
            nonlocal st_k_last, nrm_tok
            p, last_apply, st = chunk_state[tc]
            ev_toks = [None, None]

            def norm_block(h):
                nonlocal nrm_tok
                kp = h % 2
                sqo = op("vector", lambda e, kp=kp: e.tensor_mul(
                    sqb[:, kp * 512:(kp + 1) * 512],
                    kbuf[:, kp * 512:(kp + 1) * 512],
                    kbuf[:, kp * 512:(kp + 1) * 512]),
                    waits=[ev_toks[kp], nrm_tok])
                mmn = op("tensor", lambda e, kp=kp: e.matmul(
                    ps[6][0:1, :], onesb[:],
                    sqb[:, kp * 512:(kp + 1) * 512], start=True, stop=True),
                    waits=[sqo, ps_last[6]])
                prev_nrm = nrm_tok
                mx1 = op("vector", lambda e: e.reduce_max(
                    r_nrm[0:1, 0:1], ps[6][0:1, :], AX.X), waits=[mmn])
                ps_last[6] = mx1
                nrm_tok = op("vector", lambda e, h=h: e.tensor_max(
                    maxk2[0:1, h:h + 1], maxk2[0:1, h:h + 1],
                    r_nrm[0:1, 0:1]), waits=[mx1, prev_nrm])
                kb_cons[h % 2].append(mmn)

            last_mm_chunk = None
            for h in range(H):
                kp = h % 2
                bank = kp
                mm = None
                for cc in range(NCC):
                    mm = op("tensor",
                            lambda e, p=p, h=h, cc=cc, bank=bank: e.matmul(
                                ps[bank][:], wk_ap(h, cc),
                                xnc[p][:, cc * 512:(cc + 1) * 512],
                                start=(cc == 0), stop=(cc == NCC - 1)),
                            waits=[last_apply, ld_wk, ps_last[bank]]
                            if cc == 0 else [])
                last_mm_chunk = mm
                ev = op("vector",
                        lambda e, h=h, kp=kp, bank=bank: e.tensor_single_scalar(
                            kbuf[:, kp * 512:(kp + 1) * 512], ps[bank][:],
                            bkT[:, h:h + 1], ALU.add),
                        waits=[mm, t_bkT] + kb_cons[kp])
                ev_toks[kp] = ev
                ps_last[bank] = ev
                stk = op("sync", lambda e, h=h, tc=tc, kp=kp: e.dma_start(
                    out=d_k[h * 128:(h + 1) * 128, tc * 512:(tc + 1) * 512],
                    in_=kbuf[:, kp * 512:(kp + 1) * 512]),
                    waits=[ev])
                st_k_last = stk
                kb_cons[kp] = [stk]
                if h > 0:
                    norm_block(h - 1)
            norm_block(H - 1)
            xnc2_cons[p] = last_mm_chunk
            xnc_w[p] = [st, last_mm_chunk]

        for tc in range(NPOS + 1):
            if tc < NPOS:
                ln_chunk(tc)
            if tc >= 1:
                kproj_chunk(tc - 1)
        nrm_final = nrm_tok
        mmn_final = kb_cons[(H - 1) % 2][-1]

        vstage = sb("vstage", [128, 1024], BF16, M0 + 26624)
        wv_sb = sb("wv", [128, NCC * D], BF16, Z1)
        ld_wv = op("gpsimd", lambda e: e.dma_start(
            out=wv_sb[:].rearrange("p (cc e) -> p cc e", cc=NCC),
            in_=i_wvhd.rearrange("(cc cin) e -> cin cc e", cin=128)),
            waits=[xnc2_cons[0], xnc2_cons[1], ld_wk,
                   xT_cons[0], xT_cons[1]])

        vs_cons = [[], []]
        st_v_last = None
        xncv_cons = [None, None]
        for tc in range(NPOS):
            p = tc % 2
            ld = op("sync", lambda e, p=p, tc=tc: e.dma_start(
                out=xnc[p][:].rearrange("p (cc t) -> p cc t", cc=NCC),
                in_=d_xnT.rearrange("(cc cin) t -> cin cc t", cin=128)
                [:, :, tc * 512:(tc + 1) * 512]),
                waits=[st_xnT[tc], st_k_last, xncv_cons[p],
                       xnc2_cons[p] if tc < 2 else None])
            last_mm_chunk = None
            for hg in range(4):
                for tt in range(4):
                    vp = (hg * 4 + tt) % 2
                    bank = vp
                    mm = None
                    for cc in range(NCC):
                        mm = op("tensor",
                                lambda e, p=p, hg=hg, tt=tt, cc=cc, bank=bank:
                                e.matmul(
                                    ps[bank][:],
                                    xnc[p][:, cc * 512 + tt * 128:
                                           cc * 512 + (tt + 1) * 128],
                                    wv_sb[:, cc * D + hg * 512:
                                          cc * D + (hg + 1) * 512],
                                    start=(cc == 0), stop=(cc == NCC - 1)),
                                waits=[ld, ld_wv, ps_last[bank]]
                                if cc == 0 else [])
                    last_mm_chunk = mm
                    ev = op("vector",
                            lambda e, hg=hg, vp=vp, bank=bank: e.tensor_add(
                                vstage[:, vp * 512:(vp + 1) * 512],
                                ps[bank][:],
                                bvb[:, hg * 512:(hg + 1) * 512]),
                            waits=[mm, tok_bvb] + vs_cons[vp])
                    ps_last[bank] = ev
                    stv = op("sync",
                             lambda e, tc=tc, tt=tt, hg=hg, vp=vp: e.dma_start(
                                 out=d_v[tc * 512 + tt * 128:
                                         tc * 512 + (tt + 1) * 128,
                                         hg * 512:(hg + 1) * 512],
                                 in_=vstage[:, vp * 512:(vp + 1) * 512]),
                             waits=[ev])
                    st_v_last = stv
                    vs_cons[vp] = [stv]
            xncv_cons[p] = last_mm_chunk

        wq_sb = sb("wq", [128, H * NCC * 128], BF16, Z1)
        ld_wq = None
        for hq in range(4):
            ld_wq = op("gpsimd", lambda e, hq=hq: e.dma_start(
                out=wq_sb[:].rearrange("p (h cc d) -> p h cc d",
                                       h=H, cc=NCC)[:, hq * 4:(hq + 1) * 4],
                in_=i_wq.rearrange("h (cc cin) d -> cin h cc d", cin=128)
                [:, hq * 4:(hq + 1) * 4]),
                waits=[xncv_cons[0], xncv_cons[1], ld_wv] if hq == 0 else [])
        QT = sb("QT", [128, H * TOWN], BF16, Z4)
        q_ev_last = None
        q_mm_last = None
        for h in range(H):
            for sl in range(2):
                qp = (h * 2 + sl) % 2
                bank = qp
                mm = None
                for cc in range(NCC):
                    mm = op("tensor",
                            lambda e, h=h, sl=sl, cc=cc, bank=bank: e.matmul(
                                ps[bank][:],
                                wq_sb[:, (h * NCC + cc) * 128:
                                      (h * NCC + cc + 1) * 128],
                                xnT_own[:, cc * TOWN + sl * 512:
                                        cc * TOWN + (sl + 1) * 512],
                                start=(cc == 0), stop=(cc == NCC - 1)),
                            waits=([ld_wq, ps_last[bank]] + own_copy_tok)
                            if cc == 0 else [])
                q_mm_last = mm
                q_ev_last = op("vector",
                               lambda e, h=h, sl=sl, bank=bank: e.tensor_scalar(
                                   QT[:, h * TOWN + sl * 512:
                                      h * TOWN + (sl + 1) * 512],
                                   ps[bank][:], bqT[:, h:h + 1], float(SCALE),
                                   ALU.add, ALU.mult),
                               waits=[mm, t_bqT])
                ps_last[bank] = q_ev_last

        sd_row = sb("sd_row", [1, D], F32, M0 + 0)
        sdownb = sb("sdownb", [128, D], F32, M0 + 8192)
        gsil = sb("gsil", [128, 1024], BF16, M0 + 16384)
        tmp3 = sb("tmp3", [128, 1024], F32, M0 + 18432)

        tsd = op("sync", lambda e: e.dma_start(out=sd_row[:], in_=i_sdr),
                 waits=[t1_cons[0], t1_cons[1], st_k_last, nrm_final,
                        mmn_final])
        sd_toks = []
        for ec in range(4):
            mmb = op("tensor", lambda e, ec=ec: e.matmul(
                ps[7][:], ones1f[:], sd_row[0:1, ec * 512:(ec + 1) * 512],
                start=True, stop=True), waits=[tsd, ps_last[7]])
            tok = op("vector", lambda e, ec=ec: e.tensor_copy(
                sdownb[:, ec * 512:(ec + 1) * 512], ps[7][:]),
                waits=[mmb, nrm_final, chunk_a1[0]])
            ps_last[7] = tok
            sd_toks.append(tok)

        fuseT = sb("fuseT", [128, NII * 512], BF16, Z1)
        slab = [sb("slab0", [128, NCC * 512], BF16, Z2),
                sb("slab1", [128, NCC * 512], BF16, Z2 + 16384)]
        fuse_toks = [None] * NII
        st_g_last = None
        ds0 = ds1 = None
        gs_cons = [[], []]
        t3_cons = [None, None]
        for fc in range(2):
            gate_prev, up_prev = ds0, ds1
            for grp in range(NII // 4):
                extra_g = [xncv_cons[0], xncv_cons[1]] if (fc == 0 and grp == 0) else []
                ld_g = op("gpsimd", lambda e, grp=grp: e.dma_start(
                    out=slab[0][:].rearrange("p (cc i) -> p cc i", cc=NCC),
                    in_=i_wg.rearrange("(cc cin) i -> cin cc i", cin=128)
                    [:, :, grp * 512:(grp + 1) * 512]),
                    waits=[gate_prev] + extra_g)
                ld_u = op("gpsimd", lambda e, grp=grp: e.dma_start(
                    out=slab[1][:].rearrange("p (cc i) -> p cc i", cc=NCC),
                    in_=i_wu.rearrange("(cc cin) i -> cin cc i", cin=128)
                    [:, :, grp * 512:(grp + 1) * 512]),
                    waits=[up_prev] + extra_g)
                for isub in range(4):
                    ii = grp * 4 + isub
                    gp = ii % 2
                    bank = gp
                    bank2 = 2 + gp
                    mm = None
                    for cc in range(NCC):
                        mm = op("tensor",
                                lambda e, isub=isub, cc=cc, fc=fc, bank=bank:
                                e.matmul(
                                    ps[bank][:],
                                    slab[0][:, cc * 512 + isub * 128:
                                            cc * 512 + (isub + 1) * 128],
                                    xnT_own[:, cc * TOWN + fc * 512:
                                            cc * TOWN + (fc + 1) * 512],
                                    start=(cc == 0), stop=(cc == NCC - 1)),
                                waits=[ld_g, ps_last[bank], q_mm_last]
                                if cc == 0 else [])
                    gate_prev = mm
                    sil = op("scalar",
                             lambda e, ii=ii, gp=gp, bank=bank: e.activation(
                                 gsil[:, gp * 512:(gp + 1) * 512], ps[bank][:],
                                 AF.Silu, scale=sgpm[:, ii:ii + 1]),
                             waits=[mm, t_sg] + gs_cons[gp])
                    ps_last[bank] = sil
                    mmu = None
                    for cc in range(NCC):
                        mmu = op("tensor",
                                 lambda e, isub=isub, cc=cc, fc=fc, bank2=bank2:
                                 e.matmul(
                                     ps[bank2][:],
                                     slab[1][:, cc * 512 + isub * 128:
                                             cc * 512 + (isub + 1) * 128],
                                     xnT_own[:, cc * TOWN + fc * 512:
                                             cc * TOWN + (fc + 1) * 512],
                                     start=(cc == 0), stop=(cc == NCC - 1)),
                                 waits=[ld_u, ps_last[bank2]]
                                 if cc == 0 else [])
                    up_prev = mmu
                    fu = op("vector",
                            lambda e, ii=ii, gp=gp, bank2=bank2:
                            e.scalar_tensor_tensor(
                                fuseT[:, ii * 512:(ii + 1) * 512],
                                ps[bank2][:], supm[:, ii:ii + 1],
                                gsil[:, gp * 512:(gp + 1) * 512],
                                ALU.mult, ALU.mult),
                            waits=[mmu, sil, t_su, fuse_toks[ii]])
                    ps_last[bank2] = fu
                    gs_cons[gp] = [fu]
                    fuse_toks[ii] = fu
            ds = [gate_prev, up_prev]
            for ec in range(4):
                mm_last = None
                for half in range(4):
                    dp = half % 2
                    ldd = op("gpsimd",
                             lambda e, half=half, ec=ec, dp=dp: e.dma_start(
                                 out=slab[dp][:].rearrange(
                                     "p (k i) -> p k i", k=NCC),
                                 in_=i_wd.rearrange(
                                     "(ii iin) e -> iin ii e", iin=128)
                                 [:, half * 16:(half + 1) * 16,
                                  ec * 512:(ec + 1) * 512]),
                             waits=[ds[dp]])
                    for k in range(16):
                        ii = half * 16 + k
                        for tt in range(4):
                            bank = 4 + tt
                            w = []
                            if k == 0 and tt == 0:
                                w.append(ldd)
                            if tt == 0:
                                w.append(fuse_toks[ii])
                            if ii == 0:
                                w.append(ps_last[bank])
                            mm_last = op(
                                "tensor",
                                lambda e, ii=ii, k=k, tt=tt, dp=dp, bank=bank:
                                e.matmul(
                                    ps[bank][:],
                                    fuseT[:, ii * 512 + tt * 128:
                                          ii * 512 + (tt + 1) * 128],
                                    slab[dp][:, k * 512:(k + 1) * 512],
                                    start=(ii == 0), stop=(ii == NII - 1)),
                                waits=w)
                    ds[dp] = mm_last
                for tt in range(4):
                    bank = 4 + tt
                    tp = tt % 2
                    vm = op("vector",
                            lambda e, tt=tt, ec=ec, tp=tp, bank=bank:
                            e.tensor_mul(
                                tmp3[:, tp * 512:(tp + 1) * 512], ps[bank][:],
                                sdownb[:, ec * 512:(ec + 1) * 512]),
                            waits=[mm_last, t3_cons[tp]])
                    ps_last[bank] = vm
                    si = op("scalar", lambda e, tp=tp: e.activation(
                        gsil[:, tp * 512:(tp + 1) * 512],
                        tmp3[:, tp * 512:(tp + 1) * 512], AF.Silu),
                        waits=[vm] + gs_cons[tp])
                    stg = op("sync",
                             lambda e, fc=fc, tt=tt, ec=ec, tp=tp: e.dma_start(
                                 out=d_g[fc * 512 + tt * 128:
                                         fc * 512 + (tt + 1) * 128,
                                         ec * 512:(ec + 1) * 512],
                                 in_=gsil[:, tp * 512:(tp + 1) * 512]),
                             waits=[si])
                    st_g_last = stg
                    gs_cons[tp] = [stg]
                    t3_cons[tp] = si
            ds0, ds1 = ds[0], ds[1]

        Csb = sb("Csb", [128, 1024], F32, M0 + 0)
        tmp5 = sb("tmp5", [128, 3 * 512], F32, M0 + 4096)
        pbuf = sb("pbuf", [128, 8 * 512], BF16, M0 + 10240)
        rlb = sb("rlb", [128, 512], F32, M0 + 18432)
        sqb5 = sb("sqb5", [128, 512], BF16, M0 + 20480)
        R1 = sb("R1", [128, 512], F32, M0 + 21504)
        R2 = sb("R2", [128, 512], F32, M0 + 23552)
        r_c = R1[32:33, :]
        r_ln = R1[64:65, :]
        r_c2 = R1[0:1, :]
        r_rl = R2[0:1, :]

        ktb = [sb("kt0", [128, TKV], BF16, Z1),
               sb("kt1", [128, TKV], BF16, Z1 + TKV * 2)]
        vtb = [sb("vt0", [128, TKV], BF16, Z1 + TKV * 4),
               sb("vt1", [128, TKV], BF16, Z1 + TKV * 6)]
        yT = sb("yT", [128, H * TOWN], BF16, Z3)

        wos = [sb("wos0", [128, NCC * 512], BF16, Z2),
               sb("wos1", [128, NCC * 512], BF16, Z2 + 16384)]

        def ld_wo_op(ec, waits):
            return op("gpsimd", lambda e, ec=ec: e.dma_start(
                out=wos[ec % 2][:].rearrange("p (dc e) -> p dc e", dc=NCC),
                in_=i_wo.rearrange("(dc din) e -> din dc e", din=128)
                [:, :, ec * 512:(ec + 1) * 512]), waits=waits)

        wo_tok = [None] * 4
        wo_tok[0] = ld_wo_op(0, [ds0, ds1])
        wo_tok[1] = ld_wo_op(1, [ds0, ds1])

        kt_cons = [None, None]
        vt_cons = [None, None]
        pb_cons = [None] * 8
        t5_cons = [None] * 3
        C_cons = [None, None]
        sq5_cons = None
        rlb_cons = st_g_last
        mrb_prev = None
        y_done = None
        DELAY = 6
        NPB = 8
        NT5 = 3
        sbank = (0, 1, 4, 5)
        npair = NPOS * 4

        def lb_of(h, sl):
            return 3 if (h * 2 + sl) % 2 == 0 else 7

        def c_chain(h, sl):
            nonlocal sq5_cons
            lb = lb_of(h, sl)
            sqo = op("vector", lambda e, h=h, sl=sl: e.tensor_mul(
                sqb5[:],
                QT[:, h * TOWN + sl * 512: h * TOWN + (sl + 1) * 512],
                QT[:, h * TOWN + sl * 512: h * TOWN + (sl + 1) * 512]),
                waits=[q_ev_last, sq5_cons, st_v_last, st_g_last])
            mmc = op("tensor", lambda e, lb=lb: e.matmul(
                ps[lb][0:1, :], onesb[:], sqb5[:], start=True, stop=True),
                waits=[sqo, ps_last[lb]])
            sq5_cons = mmc
            vc1 = op("vector", lambda e, h=h, lb=lb: e.tensor_scalar_mul(
                r_c, ps[lb][0:1, :], maxk2[0:1, h:h + 1]),
                waits=[mmc, st_g_last])
            acl = op("scalar", lambda e: e.activation(
                r_ln, r_c, AF.Ln), waits=[vc1])
            ac = op("scalar", lambda e: e.activation(
                r_c2, r_ln, AF.Exp, scale=0.5), waits=[acl])
            mcb = op("tensor", lambda e, lb=lb: e.matmul(
                ps[lb][:], ones1f[:], r_c2, start=True, stop=True),
                waits=[ac, vc1])
            vcb = op("vector", lambda e, sl=sl, lb=lb: e.tensor_copy(
                Csb[:, sl * 512:(sl + 1) * 512], ps[lb][:]),
                waits=[mcb, C_cons[sl]])
            return vcb

        pend = {(0, 0): c_chain(0, 0)}
        first5 = [st_k_last, st_v_last, st_g_last, ds0, ds1]
        for h in range(H):
            kp = h % 2
            ldk = op("sync", lambda e, h=h, kp=kp: e.dma_start(
                out=ktb[kp][:], in_=d_k[h * 128:(h + 1) * 128, :]),
                waits=[kt_cons[kp]] + (first5 if h < 2 else []))
            ldvs = []
            for pos in range(NPOS):
                ldvs.append(op("sync",
                               lambda e, h=h, kp=kp, pos=pos: e.dma_start(
                                   out=vtb[kp][:].rearrange(
                                       "p (pos tl d) -> p pos tl d",
                                       pos=NPOS, tl=4)[:, pos],
                                   in_=d_v[pos * 512:(pos + 1) * 512,
                                           h * 128:(h + 1) * 128]
                                   .rearrange("(tl s) d -> s tl d", s=128)),
                               waits=[vt_cons[kp]] if pos == 0 else []))
            for sl in range(2):
                yb = 2 if (h * 2 + sl) % 2 == 0 else 6
                lb = lb_of(h, sl)
                vcb = pend.pop((h, sl))
                mmy_last = None
                mml_last = None

                def consume(j, first, last):
                    nonlocal mml_last, mmy_last
                    pb = j % NPB
                    pt = pt_toks[j]
                    mml_last = op("tensor",
                                  lambda e, j=j, pb=pb, lb=lb, first=first,
                                  last=last: e.matmul(
                                      ps[lb][0:1, :], onesb[:],
                                      pbuf[:, pb * 512:(pb + 1) * 512],
                                      start=first, stop=last),
                                  waits=[pt] + ([vcb] if first else []))
                    mmy_last = op("tensor",
                                  lambda e, j=j, pb=pb, kp=kp, yb=yb,
                                  first=first, last=last: e.matmul(
                                      ps[yb][:],
                                      vtb[kp][:, j * 128:(j + 1) * 128],
                                      pbuf[:, pb * 512:(pb + 1) * 512],
                                      start=first, stop=last),
                                  waits=[ldvs[j // 4]] + ([ps_last[yb]]
                                                          if first else []))
                    pb_cons[pb] = mmy_last

                pt_toks = {}
                for pos in range(NPOS):
                    for st in range(4):
                        j = pos * 4 + st
                        sp = j % NT5
                        pb = j % NPB
                        bank = sbank[j % 4]
                        mms = op("tensor",
                                 lambda e, h=h, sl=sl, pos=pos, st=st, kp=kp,
                                 bank=bank: e.matmul(
                                     ps[bank][:],
                                     ktb[kp][:, pos * 512 + st * 128:
                                             pos * 512 + (st + 1) * 128],
                                     QT[:, h * TOWN + sl * 512:
                                        h * TOWN + (sl + 1) * 512],
                                     start=True, stop=True),
                                 waits=[ldk, ps_last[bank]])
                        stt = op("vector",
                                 lambda e, sp=sp, sl=sl, bank=bank:
                                 e.scalar_tensor_tensor(
                                     tmp5[:, sp * 512:(sp + 1) * 512],
                                     ps[bank][:], 1.0,
                                     Csb[:, sl * 512:(sl + 1) * 512],
                                     ALU.mult, ALU.subtract),
                                 waits=[mms, vcb, t5_cons[sp]])
                        ps_last[bank] = stt
                        ex = op("scalar",
                                lambda e, sp=sp, pb=pb, sl=sl, pos=pos:
                                e.activation(
                                    pbuf[:, pb * 512:(pb + 1) * 512],
                                    tmp5[:, sp * 512:(sp + 1) * 512], AF.Exp,
                                    bias=bigoff[:, sl * NPOS + pos:
                                                sl * NPOS + pos + 1]),
                                waits=[stt, t_bigoff, pb_cons[pb]])
                        t5_cons[sp] = ex
                        pt = ex
                        if pos == sl:
                            pt = op("vector",
                                    lambda e, pb=pb, st=st: e.tensor_mul(
                                        pbuf[:, pb * 512:(pb + 1) * 512],
                                        pbuf[:, pb * 512:(pb + 1) * 512],
                                        dmask[:, st * 512:(st + 1) * 512]),
                                    waits=[ex, t_dmask])
                        pt_toks[j] = pt
                        if j >= DELAY:
                            consume(j - DELAY, j - DELAY == 0, False)
                        if j == 20:
                            nxt = (h, 1) if sl == 0 else (h + 1, 0)
                            if nxt[0] < H:
                                pend[nxt] = c_chain(*nxt)
                for j in range(npair - DELAY, npair):
                    consume(j, False, j == npair - 1)
                ps_last[lb] = mml_last
                if DEBUG and DEBUG_ROWS:
                    vl = op("vector", lambda e, lb=lb: e.tensor_copy(
                        R2[32:33, :], ps[lb][0:1, :]), waits=[mml_last])
                    ps_last[lb] = vl
                    op("sync", lambda e, h=h, sl=sl: e.dma_start(
                        out=o_l[h * 2 + sl: h * 2 + sl + 1, :],
                        in_=R2[32:33, :]), waits=[vl])
                    op("sync", lambda e, h=h, sl=sl: e.dma_start(
                        out=o_c[h * 2 + sl: h * 2 + sl + 1, :], in_=r_c2),
                        waits=[])
                vrec = op("vector", lambda e, lb=lb: e.reciprocal(
                    r_rl, ps[lb][0:1, :]), waits=[mml_last, mrb_prev])
                ps_last[lb] = vrec
                bb = sbank[3]
                mrb5 = op("tensor", lambda e, bb=bb: e.matmul(
                    ps[bb][:], ones1f[:], r_rl, start=True, stop=True),
                    waits=[vrec, ps_last[bb]])
                mrb_prev = mrb5
                vrb = op("vector", lambda e, bb=bb: e.tensor_copy(
                    rlb[:], ps[bb][:]), waits=[mrb5, rlb_cons])
                ps_last[bb] = vrb
                if DEBUG and DEBUG_ROWS:
                    op("sync", lambda e, h=h, sl=sl: e.dma_start(
                        out=o_rl[h * 2 + sl: h * 2 + sl + 1, :],
                        in_=rlb[0:1, :]), waits=[vrb])
                y_done = op("vector",
                            lambda e, h=h, sl=sl, yb=yb:
                            e.scalar_tensor_tensor(
                                yT[:, h * TOWN + sl * 512:
                                   h * TOWN + (sl + 1) * 512],
                                ps[yb][:], 1.0, rlb[:], ALU.mult, ALU.mult),
                            waits=[mmy_last, vrb])
                ps_last[yb] = y_done
                rlb_cons = y_done
                C_cons[sl] = y_done
            kt_cons[kp] = mmy_last
            vt_cons[kp] = mmy_last

        tmp6 = sb("tmp6", [128, 1024], F32, M0 + 0)
        xtile = sb("xtile", [128, 1024], F32, M0 + 4096)
        gtile = sb("gtile", [128, 1024], F32, M0 + 8192)
        bob = sb("bob", [128, D], F32, M0 + 12288)
        bo_row = sb("bo_row", [1, D], F32, M0 + 20480)
        t5g = max(t for t in t5_cons if t is not None)

        tbo = op("sync", lambda e: e.dma_start(out=bo_row[:], in_=i_bor),
                 waits=[y_done, t5g])
        bob_toks = []
        for ec in range(4):
            mmb = op("tensor", lambda e, ec=ec: e.matmul(
                ps[6][:], ones1f[:], bo_row[0:1, ec * 512:(ec + 1) * 512],
                start=True, stop=True), waits=[tbo, ps_last[6]])
            tok = op("vector", lambda e, ec=ec: e.tensor_copy(
                bob[:, ec * 512:(ec + 1) * 512], ps[6][:]),
                waits=[mmb, y_done, t5g])
            ps_last[6] = tok
            bob_toks.append(tok)

        t6_cons = [None, None]
        xt_cons = [t5g, t5g]
        pbg = max(t for t in pb_cons if t is not None)
        gt_cons = [pbg, pbg]
        wo_rd = [None, None]
        st_last = None
        for ec in range(4):
            if ec >= 2:
                wo_tok[ec] = ld_wo_op(ec, [wo_rd[ec % 2]])
            for tt in range(8):
                opar = tt % 2
                sl, tl = tt // 4, tt % 4
                bank = opar
                mm = None
                for hh in range(H):
                    mm = op("tensor",
                            lambda e, hh=hh, sl=sl, tl=tl, ec=ec, bank=bank:
                            e.matmul(
                                ps[bank][:],
                                yT[:, hh * TOWN + sl * 512 + tl * 128:
                                   hh * TOWN + sl * 512 + (tl + 1) * 128],
                                wos[ec % 2][:, hh * 512:(hh + 1) * 512],
                                start=(hh == 0), stop=(hh == H - 1)),
                            waits=[wo_tok[ec], y_done, ps_last[bank]]
                            if hh == 0 else [])
                wo_rd[ec % 2] = mm
                ldx = op("sync", lambda e, tt=tt, ec=ec, opar=opar: e.dma_start(
                    out=xtile[:, opar * 512:(opar + 1) * 512],
                    in_=i_xown[tt * 128:(tt + 1) * 128,
                               ec * 512:(ec + 1) * 512]),
                    waits=[xt_cons[opar]])
                ldg = op("gpsimd", lambda e, tt=tt, ec=ec, opar=opar: e.dma_start(
                    out=gtile[:, opar * 512:(opar + 1) * 512],
                    in_=d_g[tt * 128:(tt + 1) * 128,
                            ec * 512:(ec + 1) * 512]),
                    waits=[gt_cons[opar], st_g_last])
                e1 = op("vector", lambda e, opar=opar, ec=ec, bank=bank: e.tensor_add(
                    tmp6[:, opar * 512:(opar + 1) * 512], ps[bank][:],
                    bob[:, ec * 512:(ec + 1) * 512]),
                    waits=[mm, t6_cons[opar]] + (bob_toks if ec == 0 else []))
                ps_last[bank] = e1
                e2 = op("vector", lambda e, opar=opar: e.tensor_mul(
                    tmp6[:, opar * 512:(opar + 1) * 512],
                    tmp6[:, opar * 512:(opar + 1) * 512],
                    gtile[:, opar * 512:(opar + 1) * 512]), waits=[ldg, e1])
                e3 = op("vector", lambda e, opar=opar: e.tensor_add(
                    tmp6[:, opar * 512:(opar + 1) * 512],
                    tmp6[:, opar * 512:(opar + 1) * 512],
                    xtile[:, opar * 512:(opar + 1) * 512]), waits=[ldx, e2])
                xt_cons[opar] = e3
                gt_cons[opar] = e3
                st_last = op("sync",
                             lambda e, tt=tt, ec=ec, opar=opar: e.dma_start(
                                 out=o_out[tt * 128:(tt + 1) * 128,
                                           ec * 512:(ec + 1) * 512],
                                 in_=tmp6[:, opar * 512:(opar + 1) * 512]),
                             waits=[e3])
                t6_cons[opar] = st_last

        if DEBUG:
            op("sync", lambda e: e.dma_start(out=o_QT, in_=QT[:]),
               waits=[st_last])
            op("sync", lambda e: e.dma_start(out=o_yT, in_=yT[:]),
               waits=[st_last])
        em.emit(block, sems)
    return nc


def kernel(L0_attn_x, L0_attn_Wq, L0_attn_bq, L0_attn_Wk, L0_attn_bk,
           L0_attn_Wv, L0_attn_bv, L0_attn_Wo, L0_attn_bo,
           L0_ffn_w_gate, L0_ffn_w_up, L0_ffn_w_down,
           L0_ffn_s_gate, L0_ffn_s_up, L0_ffn_s_down):
    f32 = np.float32
    x = np.asarray(L0_attn_x, dtype=f32)
    Bsz, T, _ = x.shape

    wvhd = np.ascontiguousarray(
        np.asarray(L0_attn_Wv, f32).transpose(1, 0, 2).reshape(D, D))
    bqT = np.ascontiguousarray(
        np.broadcast_to(np.asarray(L0_attn_bq, f32).T, (128, H)))
    bkT = np.ascontiguousarray(
        np.broadcast_to(np.asarray(L0_attn_bk, f32).T, (128, H)))
    bvr = np.ascontiguousarray(np.asarray(L0_attn_bv, f32).reshape(1, D))
    bor = np.ascontiguousarray(np.asarray(L0_attn_bo, f32).reshape(1, D))
    sgpm = np.ascontiguousarray(
        np.asarray(L0_ffn_s_gate, f32).reshape(NII, 128).T)
    supm = np.ascontiguousarray(
        np.asarray(L0_ffn_s_up, f32).reshape(NII, 128).T)
    sdr = np.ascontiguousarray(np.asarray(L0_ffn_s_down, f32).reshape(1, D))
    dm = np.zeros((128, 4, CH), dtype=ml_dtypes.bfloat16)
    srow = np.arange(128)[:, None]
    tcol = np.arange(CH)[None, :]
    for k in range(4):
        dm[:, k, :] = ((k * 128 + srow) <= tcol).astype(ml_dtypes.bfloat16)
    dm = np.ascontiguousarray(dm.reshape(128, 4 * CH))

    shared = dict(
        wq=np.ascontiguousarray(np.asarray(L0_attn_Wq, f32)),
        wk=np.ascontiguousarray(np.asarray(L0_attn_Wk, f32)),
        wvhd=wvhd,
        wo=np.ascontiguousarray(np.asarray(L0_attn_Wo, f32)),
        bqT=bqT, bkT=bkT, bvr=bvr, bor=bor,
        wg=np.ascontiguousarray(np.asarray(L0_ffn_w_gate, np.int8)),
        wu=np.ascontiguousarray(np.asarray(L0_ffn_w_up, np.int8)),
        wd=np.ascontiguousarray(np.asarray(L0_ffn_w_down, np.int8)),
        sgpm=sgpm, supm=supm, sdr=sdr, dmask=dm,
    )

    in_maps = []
    metas = []
    for i in range(8):
        a, b = i, 7 - i
        rows = [x[0, a * CH:(a + 1) * CH], x[1, b * CH:(b + 1) * CH]]
        rows += [x[0, j * CH:(j + 1) * CH] for j in range(a)]
        rows += [x[1, j * CH:(j + 1) * CH] for j in range(b)]
        xkv = np.concatenate(rows, axis=0)
        big = np.full((2 * NPOS,), -1e9, dtype=f32)
        big[0] = 0.0
        big[2:2 + a] = 0.0
        big[NPOS + 1] = 0.0
        big[NPOS + 2 + a: NPOS + 2 + a + b] = 0.0
        m = dict(shared)
        m["xkvT"] = np.ascontiguousarray(xkv.T)
        m["xown"] = np.ascontiguousarray(xkv[:TOWN])
        m["bigoff"] = np.ascontiguousarray(
            np.broadcast_to(big[None, :], (128, 2 * NPOS)))
        in_maps.append(m)
        metas.append((a, b))

    nc = build_nc()
    res = run_bass_kernel_spmd(nc, in_maps, core_ids=list(range(8)))

    out = np.empty((2, T, D), dtype=f32)
    for i, (a, b) in enumerate(metas):
        oo = np.asarray(res.results[i]["out"], dtype=f32)
        out[0, a * CH:(a + 1) * CH] = oo[:CH]
        out[1, b * CH:(b + 1) * CH] = oo[CH:]
    return out


# version 11
# speedup vs baseline: 1.3505x; 1.0933x over previous
import numpy as np
import ml_dtypes
import concourse.bass as bass
import concourse.mybir as mybir
from concourse.bass_utils import run_bass_kernel_spmd

F32 = mybir.dt.float32
BF16 = mybir.dt.bfloat16
I8 = mybir.dt.int8
AF = mybir.ActivationFunctionType
ALU = mybir.AluOpType
AX = mybir.AxisListType

D = 2048
H = 16
HD = 128
INTER = 8192
CH = 512
NPOS = 9
TKV = NPOS * CH
TOWN = 1024
NCC = D // 128
NII = INTER // 128
SCALE = 1.0 / float(np.sqrt(HD))
EPS = 1e-5

DEBUG = False
DEBUG_ROWS = False

ENGS = ("sync", "gpsimd", "tensor", "vector", "scalar")
DMA_ENGS = ("sync", "gpsimd")


class Em:

    def __init__(self):
        self.ops = []
        self.cur_sid = None
        self.cur_arm = None
        self.nswitch = 0
        self.switch_n = {}

    def op(self, eng, fn, waits=()):
        tok = len(self.ops)
        ws = []
        for w in waits:
            if w is None:
                continue
            if isinstance(w, tuple):
                ws.append(w)
            else:
                assert 0 <= w < tok, f"wait on future token {w} at {tok}"
                ws.append(w)
        self.ops.append([eng, fn, ws, False, self.cur_sid, self.cur_arm])
        return tok

    def switch_begin(self, n):
        sid = self.nswitch
        self.nswitch += 1
        self.switch_n[sid] = n
        self.cur_sid = sid
        return sid

    def arm(self, k):
        self.cur_arm = k

    def switch_end(self):
        self.cur_sid = None
        self.cur_arm = None

    @staticmethod
    def join(sid, eng):
        return ("join", sid, eng)

    def emit(self, nc, block, sems):
        ops = self.ops
        for o in ops:
            if o[0] in DMA_ENGS or o[4] is not None:
                o[3] = True
        for i, (eng, fn, ws, _, sid, arm) in enumerate(ops):
            for w in ws:
                if isinstance(w, tuple):
                    continue
                weng = ops[w][0]
                wsid, warm = ops[w][4], ops[w][5]
                if wsid is not None:
                    assert wsid == sid and warm == arm, \
                        f"cross-arm wait {w}->{i}; use join()"
                if weng == eng and weng == "tensor":
                    continue
                ops[w][3] = True

        segs = []
        for i, o in enumerate(ops):
            sid, arm = o[4], o[5]
            if sid is None:
                if not segs or segs[-1][0] != 'c':
                    segs.append(('c', []))
                segs[-1][1].append(i)
            else:
                if not segs or segs[-1][0] != 's' or segs[-1][1] != sid:
                    segs.append(('s', sid, {}))
                segs[-1][2].setdefault(arm, []).append(i)

        val_after = [0] * len(ops)
        joins = {}
        cnt = {e: 0 for e in ENGS}
        for seg in segs:
            if seg[0] == 'c':
                for i in seg[1]:
                    eng = ops[i][0]
                    if ops[i][3]:
                        cnt[eng] += 16 if eng in DMA_ENGS else 1
                    val_after[i] = cnt[eng]
            else:
                sid, arms = seg[1], seg[2]
                finals = None
                for k in sorted(arms):
                    cnt_a = dict(cnt)
                    for i in arms[k]:
                        eng = ops[i][0]
                        if ops[i][3]:
                            cnt_a[eng] += 16 if eng in DMA_ENGS else 1
                        val_after[i] = cnt_a[eng]
                    if finals is None:
                        finals = cnt_a
                    else:
                        assert finals == cnt_a, \
                            f"switch {sid}: arm {k} counts {cnt_a} != {finals}"
                cnt = finals
                joins[sid] = dict(finals)
        finals_all = dict(cnt)

        def resolve(i):
            eng = ops[i][0]
            rw = {}
            for w in ops[i][2]:
                if isinstance(w, tuple):
                    weng, v = w[2], joins[w[1]][w[2]]
                else:
                    weng = ops[w][0]
                    if weng == eng and weng == "tensor":
                        continue
                    v = val_after[w]
                if v > 0:
                    rw[weng] = max(rw.get(weng, 0), v)
            return rw

        def mk(engname):
            def f(engine):
                hi = {e: 0 for e in ENGS}
                pid = [None]

                def emit_op(i):
                    for weng, v in resolve(i).items():
                        if v > hi[weng]:
                            engine.wait_ge(sems[weng], v)
                            hi[weng] = v
                    inst = ops[i][1](engine)
                    if ops[i][3]:
                        inst.then_inc(sems[engname],
                                      16 if engname in DMA_ENGS else 1)

                for seg in segs:
                    if seg[0] == 'c':
                        for i in seg[1]:
                            if ops[i][0] == engname:
                                emit_op(i)
                    else:
                        sid, arms = seg[1], seg[2]
                        mine = {k: [i for i in v if ops[i][0] == engname]
                                for k, v in arms.items()}
                        if not any(mine.values()):
                            continue
                        if pid[0] is None:
                            pid[0] = engine.partition_id()
                        hi0 = dict(hi)
                        ends = []
                        for k in engine.Switch(pid[0], self.switch_n[sid]):
                            hi = dict(hi0)
                            for i in mine.get(k, []):
                                emit_op(i)
                            ends.append(hi)
                        hi = {e: min(d[e] for d in ends) for e in ENGS}
                for e in ENGS:
                    if finals_all[e] > hi[e]:
                        engine.wait_ge(sems[e], finals_all[e])
            return f
        block.sync(mk("sync"))
        block.gpsimd(mk("gpsimd"))
        block.tensor(mk("tensor"))
        block.vector(mk("vector"))
        block.scalar(mk("scalar"))


def build_nc():
    nc = bass.Bass("TRN2", target_bir_lowering=False)

    i_xkvT = nc.dram_tensor("xkvT", [D, TKV], F32, kind="ExternalInput").ap()
    i_xown = nc.dram_tensor("xown", [TOWN, D], F32, kind="ExternalInput").ap()
    i_wq = nc.dram_tensor("wq", [H, D, HD], F32, kind="ExternalInput").ap()
    i_wk = nc.dram_tensor("wk", [H, D, HD], F32, kind="ExternalInput").ap()
    i_wvhd = nc.dram_tensor("wvhd", [D, D], F32, kind="ExternalInput").ap()
    i_wo = nc.dram_tensor("wo", [D, D], F32, kind="ExternalInput").ap()
    i_bqT = nc.dram_tensor("bqT", [128, H], F32, kind="ExternalInput").ap()
    i_bkT = nc.dram_tensor("bkT", [128, H], F32, kind="ExternalInput").ap()
    i_bvr = nc.dram_tensor("bvr", [1, D], F32, kind="ExternalInput").ap()
    i_bor = nc.dram_tensor("bor", [1, D], F32, kind="ExternalInput").ap()
    i_wg = nc.dram_tensor("wg", [D, INTER], I8, kind="ExternalInput").ap()
    i_wu = nc.dram_tensor("wu", [D, INTER], I8, kind="ExternalInput").ap()
    i_wd = nc.dram_tensor("wd", [INTER, D], I8, kind="ExternalInput").ap()
    i_sg = nc.dram_tensor("sgpm", [128, NII], F32, kind="ExternalInput").ap()
    i_su = nc.dram_tensor("supm", [128, NII], F32, kind="ExternalInput").ap()
    i_sdr = nc.dram_tensor("sdr", [1, D], F32, kind="ExternalInput").ap()
    i_bigoff = nc.dram_tensor("bigoff", [128, 2 * NPOS], F32,
                              kind="ExternalInput").ap()
    i_dmask = nc.dram_tensor("dmask", [128, 4 * CH], BF16,
                             kind="ExternalInput").ap()

    o_out = nc.dram_tensor("out", [TOWN, D], F32, kind="ExternalOutput").ap()

    ik = "ExternalOutput" if DEBUG else "Internal"
    d_xnT = nc.dram_tensor("xnT_d", [D, TKV], BF16, kind=ik).ap()
    d_k = nc.dram_tensor("k_d", [D, TKV], BF16, kind=ik).ap()
    d_v = nc.dram_tensor("v_d", [TKV, D], BF16, kind=ik).ap()
    d_g = nc.dram_tensor("g_d", [TOWN, D], BF16, kind=ik).ap()
    if DEBUG:
        o_QT = nc.dram_tensor("dbg_QT", [128, H * TOWN], BF16,
                              kind="ExternalOutput").ap()
        o_yT = nc.dram_tensor("dbg_yT", [128, H * TOWN], BF16,
                              kind="ExternalOutput").ap()
        o_l = nc.dram_tensor("dbg_l", [2 * H, 512], F32,
                             kind="ExternalOutput").ap()
        o_c = nc.dram_tensor("dbg_c", [2 * H, 512], F32,
                             kind="ExternalOutput").ap()
        o_rl = nc.dram_tensor("dbg_rl", [2 * H, 512], F32,
                              kind="ExternalOutput").ap()

    BASE = 16512
    TOPLIM = 229344
    P0 = BASE
    M0 = BASE + 16384
    Z1 = BASE + 45056
    Z2 = BASE + 110592
    Z3 = BASE + 143360
    Z4 = BASE + 176128
    assert Z4 + 32768 <= TOPLIM

    uid = [0]

    def sb(name, shape, dtype, off):
        uid[0] += 1
        return nc.alloc_sbuf_tensor_at(f"{name}_{uid[0]}", shape, dtype,
                                       offset=off)

    bigoff = sb("bigoff", [128, 2 * NPOS], F32, P0 + 0)
    bkT = sb("bkT", [128, H], F32, P0 + 128)
    bqT = sb("bqT", [128, H], F32, P0 + 192)
    sgpm = sb("sgpm", [128, NII], F32, P0 + 256)
    supm = sb("supm", [128, NII], F32, P0 + 512)
    dmask = sb("dmask", [128, 4 * CH], BF16, P0 + 768)
    bvb = sb("bvb", [128, D], F32, P0 + 4864)
    onesb = sb("onesb", [128, 1], BF16, P0 + 13056)
    ones1b = sb("ones1b", [1, 128], BF16, P0 + 13120)
    ones1f = sb("ones1f", [1, 128], F32, P0 + 13376)
    maxk2 = sb("maxk2", [1, H], F32, P0 + 13888)
    epsc = sb("epsc", [128, 1], F32, P0 + 13952)

    em = Em()
    op = em.op

    from contextlib import ExitStack
    with ExitStack() as ctx:
        ps = [ctx.enter_context(nc.psum_tensor(f"ps{k}", [128, 512], F32))
              for k in range(8)]
        sems = {e: ctx.enter_context(nc.semaphore(f"S_{e}")) for e in ENGS}
        block = ctx.enter_context(nc.Block())

        ps_last = {k: None for k in range(8)}

        t_bigoff = op("sync", lambda e: e.dma_start(out=bigoff[:], in_=i_bigoff))
        t_bkT = op("sync", lambda e: e.dma_start(out=bkT[:], in_=i_bkT))
        t_bqT = op("sync", lambda e: e.dma_start(out=bqT[:], in_=i_bqT))
        t_sg = op("sync", lambda e: e.dma_start(out=sgpm[:], in_=i_sg))
        t_su = op("sync", lambda e: e.dma_start(out=supm[:], in_=i_su))
        t_dmask = op("sync", lambda e: e.dma_start(out=dmask[:], in_=i_dmask))
        t_ones_b = op("vector", lambda e: e.memset(onesb[:], 1.0))
        t_ones_1b = op("vector", lambda e: e.memset(ones1b[:], 1.0))
        t_ones_1f = op("vector", lambda e: e.memset(ones1f[:], 1.0))
        op("vector", lambda e: e.memset(maxk2[:], 0.0))
        t_eps = op("vector", lambda e: e.memset(epsc[:], float(EPS)))

        bvrow = sb("bvrow", [1, D], F32, M0)
        tv = op("sync", lambda e: e.dma_start(out=bvrow[:], in_=i_bvr))
        tok_bvb = None
        bv_mm_last = None
        for ec in range(4):
            mmb = op("tensor", lambda e, ec=ec: e.matmul(
                ps[7][:], ones1f[:], bvrow[0:1, ec * 512:(ec + 1) * 512],
                start=True, stop=True), waits=[tv, t_ones_1f, ps_last[7]])
            bv_mm_last = mmb
            tok_bvb = op("vector", lambda e, ec=ec: e.tensor_copy(
                bvb[:, ec * 512:(ec + 1) * 512], ps[7][:]), waits=[mmb])
            ps_last[7] = tok_bvb

        tmp1 = sb("tmp1", [128, 1024], BF16, M0 + 0)
        sqcc = sb("sqcc", [128, 1024], BF16, M0 + 2048)
        kbuf = sb("kbuf", [128, 1024], BF16, M0 + 4096)
        sqb = sb("sqb", [128, 1024], BF16, M0 + 6144)
        r_m = sb("r_m", [1, 512], F32, M0 + 8192)[:]
        r_msq = sb("r_msq", [1, 512], F32, M0 + 10240)[:]
        r_m2 = sb("r_m2", [1, 512], F32, M0 + 12288)[:]
        r_var = sb("r_var", [1, 512], F32, M0 + 14336)[:]
        r_std = sb("r_std", [1, 512], F32, M0 + 16384)[:]
        r_rstd = sb("r_rstd", [1, 512], F32, M0 + 18432)[:]
        r_mr = sb("r_mr", [1, 512], F32, M0 + 20480)[:]
        r_nrm = sb("r_nrm", [1, 512], F32, M0 + 22528)
        rstdb = sb("rstdb", [128, 512], BF16, M0 + 24576)
        mrb = sb("mrb", [128, 512], BF16, M0 + 25600)

        xT = [sb("xT0", [128, NCC * 512], BF16, Z1),
              sb("xT1", [128, NCC * 512], BF16, Z1 + 16384)]
        wk_sb = sb("wkA", [128, 8 * NCC * 128], BF16, Z1 + 32768)
        wk_sbB = sb("wkB", [128, 8 * NCC * 128], BF16, Z4)
        xnc = [sb("xnc0", [128, NCC * 512], BF16, Z2),
               sb("xnc1", [128, NCC * 512], BF16, Z2 + 16384)]
        xnT_own = sb("xnT_own", [128, NCC * TOWN], BF16, Z3)

        def wk_ap(h, cc):
            if h < 8:
                return wk_sb[:, (h * NCC + cc) * 128:(h * NCC + cc + 1) * 128]
            return wk_sbB[:, ((h - 8) * NCC + cc) * 128:
                          ((h - 8) * NCC + cc + 1) * 128]

        ld_wk = None
        for hq in range(4):
            dst = wk_sb if hq < 2 else wk_sbB
            ld_wk = op("gpsimd", lambda e, hq=hq, dst=dst: e.dma_start(
                out=dst[:].rearrange("p (h cc d) -> p h cc d",
                                     h=8, cc=NCC)[:, (hq % 2) * 4:
                                                  (hq % 2) * 4 + 4],
                in_=i_wk.rearrange("h (cc cin) d -> cin h cc d", cin=128)
                [:, hq * 4:(hq + 1) * 4]))

        xT_cons = [None, None]
        xnc_w = [[], []]
        sq_cons = [None, None]
        t1_cons = [None, None]
        st_xnT = []
        own_copy_tok = []
        xnc2_cons = [None, None]
        kb_cons = [[bv_mm_last], [bv_mm_last]]
        st_k_last = None
        nrm_tok = None
        sqcc_guard = bv_mm_last
        tmp1_guard = bv_mm_last
        chunk_state = {}
        chunk_a1 = [None]

        def ln_chunk(tc):
            p = tc % 2
            ld = op("gpsimd", lambda e, p=p, tc=tc: e.dma_start(
                out=xT[p][:].rearrange("p (cc t) -> p cc t", cc=NCC),
                in_=i_xkvT[:, tc * 512:(tc + 1) * 512]
                .rearrange("(cc cin) t -> cin cc t", cin=128)),
                waits=[xT_cons[p]])
            mmS = mmQ = None
            nonlocal sqcc_guard, tmp1_guard
            for cc in range(NCC):
                q = cc % 2
                sq = op("vector", lambda e, p=p, cc=cc, q=q: e.tensor_mul(
                    sqcc[:, q * 512:(q + 1) * 512],
                    xT[p][:, cc * 512:(cc + 1) * 512],
                    xT[p][:, cc * 512:(cc + 1) * 512]),
                    waits=[ld, sq_cons[q], sqcc_guard])
                sqcc_guard = None
                mmS = op("tensor", lambda e, p=p, cc=cc: e.matmul(
                    ps[6][0:1, :], onesb[:],
                    xT[p][:, cc * 512:(cc + 1) * 512],
                    start=(cc == 0), stop=(cc == NCC - 1)),
                    waits=[ld, t_ones_b] + ([ps_last[6]] if cc == 0 else []))
                mmQ = op("tensor", lambda e, q=q, cc=cc: e.matmul(
                    ps[5][0:1, :], onesb[:], sqcc[:, q * 512:(q + 1) * 512],
                    start=(cc == 0), stop=(cc == NCC - 1)),
                    waits=[sq] + ([ps_last[5]] if cc == 0 else []))
                sq_cons[q] = mmQ
            v1 = op("vector", lambda e: e.tensor_scalar_mul(
                r_m, ps[6][0:1, :], 1.0 / D), waits=[mmS])
            v2 = op("vector", lambda e: e.tensor_scalar_mul(
                r_msq, ps[5][0:1, :], 1.0 / D), waits=[mmQ])
            ps_last[6] = v1
            ps_last[5] = v2
            v3 = op("vector", lambda e: e.tensor_mul(
                r_m2, r_m, r_m), waits=[v1])
            v4 = op("vector", lambda e: e.tensor_sub(
                r_var, r_msq, r_m2), waits=[v2, v3])
            a1 = op("scalar", lambda e: e.activation(
                r_std, r_var, AF.Sqrt, bias=epsc[0:1, 0:1], scale=1.0),
                waits=[v4, t_eps])
            chunk_a1[0] = a1
            v5 = op("vector", lambda e: e.reciprocal(r_rstd, r_std),
                    waits=[a1])
            v6 = op("vector", lambda e: e.tensor_mul(
                r_mr, r_m, r_rstd), waits=[v5, v1])
            mB1 = op("tensor", lambda e: e.matmul(
                ps[4][:], ones1f[:], r_rstd, start=True, stop=True),
                waits=[v5, t_ones_1f, ps_last[4]])
            vB1 = op("vector", lambda e: e.tensor_copy(rstdb[:], ps[4][:]),
                     waits=[mB1])
            ps_last[4] = vB1
            mB2 = op("tensor", lambda e: e.matmul(
                ps[3][:], ones1f[:], r_mr, start=True, stop=True),
                waits=[v6, ps_last[3]])
            vB2 = op("vector", lambda e: e.tensor_copy(mrb[:], ps[3][:]),
                     waits=[mB2])
            ps_last[3] = vB2
            last_apply = None
            for cc in range(NCC):
                q = cc % 2
                va = op("vector", lambda e, p=p, cc=cc, q=q: e.tensor_mul(
                    tmp1[:, q * 512:(q + 1) * 512],
                    xT[p][:, cc * 512:(cc + 1) * 512], rstdb[:]),
                    waits=[vB1, t1_cons[q], tmp1_guard] +
                    (xnc_w[p] if cc == 0 else []))
                tmp1_guard = None
                vb = op("vector", lambda e, p=p, cc=cc, q=q: e.tensor_sub(
                    xnc[p][:, cc * 512:(cc + 1) * 512],
                    tmp1[:, q * 512:(q + 1) * 512], mrb[:]),
                    waits=[vB2, va])
                t1_cons[q] = vb
                last_apply = vb
            xT_cons[p] = last_apply
            deps = [last_apply]
            if tc < 2:
                oc = op("vector", lambda e, p=p, tc=tc: e.tensor_copy(
                    xnT_own[:].rearrange("p (cc t) -> p cc t", cc=NCC)
                    [:, :, tc * 512:(tc + 1) * 512],
                    xnc[p][:].rearrange("p (cc t) -> p cc t", cc=NCC)),
                    waits=[last_apply])
                own_copy_tok.append(oc)
                deps.append(oc)
            st = op("sync", lambda e, p=p, tc=tc: e.dma_start(
                out=d_xnT.rearrange("(cc cin) t -> cin cc t", cin=128)
                [:, :, tc * 512:(tc + 1) * 512],
                in_=xnc[p][:].rearrange("p (cc t) -> p cc t", cc=NCC)),
                waits=deps)
            st_xnT.append(st)
            chunk_state[tc] = (p, last_apply, st)

        def kproj_chunk(tc):
            nonlocal st_k_last, nrm_tok
            p, last_apply, st = chunk_state[tc]
            ev_toks = [None, None]

            def norm_block(h):
                nonlocal nrm_tok
                kp = h % 2
                sqo = op("vector", lambda e, kp=kp: e.tensor_mul(
                    sqb[:, kp * 512:(kp + 1) * 512],
                    kbuf[:, kp * 512:(kp + 1) * 512],
                    kbuf[:, kp * 512:(kp + 1) * 512]),
                    waits=[ev_toks[kp], nrm_tok])
                mmn = op("tensor", lambda e, kp=kp: e.matmul(
                    ps[6][0:1, :], onesb[:],
                    sqb[:, kp * 512:(kp + 1) * 512], start=True, stop=True),
                    waits=[sqo, ps_last[6]])
                prev_nrm = nrm_tok
                mx1 = op("vector", lambda e: e.reduce_max(
                    r_nrm[0:1, 0:1], ps[6][0:1, :], AX.X), waits=[mmn])
                ps_last[6] = mx1
                nrm_tok = op("vector", lambda e, h=h: e.tensor_max(
                    maxk2[0:1, h:h + 1], maxk2[0:1, h:h + 1],
                    r_nrm[0:1, 0:1]), waits=[mx1, prev_nrm])
                kb_cons[h % 2].append(mmn)

            last_mm_chunk = None
            for h in range(H):
                kp = h % 2
                bank = kp
                mm = None
                for cc in range(NCC):
                    mm = op("tensor",
                            lambda e, p=p, h=h, cc=cc, bank=bank: e.matmul(
                                ps[bank][:], wk_ap(h, cc),
                                xnc[p][:, cc * 512:(cc + 1) * 512],
                                start=(cc == 0), stop=(cc == NCC - 1)),
                            waits=[last_apply, ld_wk, ps_last[bank]]
                            if cc == 0 else [])
                last_mm_chunk = mm
                ev = op("vector",
                        lambda e, h=h, kp=kp, bank=bank: e.tensor_single_scalar(
                            kbuf[:, kp * 512:(kp + 1) * 512], ps[bank][:],
                            bkT[:, h:h + 1], ALU.add),
                        waits=[mm, t_bkT] + kb_cons[kp])
                ev_toks[kp] = ev
                ps_last[bank] = ev
                stk = op("sync", lambda e, h=h, tc=tc, kp=kp: e.dma_start(
                    out=d_k[h * 128:(h + 1) * 128, tc * 512:(tc + 1) * 512],
                    in_=kbuf[:, kp * 512:(kp + 1) * 512]),
                    waits=[ev])
                st_k_last = stk
                kb_cons[kp] = [stk]
                if h > 0:
                    norm_block(h - 1)
            norm_block(H - 1)
            xnc2_cons[p] = last_mm_chunk
            xnc_w[p] = [st, last_mm_chunk]

        for tc in range(NPOS + 1):
            if tc < NPOS:
                ln_chunk(tc)
            if tc >= 1:
                kproj_chunk(tc - 1)
        nrm_final = nrm_tok
        mmn_final = kb_cons[(H - 1) % 2][-1]

        vstage = sb("vstage", [128, 1024], BF16, M0 + 26624)
        wv_sb = sb("wv", [128, NCC * D], BF16, Z1)
        ld_wv = op("gpsimd", lambda e: e.dma_start(
            out=wv_sb[:].rearrange("p (cc e) -> p cc e", cc=NCC),
            in_=i_wvhd.rearrange("(cc cin) e -> cin cc e", cin=128)),
            waits=[xnc2_cons[0], xnc2_cons[1], ld_wk,
                   xT_cons[0], xT_cons[1]])

        vs_cons = [[], []]
        st_v_last = None
        xncv_cons = [None, None]
        for tc in range(NPOS):
            p = tc % 2
            ld = op("sync", lambda e, p=p, tc=tc: e.dma_start(
                out=xnc[p][:].rearrange("p (cc t) -> p cc t", cc=NCC),
                in_=d_xnT.rearrange("(cc cin) t -> cin cc t", cin=128)
                [:, :, tc * 512:(tc + 1) * 512]),
                waits=[st_xnT[tc], st_k_last, xncv_cons[p],
                       xnc2_cons[p] if tc < 2 else None])
            last_mm_chunk = None
            for hg in range(4):
                for tt in range(4):
                    vp = (hg * 4 + tt) % 2
                    bank = vp
                    mm = None
                    for cc in range(NCC):
                        mm = op("tensor",
                                lambda e, p=p, hg=hg, tt=tt, cc=cc, bank=bank:
                                e.matmul(
                                    ps[bank][:],
                                    xnc[p][:, cc * 512 + tt * 128:
                                           cc * 512 + (tt + 1) * 128],
                                    wv_sb[:, cc * D + hg * 512:
                                          cc * D + (hg + 1) * 512],
                                    start=(cc == 0), stop=(cc == NCC - 1)),
                                waits=[ld, ld_wv, ps_last[bank]]
                                if cc == 0 else [])
                    last_mm_chunk = mm
                    ev = op("vector",
                            lambda e, hg=hg, vp=vp, bank=bank: e.tensor_add(
                                vstage[:, vp * 512:(vp + 1) * 512],
                                ps[bank][:],
                                bvb[:, hg * 512:(hg + 1) * 512]),
                            waits=[mm, tok_bvb] + vs_cons[vp])
                    ps_last[bank] = ev
                    stv = op("sync",
                             lambda e, tc=tc, tt=tt, hg=hg, vp=vp: e.dma_start(
                                 out=d_v[tc * 512 + tt * 128:
                                         tc * 512 + (tt + 1) * 128,
                                         hg * 512:(hg + 1) * 512],
                                 in_=vstage[:, vp * 512:(vp + 1) * 512]),
                             waits=[ev])
                    st_v_last = stv
                    vs_cons[vp] = [stv]
            xncv_cons[p] = last_mm_chunk

        wq_sb = sb("wq", [128, H * NCC * 128], BF16, Z1)
        ld_wq = None
        for hq in range(4):
            ld_wq = op("gpsimd", lambda e, hq=hq: e.dma_start(
                out=wq_sb[:].rearrange("p (h cc d) -> p h cc d",
                                       h=H, cc=NCC)[:, hq * 4:(hq + 1) * 4],
                in_=i_wq.rearrange("h (cc cin) d -> cin h cc d", cin=128)
                [:, hq * 4:(hq + 1) * 4]),
                waits=[xncv_cons[0], xncv_cons[1], ld_wv] if hq == 0 else [])
        QT = sb("QT", [128, H * TOWN], BF16, Z4)
        q_ev_last = None
        q_mm_last = None
        for h in range(H):
            for sl in range(2):
                qp = (h * 2 + sl) % 2
                bank = qp
                mm = None
                for cc in range(NCC):
                    mm = op("tensor",
                            lambda e, h=h, sl=sl, cc=cc, bank=bank: e.matmul(
                                ps[bank][:],
                                wq_sb[:, (h * NCC + cc) * 128:
                                      (h * NCC + cc + 1) * 128],
                                xnT_own[:, cc * TOWN + sl * 512:
                                        cc * TOWN + (sl + 1) * 512],
                                start=(cc == 0), stop=(cc == NCC - 1)),
                            waits=([ld_wq, ps_last[bank]] + own_copy_tok)
                            if cc == 0 else [])
                q_mm_last = mm
                q_ev_last = op("vector",
                               lambda e, h=h, sl=sl, bank=bank: e.tensor_scalar(
                                   QT[:, h * TOWN + sl * 512:
                                      h * TOWN + (sl + 1) * 512],
                                   ps[bank][:], bqT[:, h:h + 1], float(SCALE),
                                   ALU.add, ALU.mult),
                               waits=[mm, t_bqT])
                ps_last[bank] = q_ev_last

        sd_row = sb("sd_row", [1, D], F32, M0 + 0)
        sdownb = sb("sdownb", [128, D], F32, M0 + 8192)
        gsil = sb("gsil", [128, 1024], BF16, M0 + 16384)
        tmp3 = sb("tmp3", [128, 1024], F32, M0 + 18432)

        tsd = op("sync", lambda e: e.dma_start(out=sd_row[:], in_=i_sdr),
                 waits=[t1_cons[0], t1_cons[1], st_k_last, nrm_final,
                        mmn_final])
        sd_toks = []
        for ec in range(4):
            mmb = op("tensor", lambda e, ec=ec: e.matmul(
                ps[7][:], ones1f[:], sd_row[0:1, ec * 512:(ec + 1) * 512],
                start=True, stop=True), waits=[tsd, ps_last[7]])
            tok = op("vector", lambda e, ec=ec: e.tensor_copy(
                sdownb[:, ec * 512:(ec + 1) * 512], ps[7][:]),
                waits=[mmb, nrm_final, chunk_a1[0]])
            ps_last[7] = tok
            sd_toks.append(tok)

        fuseT = sb("fuseT", [128, NII * 512], BF16, Z1)
        slab = [sb("slab0", [128, NCC * 512], BF16, Z2),
                sb("slab1", [128, NCC * 512], BF16, Z2 + 16384)]
        fuse_toks = [None] * NII
        st_g_last = None
        ds0 = ds1 = None
        gs_cons = [[], []]
        t3_cons = [None, None]
        for fc in range(2):
            gate_prev, up_prev = ds0, ds1
            for grp in range(NII // 4):
                extra_g = [xncv_cons[0], xncv_cons[1]] if (fc == 0 and grp == 0) else []
                ld_g = op("gpsimd", lambda e, grp=grp: e.dma_start(
                    out=slab[0][:].rearrange("p (cc i) -> p cc i", cc=NCC),
                    in_=i_wg.rearrange("(cc cin) i -> cin cc i", cin=128)
                    [:, :, grp * 512:(grp + 1) * 512]),
                    waits=[gate_prev] + extra_g)
                ld_u = op("gpsimd", lambda e, grp=grp: e.dma_start(
                    out=slab[1][:].rearrange("p (cc i) -> p cc i", cc=NCC),
                    in_=i_wu.rearrange("(cc cin) i -> cin cc i", cin=128)
                    [:, :, grp * 512:(grp + 1) * 512]),
                    waits=[up_prev] + extra_g)
                for isub in range(4):
                    ii = grp * 4 + isub
                    gp = ii % 2
                    bank = gp
                    bank2 = 2 + gp
                    mm = None
                    for cc in range(NCC):
                        mm = op("tensor",
                                lambda e, isub=isub, cc=cc, fc=fc, bank=bank:
                                e.matmul(
                                    ps[bank][:],
                                    slab[0][:, cc * 512 + isub * 128:
                                            cc * 512 + (isub + 1) * 128],
                                    xnT_own[:, cc * TOWN + fc * 512:
                                            cc * TOWN + (fc + 1) * 512],
                                    start=(cc == 0), stop=(cc == NCC - 1)),
                                waits=[ld_g, ps_last[bank], q_mm_last]
                                if cc == 0 else [])
                    gate_prev = mm
                    sil = op("scalar",
                             lambda e, ii=ii, gp=gp, bank=bank: e.activation(
                                 gsil[:, gp * 512:(gp + 1) * 512], ps[bank][:],
                                 AF.Silu, scale=sgpm[:, ii:ii + 1]),
                             waits=[mm, t_sg] + gs_cons[gp])
                    ps_last[bank] = sil
                    mmu = None
                    for cc in range(NCC):
                        mmu = op("tensor",
                                 lambda e, isub=isub, cc=cc, fc=fc, bank2=bank2:
                                 e.matmul(
                                     ps[bank2][:],
                                     slab[1][:, cc * 512 + isub * 128:
                                             cc * 512 + (isub + 1) * 128],
                                     xnT_own[:, cc * TOWN + fc * 512:
                                             cc * TOWN + (fc + 1) * 512],
                                     start=(cc == 0), stop=(cc == NCC - 1)),
                                 waits=[ld_u, ps_last[bank2]]
                                 if cc == 0 else [])
                    up_prev = mmu
                    fu = op("vector",
                            lambda e, ii=ii, gp=gp, bank2=bank2:
                            e.scalar_tensor_tensor(
                                fuseT[:, ii * 512:(ii + 1) * 512],
                                ps[bank2][:], supm[:, ii:ii + 1],
                                gsil[:, gp * 512:(gp + 1) * 512],
                                ALU.mult, ALU.mult),
                            waits=[mmu, sil, t_su, fuse_toks[ii]])
                    ps_last[bank2] = fu
                    gs_cons[gp] = [fu]
                    fuse_toks[ii] = fu
            ds = [gate_prev, up_prev]
            for ec in range(4):
                mm_last = None
                for half in range(4):
                    dp = half % 2
                    ldd = op("gpsimd",
                             lambda e, half=half, ec=ec, dp=dp: e.dma_start(
                                 out=slab[dp][:].rearrange(
                                     "p (k i) -> p k i", k=NCC),
                                 in_=i_wd.rearrange(
                                     "(ii iin) e -> iin ii e", iin=128)
                                 [:, half * 16:(half + 1) * 16,
                                  ec * 512:(ec + 1) * 512]),
                             waits=[ds[dp]])
                    for k in range(16):
                        ii = half * 16 + k
                        for tt in range(4):
                            bank = 4 + tt
                            w = []
                            if k == 0 and tt == 0:
                                w.append(ldd)
                            if tt == 0:
                                w.append(fuse_toks[ii])
                            if ii == 0:
                                w.append(ps_last[bank])
                            mm_last = op(
                                "tensor",
                                lambda e, ii=ii, k=k, tt=tt, dp=dp, bank=bank:
                                e.matmul(
                                    ps[bank][:],
                                    fuseT[:, ii * 512 + tt * 128:
                                          ii * 512 + (tt + 1) * 128],
                                    slab[dp][:, k * 512:(k + 1) * 512],
                                    start=(ii == 0), stop=(ii == NII - 1)),
                                waits=w)
                    ds[dp] = mm_last
                for tt in range(4):
                    bank = 4 + tt
                    tp = tt % 2
                    vm = op("vector",
                            lambda e, tt=tt, ec=ec, tp=tp, bank=bank:
                            e.tensor_mul(
                                tmp3[:, tp * 512:(tp + 1) * 512], ps[bank][:],
                                sdownb[:, ec * 512:(ec + 1) * 512]),
                            waits=[mm_last, t3_cons[tp]])
                    ps_last[bank] = vm
                    si = op("scalar", lambda e, tp=tp: e.activation(
                        gsil[:, tp * 512:(tp + 1) * 512],
                        tmp3[:, tp * 512:(tp + 1) * 512], AF.Silu),
                        waits=[vm] + gs_cons[tp])
                    stg = op("sync",
                             lambda e, fc=fc, tt=tt, ec=ec, tp=tp: e.dma_start(
                                 out=d_g[fc * 512 + tt * 128:
                                         fc * 512 + (tt + 1) * 128,
                                         ec * 512:(ec + 1) * 512],
                                 in_=gsil[:, tp * 512:(tp + 1) * 512]),
                             waits=[si])
                    st_g_last = stg
                    gs_cons[tp] = [stg]
                    t3_cons[tp] = si
            ds0, ds1 = ds[0], ds[1]

        Csb = sb("Csb", [128, 1024], F32, M0 + 0)
        tmp5 = sb("tmp5", [128, 3 * 512], F32, M0 + 4096)
        pbuf = sb("pbuf", [128, 8 * 512], BF16, M0 + 10240)
        rlb = sb("rlb", [128, 512], F32, M0 + 18432)
        sqb5 = sb("sqb5", [128, 512], BF16, M0 + 20480)
        R1 = sb("R1", [128, 512], F32, M0 + 21504)
        R2 = sb("R2", [128, 512], F32, M0 + 23552)
        r_c = R1[32:33, :]
        r_ln = R1[64:65, :]
        r_c2 = R1[0:1, :]
        r_rl = R2[0:1, :]

        ktb = [sb("kt0", [128, TKV], BF16, Z1),
               sb("kt1", [128, TKV], BF16, Z1 + TKV * 2)]
        vtb = [sb("vt0", [128, TKV], BF16, Z1 + TKV * 4),
               sb("vt1", [128, TKV], BF16, Z1 + TKV * 6)]
        yT = sb("yT", [128, H * TOWN], BF16, Z3)

        wos = [sb("wos0", [128, NCC * 512], BF16, Z2),
               sb("wos1", [128, NCC * 512], BF16, Z2 + 16384)]

        def ld_wo_op(ec, waits):
            return op("gpsimd", lambda e, ec=ec: e.dma_start(
                out=wos[ec % 2][:].rearrange("p (dc e) -> p dc e", dc=NCC),
                in_=i_wo.rearrange("(dc din) e -> din dc e", din=128)
                [:, :, ec * 512:(ec + 1) * 512]), waits=waits)

        wo_tok = [None] * 4
        wo_tok[0] = ld_wo_op(0, [ds0, ds1])
        wo_tok[1] = ld_wo_op(1, [ds0, ds1])

        DELAY = 6
        NPB = 8
        NT5 = 3
        sbank = (0, 1, 4, 5)
        npair = NPOS * 4

        def c_chain(h, sl, lb, w_sqo, w_vc1, w_mmc):
            sqo = op("vector", lambda e, h=h, sl=sl: e.tensor_mul(
                sqb5[:],
                QT[:, h * TOWN + sl * 512: h * TOWN + (sl + 1) * 512],
                QT[:, h * TOWN + sl * 512: h * TOWN + (sl + 1) * 512]),
                waits=[q_ev_last, st_v_last, st_g_last] + w_sqo)
            mmc = op("tensor", lambda e, lb=lb: e.matmul(
                ps[lb][0:1, :], onesb[:], sqb5[:], start=True, stop=True),
                waits=[sqo] + w_mmc)
            vc1 = op("vector", lambda e, h=h, lb=lb: e.tensor_scalar_mul(
                r_c, ps[lb][0:1, :], maxk2[0:1, h:h + 1]),
                waits=[mmc, st_g_last] + w_vc1)
            acl = op("scalar", lambda e: e.activation(
                r_ln, r_c, AF.Ln), waits=[vc1])
            ac = op("scalar", lambda e: e.activation(
                r_c2, r_ln, AF.Exp, scale=0.5), waits=[acl])
            mcb = op("tensor", lambda e, lb=lb: e.matmul(
                ps[lb][:], ones1f[:], r_c2, start=True, stop=True),
                waits=[ac, vc1])
            vcb = op("vector", lambda e, sl=sl, lb=lb: e.tensor_copy(
                Csb[:, sl * 512:(sl + 1) * 512], ps[lb][:]),
                waits=[mcb])
            return {"sqo": sqo, "mmc": mmc, "vcb": vcb, "ac": ac}

        def pairs_of(a):
            A = [0] + list(range(2, 2 + a))
            B = [1] + list(range(2 + a, NPOS))
            lst = [(0, p) for p in A] + [(1, p) for p in B]
            return ([(sl, p, st) for (sl, p) in lst for st in range(4)],
                    4 * len(A))

        chain00 = c_chain(0, 0, 3, [], [], [ps_last[3]])
        first5 = [st_k_last, st_v_last, st_g_last, ds0, ds1]
        kt_cons = [None, None]
        vt_cons = [None, None]
        rlb_init = st_g_last
        prev_sid = None
        y_done = None

        for g in range(H // 2):
            ldx_toks = {}
            for h in (2 * g, 2 * g + 1):
                kp = h % 2
                jprev = ([em.join(prev_sid, "tensor")]
                         if prev_sid is not None else [])
                ldk = op("sync", lambda e, h=h, kp=kp: e.dma_start(
                    out=ktb[kp][:], in_=d_k[h * 128:(h + 1) * 128, :]),
                    waits=jprev + (first5 if g == 0 else []))
                ldvs = []
                for pos in range(NPOS):
                    ldvs.append(op(
                        "sync", lambda e, h=h, kp=kp, pos=pos: e.dma_start(
                            out=vtb[kp][:].rearrange(
                                "p (pos tl d) -> p pos tl d",
                                pos=NPOS, tl=4)[:, pos],
                            in_=d_v[pos * 512:(pos + 1) * 512,
                                    h * 128:(h + 1) * 128]
                            .rearrange("(tl s) d -> s tl d", s=128)),
                        waits=[]))
                ldx_toks[h] = (ldk, ldvs)
            sid = em.switch_begin(8)
            for a in range(8):
                em.arm(a)
                plist, nA = pairs_of(a)
                jt = (em.join(prev_sid, "tensor")
                      if prev_sid is not None else None)
                jv = (em.join(prev_sid, "vector")
                      if prev_sid is not None else None)
                js = (em.join(prev_sid, "scalar")
                      if prev_sid is not None else None)
                sba = {b: jv for b in sbank}
                yl_ps = {2: jv, 3: jv, 6: jv, 7: jv}
                mrb_prev = jt
                rlb_cons = rlb_init if g == 0 else jv
                pba = [jt] * NPB
                t5a = [js] * NT5
                chA = {2 * g: chain00 if g == 0 else None}
                if g > 0:
                    chA[2 * g] = {"vcb": jv, "mmc": jt, "ac": js}
                for h in (2 * g, 2 * g + 1):
                    kp = h % 2
                    ldk, ldvs = ldx_toks[h]
                    chB = c_chain(h, 1, 7, [chA[h]["mmc"]], [chA[h]["ac"]],
                                  [yl_ps[7]])
                    cbA = chA[h]["vcb"]
                    pt_toks = {}
                    mml_last = {0: None, 1: None}
                    mmy_last = {0: None, 1: None}

                    def consume(j):
                        sl, pos, st = plist[j]
                        lbank = 3 if sl == 0 else 7
                        ybank = 2 if sl == 0 else 6
                        first = (j == 0) or (j == nA)
                        last = (j == nA - 1) or (j == npair - 1)
                        pb = j % NPB
                        pt = pt_toks[j]
                        mml_last[sl] = op(
                            "tensor", lambda e, pb=pb, lbank=lbank,
                            first=first, last=last: e.matmul(
                                ps[lbank][0:1, :], onesb[:],
                                pbuf[:, pb * 512:(pb + 1) * 512],
                                start=first, stop=last),
                            waits=[pt] + ([yl_ps[lbank], chB["vcb"]]
                                          if first else []))
                        mmy_last[sl] = op(
                            "tensor", lambda e, pb=pb, kp=kp, ybank=ybank,
                            pos=pos, st=st, first=first, last=last: e.matmul(
                                ps[ybank][:],
                                vtb[kp][:, (pos * 4 + st) * 128:
                                        (pos * 4 + st + 1) * 128],
                                pbuf[:, pb * 512:(pb + 1) * 512],
                                start=first, stop=last),
                            waits=[ldvs[pos]] + ([yl_ps[ybank]]
                                                 if first else []))
                        pba[j % NPB] = mmy_last[sl]

                    for j in range(npair):
                        sl, pos, st = plist[j]
                        sp = j % NT5
                        pb = j % NPB
                        bank = sbank[j % 4]
                        cb = cbA if sl == 0 else chB["vcb"]
                        mms = op("tensor",
                                 lambda e, h=h, sl=sl, pos=pos, st=st, kp=kp,
                                 bank=bank: e.matmul(
                                     ps[bank][:],
                                     ktb[kp][:, pos * 512 + st * 128:
                                             pos * 512 + (st + 1) * 128],
                                     QT[:, h * TOWN + sl * 512:
                                        h * TOWN + (sl + 1) * 512],
                                     start=True, stop=True),
                                 waits=[ldk, sba[bank]])
                        stt = op("vector",
                                 lambda e, sp=sp, sl=sl, bank=bank:
                                 e.scalar_tensor_tensor(
                                     tmp5[:, sp * 512:(sp + 1) * 512],
                                     ps[bank][:], 1.0,
                                     Csb[:, sl * 512:(sl + 1) * 512],
                                     ALU.mult, ALU.subtract),
                                 waits=[mms, cb, t5a[sp]])
                        sba[bank] = stt
                        ex = op("scalar",
                                lambda e, sp=sp, pb=pb: e.activation(
                                    pbuf[:, pb * 512:(pb + 1) * 512],
                                    tmp5[:, sp * 512:(sp + 1) * 512],
                                    AF.Exp),
                                waits=[stt, pba[pb]])
                        t5a[sp] = ex
                        pt = ex
                        if pos == sl:
                            pt = op("vector",
                                    lambda e, pb=pb, st=st: e.tensor_mul(
                                        pbuf[:, pb * 512:(pb + 1) * 512],
                                        pbuf[:, pb * 512:(pb + 1) * 512],
                                        dmask[:, st * 512:(st + 1) * 512]),
                                    waits=[ex, t_dmask])
                        pt_toks[j] = pt
                        if j >= DELAY:
                            consume(j - DELAY)
                    for j in range(npair - DELAY, npair):
                        consume(j)
                    if h + 1 < H:
                        chA[h + 1] = c_chain(h + 1, 0, 0,
                                             [chB["mmc"]], [chB["ac"]],
                                             [sba[0]])
                        sba[0] = chA[h + 1]["vcb"]
                    vrecA = op("vector", lambda e: e.reciprocal(
                        r_rl, ps[3][0:1, :]),
                        waits=[mml_last[0], mrb_prev])
                    mrbA = op("tensor", lambda e: e.matmul(
                        ps[5][:], ones1f[:], r_rl, start=True, stop=True),
                        waits=[vrecA, sba[5]])
                    vrbA = op("vector", lambda e: e.tensor_copy(
                        rlb[:], ps[5][:]), waits=[mrbA, rlb_cons])
                    y_doneA = op("vector",
                                 lambda e, h=h: e.scalar_tensor_tensor(
                                     yT[:, h * TOWN: h * TOWN + 512],
                                     ps[2][:], 1.0, rlb[:],
                                     ALU.mult, ALU.mult),
                                 waits=[mmy_last[0], vrbA])
                    vrecB = op("vector", lambda e: e.reciprocal(
                        r_rl, ps[7][0:1, :]), waits=[mml_last[1], mrbA])
                    mrbB = op("tensor", lambda e: e.matmul(
                        ps[5][:], ones1f[:], r_rl, start=True, stop=True),
                        waits=[vrecB, vrbA])
                    vrbB = op("vector", lambda e: e.tensor_copy(
                        rlb[:], ps[5][:]), waits=[mrbB, y_doneA])
                    y_doneB = op("vector",
                                 lambda e, h=h: e.scalar_tensor_tensor(
                                     yT[:, h * TOWN + 512:
                                        h * TOWN + 1024],
                                     ps[6][:], 1.0, rlb[:],
                                     ALU.mult, ALU.mult),
                                 waits=[mmy_last[1], vrbB])
                    sba[5] = vrbB
                    yl_ps[2] = y_doneA
                    yl_ps[3] = vrecA
                    yl_ps[6] = y_doneB
                    yl_ps[7] = vrecB
                    mrb_prev = mrbB
                    rlb_cons = y_doneB
            em.switch_end()
            prev_sid = sid
        jt = em.join(prev_sid, "tensor")
        jv = em.join(prev_sid, "vector")
        js = em.join(prev_sid, "scalar")
        y_done = jv
        ex_last = js
        for b in range(8):
            ps_last[b] = jv if b in (2, 3, 6, 7) else jv
        ps_last[5] = jv

        tmp6 = sb("tmp6", [128, 1024], F32, M0 + 0)
        xtile = sb("xtile", [128, 1024], F32, M0 + 4096)
        gtile = sb("gtile", [128, 1024], F32, M0 + 8192)
        bob = sb("bob", [128, D], F32, M0 + 12288)
        bo_row = sb("bo_row", [1, D], F32, M0 + 20480)
        t5g = ex_last

        tbo = op("sync", lambda e: e.dma_start(out=bo_row[:], in_=i_bor),
                 waits=[y_done, t5g])
        bob_toks = []
        for ec in range(4):
            mmb = op("tensor", lambda e, ec=ec: e.matmul(
                ps[6][:], ones1f[:], bo_row[0:1, ec * 512:(ec + 1) * 512],
                start=True, stop=True), waits=[tbo, ps_last[6]])
            tok = op("vector", lambda e, ec=ec: e.tensor_copy(
                bob[:, ec * 512:(ec + 1) * 512], ps[6][:]),
                waits=[mmb, y_done, t5g])
            ps_last[6] = tok
            bob_toks.append(tok)

        t6_cons = [None, None]
        xt_cons = [t5g, t5g]
        pbg = em.join(prev_sid, "tensor")
        gt_cons = [pbg, pbg]
        wo_rd = [None, None]
        st_last = None
        for ec in range(4):
            if ec >= 2:
                wo_tok[ec] = ld_wo_op(ec, [wo_rd[ec % 2]])
            for tt in range(8):
                opar = tt % 2
                sl, tl = tt // 4, tt % 4
                bank = opar
                mm = None
                for hh in range(H):
                    mm = op("tensor",
                            lambda e, hh=hh, sl=sl, tl=tl, ec=ec, bank=bank:
                            e.matmul(
                                ps[bank][:],
                                yT[:, hh * TOWN + sl * 512 + tl * 128:
                                   hh * TOWN + sl * 512 + (tl + 1) * 128],
                                wos[ec % 2][:, hh * 512:(hh + 1) * 512],
                                start=(hh == 0), stop=(hh == H - 1)),
                            waits=[wo_tok[ec], y_done, ps_last[bank]]
                            if hh == 0 else [])
                wo_rd[ec % 2] = mm
                ldx = op("sync", lambda e, tt=tt, ec=ec, opar=opar: e.dma_start(
                    out=xtile[:, opar * 512:(opar + 1) * 512],
                    in_=i_xown[tt * 128:(tt + 1) * 128,
                               ec * 512:(ec + 1) * 512]),
                    waits=[xt_cons[opar]])
                ldg = op("gpsimd", lambda e, tt=tt, ec=ec, opar=opar: e.dma_start(
                    out=gtile[:, opar * 512:(opar + 1) * 512],
                    in_=d_g[tt * 128:(tt + 1) * 128,
                            ec * 512:(ec + 1) * 512]),
                    waits=[gt_cons[opar], st_g_last])
                e1 = op("vector", lambda e, opar=opar, ec=ec, bank=bank: e.tensor_add(
                    tmp6[:, opar * 512:(opar + 1) * 512], ps[bank][:],
                    bob[:, ec * 512:(ec + 1) * 512]),
                    waits=[mm, t6_cons[opar]] + (bob_toks if ec == 0 else []))
                ps_last[bank] = e1
                e2 = op("vector", lambda e, opar=opar: e.tensor_mul(
                    tmp6[:, opar * 512:(opar + 1) * 512],
                    tmp6[:, opar * 512:(opar + 1) * 512],
                    gtile[:, opar * 512:(opar + 1) * 512]), waits=[ldg, e1])
                e3 = op("vector", lambda e, opar=opar: e.tensor_add(
                    tmp6[:, opar * 512:(opar + 1) * 512],
                    tmp6[:, opar * 512:(opar + 1) * 512],
                    xtile[:, opar * 512:(opar + 1) * 512]), waits=[ldx, e2])
                xt_cons[opar] = e3
                gt_cons[opar] = e3
                st_last = op("sync",
                             lambda e, tt=tt, ec=ec, opar=opar: e.dma_start(
                                 out=o_out[tt * 128:(tt + 1) * 128,
                                           ec * 512:(ec + 1) * 512],
                                 in_=tmp6[:, opar * 512:(opar + 1) * 512]),
                             waits=[e3])
                t6_cons[opar] = st_last

        if DEBUG:
            op("sync", lambda e: e.dma_start(out=o_QT, in_=QT[:]),
               waits=[st_last])
            op("sync", lambda e: e.dma_start(out=o_yT, in_=yT[:]),
               waits=[st_last])
        em.emit(nc, block, sems)
    return nc


def kernel(L0_attn_x, L0_attn_Wq, L0_attn_bq, L0_attn_Wk, L0_attn_bk,
           L0_attn_Wv, L0_attn_bv, L0_attn_Wo, L0_attn_bo,
           L0_ffn_w_gate, L0_ffn_w_up, L0_ffn_w_down,
           L0_ffn_s_gate, L0_ffn_s_up, L0_ffn_s_down):
    f32 = np.float32
    x = np.asarray(L0_attn_x, dtype=f32)
    Bsz, T, _ = x.shape

    wvhd = np.ascontiguousarray(
        np.asarray(L0_attn_Wv, f32).transpose(1, 0, 2).reshape(D, D))
    bqT = np.ascontiguousarray(
        np.broadcast_to(np.asarray(L0_attn_bq, f32).T, (128, H)))
    bkT = np.ascontiguousarray(
        np.broadcast_to(np.asarray(L0_attn_bk, f32).T, (128, H)))
    bvr = np.ascontiguousarray(np.asarray(L0_attn_bv, f32).reshape(1, D))
    bor = np.ascontiguousarray(np.asarray(L0_attn_bo, f32).reshape(1, D))
    sgpm = np.ascontiguousarray(
        np.asarray(L0_ffn_s_gate, f32).reshape(NII, 128).T)
    supm = np.ascontiguousarray(
        np.asarray(L0_ffn_s_up, f32).reshape(NII, 128).T)
    sdr = np.ascontiguousarray(np.asarray(L0_ffn_s_down, f32).reshape(1, D))
    dm = np.zeros((128, 4, CH), dtype=ml_dtypes.bfloat16)
    srow = np.arange(128)[:, None]
    tcol = np.arange(CH)[None, :]
    for k in range(4):
        dm[:, k, :] = ((k * 128 + srow) <= tcol).astype(ml_dtypes.bfloat16)
    dm = np.ascontiguousarray(dm.reshape(128, 4 * CH))

    shared = dict(
        wq=np.ascontiguousarray(np.asarray(L0_attn_Wq, f32)),
        wk=np.ascontiguousarray(np.asarray(L0_attn_Wk, f32)),
        wvhd=wvhd,
        wo=np.ascontiguousarray(np.asarray(L0_attn_Wo, f32)),
        bqT=bqT, bkT=bkT, bvr=bvr, bor=bor,
        wg=np.ascontiguousarray(np.asarray(L0_ffn_w_gate, np.int8)),
        wu=np.ascontiguousarray(np.asarray(L0_ffn_w_up, np.int8)),
        wd=np.ascontiguousarray(np.asarray(L0_ffn_w_down, np.int8)),
        sgpm=sgpm, supm=supm, sdr=sdr, dmask=dm,
    )

    in_maps = []
    metas = []
    for i in range(8):
        a, b = i, 7 - i
        rows = [x[0, a * CH:(a + 1) * CH], x[1, b * CH:(b + 1) * CH]]
        rows += [x[0, j * CH:(j + 1) * CH] for j in range(a)]
        rows += [x[1, j * CH:(j + 1) * CH] for j in range(b)]
        xkv = np.concatenate(rows, axis=0)
        big = np.full((2 * NPOS,), -1e9, dtype=f32)
        big[0] = 0.0
        big[2:2 + a] = 0.0
        big[NPOS + 1] = 0.0
        big[NPOS + 2 + a: NPOS + 2 + a + b] = 0.0
        m = dict(shared)
        m["xkvT"] = np.ascontiguousarray(xkv.T)
        m["xown"] = np.ascontiguousarray(xkv[:TOWN])
        m["bigoff"] = np.ascontiguousarray(
            np.broadcast_to(big[None, :], (128, 2 * NPOS)))
        in_maps.append(m)
        metas.append((a, b))

    nc = build_nc()
    res = run_bass_kernel_spmd(nc, in_maps, core_ids=list(range(8)))

    out = np.empty((2, T, D), dtype=f32)
    for i, (a, b) in enumerate(metas):
        oo = np.asarray(res.results[i]["out"], dtype=f32)
        out[0, a * CH:(a + 1) * CH] = oo[:CH]
        out[1, b * CH:(b + 1) * CH] = oo[CH:]
    return out


# version 12
# speedup vs baseline: 1.3543x; 1.0028x over previous
import numpy as np
import ml_dtypes
import concourse.bass as bass
import concourse.mybir as mybir
from concourse.bass_utils import run_bass_kernel_spmd

F32 = mybir.dt.float32
BF16 = mybir.dt.bfloat16
I8 = mybir.dt.int8
AF = mybir.ActivationFunctionType
ALU = mybir.AluOpType
AX = mybir.AxisListType

D = 2048
H = 16
HD = 128
INTER = 8192
CH = 512
NPOS = 9
TKV = NPOS * CH
TOWN = 1024
NCC = D // 128
NII = INTER // 128
SCALE = 1.0 / float(np.sqrt(HD))
EPS = 1e-5

DEBUG = False
DEBUG_ROWS = False

ENGS = ("sync", "gpsimd", "tensor", "vector", "scalar")
DMA_ENGS = ("sync", "gpsimd")


class Em:

    def __init__(self):
        self.ops = []
        self.cur_sid = None
        self.cur_arm = None
        self.nswitch = 0
        self.switch_n = {}

    def op(self, eng, fn, waits=()):
        tok = len(self.ops)
        ws = []
        for w in waits:
            if w is None:
                continue
            if isinstance(w, tuple):
                ws.append(w)
            else:
                assert 0 <= w < tok, f"wait on future token {w} at {tok}"
                ws.append(w)
        self.ops.append([eng, fn, ws, False, self.cur_sid, self.cur_arm])
        return tok

    def switch_begin(self, n):
        sid = self.nswitch
        self.nswitch += 1
        self.switch_n[sid] = n
        self.cur_sid = sid
        return sid

    def arm(self, k):
        self.cur_arm = k

    def switch_end(self):
        self.cur_sid = None
        self.cur_arm = None

    @staticmethod
    def join(sid, eng):
        return ("join", sid, eng)

    def emit(self, nc, block, sems):
        ops = self.ops
        for o in ops:
            if o[0] in DMA_ENGS or o[4] is not None:
                o[3] = True
        for i, (eng, fn, ws, _, sid, arm) in enumerate(ops):
            for w in ws:
                if isinstance(w, tuple):
                    continue
                weng = ops[w][0]
                wsid, warm = ops[w][4], ops[w][5]
                if wsid is not None:
                    assert wsid == sid and warm == arm, \
                        f"cross-arm wait {w}->{i}; use join()"
                if weng == eng and weng == "tensor":
                    continue
                ops[w][3] = True

        segs = []
        for i, o in enumerate(ops):
            sid, arm = o[4], o[5]
            if sid is None:
                if not segs or segs[-1][0] != 'c':
                    segs.append(('c', []))
                segs[-1][1].append(i)
            else:
                if not segs or segs[-1][0] != 's' or segs[-1][1] != sid:
                    segs.append(('s', sid, {}))
                segs[-1][2].setdefault(arm, []).append(i)

        val_after = [0] * len(ops)
        joins = {}
        cnt = {e: 0 for e in ENGS}
        for seg in segs:
            if seg[0] == 'c':
                for i in seg[1]:
                    eng = ops[i][0]
                    if ops[i][3]:
                        cnt[eng] += 16 if eng in DMA_ENGS else 1
                    val_after[i] = cnt[eng]
            else:
                sid, arms = seg[1], seg[2]
                finals = None
                for k in sorted(arms):
                    cnt_a = dict(cnt)
                    for i in arms[k]:
                        eng = ops[i][0]
                        if ops[i][3]:
                            cnt_a[eng] += 16 if eng in DMA_ENGS else 1
                        val_after[i] = cnt_a[eng]
                    if finals is None:
                        finals = cnt_a
                    else:
                        assert finals == cnt_a, \
                            f"switch {sid}: arm {k} counts {cnt_a} != {finals}"
                cnt = finals
                joins[sid] = dict(finals)
        finals_all = dict(cnt)

        def resolve(i):
            eng = ops[i][0]
            rw = {}
            for w in ops[i][2]:
                if isinstance(w, tuple):
                    weng, v = w[2], joins[w[1]][w[2]]
                else:
                    weng = ops[w][0]
                    if weng == eng and weng == "tensor":
                        continue
                    v = val_after[w]
                if v > 0:
                    rw[weng] = max(rw.get(weng, 0), v)
            return rw

        def mk(engname):
            def f(engine):
                hi = {e: 0 for e in ENGS}
                pid = [None]

                def emit_op(i):
                    for weng, v in resolve(i).items():
                        if v > hi[weng]:
                            engine.wait_ge(sems[weng], v)
                            hi[weng] = v
                    inst = ops[i][1](engine)
                    if ops[i][3]:
                        inst.then_inc(sems[engname],
                                      16 if engname in DMA_ENGS else 1)

                for seg in segs:
                    if seg[0] == 'c':
                        for i in seg[1]:
                            if ops[i][0] == engname:
                                emit_op(i)
                    else:
                        sid, arms = seg[1], seg[2]
                        mine = {k: [i for i in v if ops[i][0] == engname]
                                for k, v in arms.items()}
                        if not any(mine.values()):
                            continue
                        if pid[0] is None:
                            pid[0] = engine.partition_id()
                        hi0 = dict(hi)
                        ends = []
                        for k in engine.Switch(pid[0], self.switch_n[sid]):
                            hi = dict(hi0)
                            for i in mine.get(k, []):
                                emit_op(i)
                            ends.append(hi)
                        hi = {e: min(d[e] for d in ends) for e in ENGS}
                for e in ENGS:
                    if finals_all[e] > hi[e]:
                        engine.wait_ge(sems[e], finals_all[e])
            return f
        block.sync(mk("sync"))
        block.gpsimd(mk("gpsimd"))
        block.tensor(mk("tensor"))
        block.vector(mk("vector"))
        block.scalar(mk("scalar"))


def build_nc():
    nc = bass.Bass("TRN2", target_bir_lowering=False)

    i_xkvT = nc.dram_tensor("xkvT", [D, TKV], F32, kind="ExternalInput").ap()
    i_xown = nc.dram_tensor("xown", [TOWN, D], F32, kind="ExternalInput").ap()
    i_wq = nc.dram_tensor("wq", [H, D, HD], F32, kind="ExternalInput").ap()
    i_wk = nc.dram_tensor("wk", [H, D, HD], F32, kind="ExternalInput").ap()
    i_wvhd = nc.dram_tensor("wvhd", [D, D], F32, kind="ExternalInput").ap()
    i_wo = nc.dram_tensor("wo", [D, D], F32, kind="ExternalInput").ap()
    i_bqT = nc.dram_tensor("bqT", [128, H], F32, kind="ExternalInput").ap()
    i_bkT = nc.dram_tensor("bkT", [128, H], F32, kind="ExternalInput").ap()
    i_bvr = nc.dram_tensor("bvr", [1, D], F32, kind="ExternalInput").ap()
    i_bor = nc.dram_tensor("bor", [1, D], F32, kind="ExternalInput").ap()
    i_wg = nc.dram_tensor("wg", [D, INTER], I8, kind="ExternalInput").ap()
    i_wu = nc.dram_tensor("wu", [D, INTER], I8, kind="ExternalInput").ap()
    i_wd = nc.dram_tensor("wd", [INTER, D], I8, kind="ExternalInput").ap()
    i_sg = nc.dram_tensor("sgpm", [128, NII], F32, kind="ExternalInput").ap()
    i_su = nc.dram_tensor("supm", [128, NII], F32, kind="ExternalInput").ap()
    i_sdr = nc.dram_tensor("sdr", [1, D], F32, kind="ExternalInput").ap()
    i_bigoff = nc.dram_tensor("bigoff", [128, 2 * NPOS], F32,
                              kind="ExternalInput").ap()
    i_dmask = nc.dram_tensor("dmask", [128, 4 * CH], BF16,
                             kind="ExternalInput").ap()

    o_out = nc.dram_tensor("out", [TOWN, D], F32, kind="ExternalOutput").ap()

    ik = "ExternalOutput" if DEBUG else "Internal"
    d_xnT = nc.dram_tensor("xnT_d", [D, TKV], BF16, kind=ik).ap()
    d_k = nc.dram_tensor("k_d", [D, TKV], BF16, kind=ik).ap()
    d_v = nc.dram_tensor("v_d", [TKV, D], BF16, kind=ik).ap()
    d_g = nc.dram_tensor("g_d", [TOWN, D], BF16, kind=ik).ap()
    if DEBUG:
        o_QT = nc.dram_tensor("dbg_QT", [128, H * TOWN], BF16,
                              kind="ExternalOutput").ap()
        o_yT = nc.dram_tensor("dbg_yT", [128, H * TOWN], BF16,
                              kind="ExternalOutput").ap()
        o_l = nc.dram_tensor("dbg_l", [2 * H, 512], F32,
                             kind="ExternalOutput").ap()
        o_c = nc.dram_tensor("dbg_c", [2 * H, 512], F32,
                             kind="ExternalOutput").ap()
        o_rl = nc.dram_tensor("dbg_rl", [2 * H, 512], F32,
                              kind="ExternalOutput").ap()

    BASE = 16512
    TOPLIM = 229344
    P0 = BASE
    M0 = BASE + 16384
    Z1 = BASE + 45056
    Z2 = BASE + 110592
    Z3 = BASE + 143360
    Z4 = BASE + 176128
    assert Z4 + 32768 <= TOPLIM

    uid = [0]

    def sb(name, shape, dtype, off):
        uid[0] += 1
        return nc.alloc_sbuf_tensor_at(f"{name}_{uid[0]}", shape, dtype,
                                       offset=off)

    bigoff = sb("bigoff", [128, 2 * NPOS], F32, P0 + 0)
    bkT = sb("bkT", [128, H], F32, P0 + 128)
    bqT = sb("bqT", [128, H], F32, P0 + 192)
    sgpm = sb("sgpm", [128, NII], F32, P0 + 256)
    supm = sb("supm", [128, NII], F32, P0 + 512)
    dmask = sb("dmask", [128, 4 * CH], BF16, P0 + 768)
    bvb = sb("bvb", [128, D], F32, P0 + 4864)
    onesb = sb("onesb", [128, 1], BF16, P0 + 13056)
    ones1b = sb("ones1b", [1, 128], BF16, P0 + 13120)
    ones1f = sb("ones1f", [1, 128], F32, P0 + 13376)
    maxk2 = sb("maxk2", [1, H], F32, P0 + 13888)
    epsc = sb("epsc", [128, 1], F32, P0 + 13952)

    em = Em()
    op = em.op

    from contextlib import ExitStack
    with ExitStack() as ctx:
        ps = [ctx.enter_context(nc.psum_tensor(f"ps{k}", [128, 512], F32))
              for k in range(8)]
        sems = {e: ctx.enter_context(nc.semaphore(f"S_{e}")) for e in ENGS}
        block = ctx.enter_context(nc.Block())

        ps_last = {k: None for k in range(8)}

        t_bigoff = op("sync", lambda e: e.dma_start(out=bigoff[:], in_=i_bigoff))
        t_bkT = op("sync", lambda e: e.dma_start(out=bkT[:], in_=i_bkT))
        t_bqT = op("sync", lambda e: e.dma_start(out=bqT[:], in_=i_bqT))
        t_sg = op("sync", lambda e: e.dma_start(out=sgpm[:], in_=i_sg))
        t_su = op("sync", lambda e: e.dma_start(out=supm[:], in_=i_su))
        t_dmask = op("sync", lambda e: e.dma_start(out=dmask[:], in_=i_dmask))
        t_ones_b = op("vector", lambda e: e.memset(onesb[:], 1.0))
        t_ones_1b = op("vector", lambda e: e.memset(ones1b[:], 1.0))
        t_ones_1f = op("vector", lambda e: e.memset(ones1f[:], 1.0))
        op("vector", lambda e: e.memset(maxk2[:], 0.0))
        t_eps = op("vector", lambda e: e.memset(epsc[:], float(EPS)))

        bvrow = sb("bvrow", [1, D], F32, M0)
        tv = op("sync", lambda e: e.dma_start(out=bvrow[:], in_=i_bvr))
        tok_bvb = None
        bv_mm_last = None
        for ec in range(4):
            mmb = op("tensor", lambda e, ec=ec: e.matmul(
                ps[7][:], ones1f[:], bvrow[0:1, ec * 512:(ec + 1) * 512],
                start=True, stop=True), waits=[tv, t_ones_1f, ps_last[7]])
            bv_mm_last = mmb
            tok_bvb = op("vector", lambda e, ec=ec: e.tensor_copy(
                bvb[:, ec * 512:(ec + 1) * 512], ps[7][:]), waits=[mmb])
            ps_last[7] = tok_bvb

        tmp1 = sb("tmp1", [128, 1024], BF16, M0 + 0)
        sqcc = sb("sqcc", [128, 1024], BF16, M0 + 2048)
        kbuf = sb("kbuf", [128, 1024], BF16, M0 + 4096)
        sqb = sb("sqb", [128, 1024], BF16, M0 + 6144)
        r_m = sb("r_m", [1, 512], F32, M0 + 8192)[:]
        r_msq = sb("r_msq", [1, 512], F32, M0 + 10240)[:]
        r_m2 = sb("r_m2", [1, 512], F32, M0 + 12288)[:]
        r_var = sb("r_var", [1, 512], F32, M0 + 14336)[:]
        r_std = sb("r_std", [1, 512], F32, M0 + 16384)[:]
        r_rstd = sb("r_rstd", [1, 512], F32, M0 + 18432)[:]
        r_mr = sb("r_mr", [1, 512], F32, M0 + 20480)[:]
        r_nrm = sb("r_nrm", [1, 512], F32, M0 + 22528)
        rstdb = sb("rstdb", [128, 512], BF16, M0 + 24576)
        mrb = sb("mrb", [128, 512], BF16, M0 + 25600)

        xT = [sb("xT0", [128, NCC * 512], BF16, Z1),
              sb("xT1", [128, NCC * 512], BF16, Z1 + 16384)]
        wk_sb = sb("wkA", [128, 8 * NCC * 128], BF16, Z1 + 32768)
        wk_sbB = sb("wkB", [128, 8 * NCC * 128], BF16, Z4)
        xnc = [sb("xnc0", [128, NCC * 512], BF16, Z2),
               sb("xnc1", [128, NCC * 512], BF16, Z2 + 16384)]
        xnT_own = sb("xnT_own", [128, NCC * TOWN], BF16, Z3)

        def wk_ap(h, cc):
            if h < 8:
                return wk_sb[:, (h * NCC + cc) * 128:(h * NCC + cc + 1) * 128]
            return wk_sbB[:, ((h - 8) * NCC + cc) * 128:
                          ((h - 8) * NCC + cc + 1) * 128]

        ld_wk = None
        for hq in range(4):
            dst = wk_sb if hq < 2 else wk_sbB
            ld_wk = op("gpsimd", lambda e, hq=hq, dst=dst: e.dma_start(
                out=dst[:].rearrange("p (h cc d) -> p h cc d",
                                     h=8, cc=NCC)[:, (hq % 2) * 4:
                                                  (hq % 2) * 4 + 4],
                in_=i_wk.rearrange("h (cc cin) d -> cin h cc d", cin=128)
                [:, hq * 4:(hq + 1) * 4]))

        xT_cons = [None, None]
        xnc_w = [[], []]
        sq_cons = [None, None]
        t1_cons = [None, None]
        st_xnT = []
        own_copy_tok = []
        xnc2_cons = [None, None]
        kb_cons = [[bv_mm_last], [bv_mm_last]]
        st_k_last = None
        nrm_tok = None
        sqcc_guard = bv_mm_last
        tmp1_guard = bv_mm_last
        chunk_state = {}
        chunk_a1 = [None]

        def ln_chunk(tc):
            p = tc % 2
            ld = op("gpsimd", lambda e, p=p, tc=tc: e.dma_start(
                out=xT[p][:].rearrange("p (cc t) -> p cc t", cc=NCC),
                in_=i_xkvT[:, tc * 512:(tc + 1) * 512]
                .rearrange("(cc cin) t -> cin cc t", cin=128)),
                waits=[xT_cons[p]])
            mmS = mmQ = None
            nonlocal sqcc_guard, tmp1_guard
            for cc in range(NCC):
                q = cc % 2
                sq = op("vector", lambda e, p=p, cc=cc, q=q: e.tensor_mul(
                    sqcc[:, q * 512:(q + 1) * 512],
                    xT[p][:, cc * 512:(cc + 1) * 512],
                    xT[p][:, cc * 512:(cc + 1) * 512]),
                    waits=[ld, sq_cons[q], sqcc_guard])
                sqcc_guard = None
                mmS = op("tensor", lambda e, p=p, cc=cc: e.matmul(
                    ps[6][0:1, :], onesb[:],
                    xT[p][:, cc * 512:(cc + 1) * 512],
                    start=(cc == 0), stop=(cc == NCC - 1)),
                    waits=[ld, t_ones_b] + ([ps_last[6]] if cc == 0 else []))
                mmQ = op("tensor", lambda e, q=q, cc=cc: e.matmul(
                    ps[5][0:1, :], onesb[:], sqcc[:, q * 512:(q + 1) * 512],
                    start=(cc == 0), stop=(cc == NCC - 1)),
                    waits=[sq] + ([ps_last[5]] if cc == 0 else []))
                sq_cons[q] = mmQ
            v1 = op("vector", lambda e: e.tensor_scalar_mul(
                r_m, ps[6][0:1, :], 1.0 / D), waits=[mmS])
            v2 = op("vector", lambda e: e.tensor_scalar_mul(
                r_msq, ps[5][0:1, :], 1.0 / D), waits=[mmQ])
            ps_last[6] = v1
            ps_last[5] = v2
            v3 = op("vector", lambda e: e.tensor_mul(
                r_m2, r_m, r_m), waits=[v1])
            v4 = op("vector", lambda e: e.tensor_sub(
                r_var, r_msq, r_m2), waits=[v2, v3])
            a1 = op("scalar", lambda e: e.activation(
                r_std, r_var, AF.Sqrt, bias=epsc[0:1, 0:1], scale=1.0),
                waits=[v4, t_eps])
            chunk_a1[0] = a1
            v5 = op("vector", lambda e: e.reciprocal(r_rstd, r_std),
                    waits=[a1])
            v6 = op("vector", lambda e: e.tensor_mul(
                r_mr, r_m, r_rstd), waits=[v5, v1])
            mB1 = op("tensor", lambda e: e.matmul(
                ps[4][:], ones1f[:], r_rstd, start=True, stop=True),
                waits=[v5, t_ones_1f, ps_last[4]])
            vB1 = op("vector", lambda e: e.tensor_copy(rstdb[:], ps[4][:]),
                     waits=[mB1])
            ps_last[4] = vB1
            mB2 = op("tensor", lambda e: e.matmul(
                ps[3][:], ones1f[:], r_mr, start=True, stop=True),
                waits=[v6, ps_last[3]])
            vB2 = op("vector", lambda e: e.tensor_copy(mrb[:], ps[3][:]),
                     waits=[mB2])
            ps_last[3] = vB2
            last_apply = None
            for cc in range(NCC):
                q = cc % 2
                va = op("vector", lambda e, p=p, cc=cc, q=q: e.tensor_mul(
                    tmp1[:, q * 512:(q + 1) * 512],
                    xT[p][:, cc * 512:(cc + 1) * 512], rstdb[:]),
                    waits=[vB1, t1_cons[q], tmp1_guard] +
                    (xnc_w[p] if cc == 0 else []))
                tmp1_guard = None
                vb = op("vector", lambda e, p=p, cc=cc, q=q: e.tensor_sub(
                    xnc[p][:, cc * 512:(cc + 1) * 512],
                    tmp1[:, q * 512:(q + 1) * 512], mrb[:]),
                    waits=[vB2, va])
                t1_cons[q] = vb
                last_apply = vb
            xT_cons[p] = last_apply
            deps = [last_apply]
            if tc < 2:
                oc = op("vector", lambda e, p=p, tc=tc: e.tensor_copy(
                    xnT_own[:].rearrange("p (cc t) -> p cc t", cc=NCC)
                    [:, :, tc * 512:(tc + 1) * 512],
                    xnc[p][:].rearrange("p (cc t) -> p cc t", cc=NCC)),
                    waits=[last_apply])
                own_copy_tok.append(oc)
                deps.append(oc)
            st = op("sync", lambda e, p=p, tc=tc: e.dma_start(
                out=d_xnT.rearrange("(cc cin) t -> cin cc t", cin=128)
                [:, :, tc * 512:(tc + 1) * 512],
                in_=xnc[p][:].rearrange("p (cc t) -> p cc t", cc=NCC)),
                waits=deps)
            st_xnT.append(st)
            chunk_state[tc] = (p, last_apply, st)

        def kproj_chunk(tc):
            nonlocal st_k_last, nrm_tok
            p, last_apply, st = chunk_state[tc]
            ev_toks = [None, None]

            def norm_block(h):
                nonlocal nrm_tok
                kp = h % 2
                sqo = op("vector", lambda e, kp=kp: e.tensor_mul(
                    sqb[:, kp * 512:(kp + 1) * 512],
                    kbuf[:, kp * 512:(kp + 1) * 512],
                    kbuf[:, kp * 512:(kp + 1) * 512]),
                    waits=[ev_toks[kp], nrm_tok])
                mmn = op("tensor", lambda e, kp=kp: e.matmul(
                    ps[6][0:1, :], onesb[:],
                    sqb[:, kp * 512:(kp + 1) * 512], start=True, stop=True),
                    waits=[sqo, ps_last[6]])
                prev_nrm = nrm_tok
                mx1 = op("vector", lambda e: e.reduce_max(
                    r_nrm[0:1, 0:1], ps[6][0:1, :], AX.X), waits=[mmn])
                ps_last[6] = mx1
                nrm_tok = op("vector", lambda e, h=h: e.tensor_max(
                    maxk2[0:1, h:h + 1], maxk2[0:1, h:h + 1],
                    r_nrm[0:1, 0:1]), waits=[mx1, prev_nrm])
                kb_cons[h % 2].append(mmn)

            last_mm_chunk = None
            for h in range(H):
                kp = h % 2
                bank = kp
                mm = None
                for cc in range(NCC):
                    mm = op("tensor",
                            lambda e, p=p, h=h, cc=cc, bank=bank: e.matmul(
                                ps[bank][:], wk_ap(h, cc),
                                xnc[p][:, cc * 512:(cc + 1) * 512],
                                start=(cc == 0), stop=(cc == NCC - 1)),
                            waits=[last_apply, ld_wk, ps_last[bank]]
                            if cc == 0 else [])
                last_mm_chunk = mm
                ev = op("vector",
                        lambda e, h=h, kp=kp, bank=bank: e.tensor_single_scalar(
                            kbuf[:, kp * 512:(kp + 1) * 512], ps[bank][:],
                            bkT[:, h:h + 1], ALU.add),
                        waits=[mm, t_bkT] + kb_cons[kp])
                ev_toks[kp] = ev
                ps_last[bank] = ev
                stk = op("sync", lambda e, h=h, tc=tc, kp=kp: e.dma_start(
                    out=d_k[h * 128:(h + 1) * 128, tc * 512:(tc + 1) * 512],
                    in_=kbuf[:, kp * 512:(kp + 1) * 512]),
                    waits=[ev])
                st_k_last = stk
                kb_cons[kp] = [stk]
                if h > 0:
                    norm_block(h - 1)
            norm_block(H - 1)
            xnc2_cons[p] = last_mm_chunk
            xnc_w[p] = [st, last_mm_chunk]

        for tc in range(NPOS + 1):
            if tc < NPOS:
                ln_chunk(tc)
            if tc >= 1:
                kproj_chunk(tc - 1)
        nrm_final = nrm_tok
        mmn_final = kb_cons[(H - 1) % 2][-1]

        vstage = sb("vstage", [128, 1024], BF16, M0 + 26624)
        wv_sb = sb("wv", [128, NCC * D], BF16, Z1)
        ld_wv = op("gpsimd", lambda e: e.dma_start(
            out=wv_sb[:].rearrange("p (cc e) -> p cc e", cc=NCC),
            in_=i_wvhd.rearrange("(cc cin) e -> cin cc e", cin=128)),
            waits=[xnc2_cons[0], xnc2_cons[1], ld_wk,
                   xT_cons[0], xT_cons[1]])

        vs_cons = [[], []]
        st_v_last = None
        xncv_cons = [None, None]
        for tc in range(NPOS):
            p = tc % 2
            ld = op("sync", lambda e, p=p, tc=tc: e.dma_start(
                out=xnc[p][:].rearrange("p (cc t) -> p cc t", cc=NCC),
                in_=d_xnT.rearrange("(cc cin) t -> cin cc t", cin=128)
                [:, :, tc * 512:(tc + 1) * 512]),
                waits=[st_xnT[tc], st_k_last, xncv_cons[p],
                       xnc2_cons[p] if tc < 2 else None])
            last_mm_chunk = None
            for hg in range(4):
                for tt in range(4):
                    vp = (hg * 4 + tt) % 2
                    bank = vp
                    mm = None
                    for cc in range(NCC):
                        mm = op("tensor",
                                lambda e, p=p, hg=hg, tt=tt, cc=cc, bank=bank:
                                e.matmul(
                                    ps[bank][:],
                                    xnc[p][:, cc * 512 + tt * 128:
                                           cc * 512 + (tt + 1) * 128],
                                    wv_sb[:, cc * D + hg * 512:
                                          cc * D + (hg + 1) * 512],
                                    start=(cc == 0), stop=(cc == NCC - 1)),
                                waits=[ld, ld_wv, ps_last[bank]]
                                if cc == 0 else [])
                    last_mm_chunk = mm
                    ev = op("vector",
                            lambda e, hg=hg, vp=vp, bank=bank: e.tensor_add(
                                vstage[:, vp * 512:(vp + 1) * 512],
                                ps[bank][:],
                                bvb[:, hg * 512:(hg + 1) * 512]),
                            waits=[mm, tok_bvb] + vs_cons[vp])
                    ps_last[bank] = ev
                    stv = op("sync",
                             lambda e, tc=tc, tt=tt, hg=hg, vp=vp: e.dma_start(
                                 out=d_v[tc * 512 + tt * 128:
                                         tc * 512 + (tt + 1) * 128,
                                         hg * 512:(hg + 1) * 512],
                                 in_=vstage[:, vp * 512:(vp + 1) * 512]),
                             waits=[ev])
                    st_v_last = stv
                    vs_cons[vp] = [stv]
            xncv_cons[p] = last_mm_chunk

        wq_sb = sb("wq", [128, H * NCC * 128], BF16, Z1)
        ld_wq = None
        for hq in range(4):
            ld_wq = op("gpsimd", lambda e, hq=hq: e.dma_start(
                out=wq_sb[:].rearrange("p (h cc d) -> p h cc d",
                                       h=H, cc=NCC)[:, hq * 4:(hq + 1) * 4],
                in_=i_wq.rearrange("h (cc cin) d -> cin h cc d", cin=128)
                [:, hq * 4:(hq + 1) * 4]),
                waits=[xncv_cons[0], xncv_cons[1], ld_wv] if hq == 0 else [])
        QT = sb("QT", [128, H * TOWN], BF16, Z4)
        q_ev_last = None
        q_mm_last = None
        for h in range(H):
            for sl in range(2):
                qp = (h * 2 + sl) % 2
                bank = qp
                mm = None
                for cc in range(NCC):
                    mm = op("tensor",
                            lambda e, h=h, sl=sl, cc=cc, bank=bank: e.matmul(
                                ps[bank][:],
                                wq_sb[:, (h * NCC + cc) * 128:
                                      (h * NCC + cc + 1) * 128],
                                xnT_own[:, cc * TOWN + sl * 512:
                                        cc * TOWN + (sl + 1) * 512],
                                start=(cc == 0), stop=(cc == NCC - 1)),
                            waits=([ld_wq, ps_last[bank]] + own_copy_tok)
                            if cc == 0 else [])
                q_mm_last = mm
                q_ev_last = op("vector",
                               lambda e, h=h, sl=sl, bank=bank: e.tensor_scalar(
                                   QT[:, h * TOWN + sl * 512:
                                      h * TOWN + (sl + 1) * 512],
                                   ps[bank][:], bqT[:, h:h + 1], float(SCALE),
                                   ALU.add, ALU.mult),
                               waits=[mm, t_bqT])
                ps_last[bank] = q_ev_last

        sd_row = sb("sd_row", [1, D], F32, M0 + 0)
        sdownb = sb("sdownb", [128, D], F32, M0 + 8192)
        gsil = sb("gsil", [128, 1024], BF16, M0 + 16384)
        tmp3 = sb("tmp3", [128, 1024], F32, M0 + 18432)

        tsd = op("sync", lambda e: e.dma_start(out=sd_row[:], in_=i_sdr),
                 waits=[t1_cons[0], t1_cons[1], st_k_last, nrm_final,
                        mmn_final])
        sd_toks = []
        for ec in range(4):
            mmb = op("tensor", lambda e, ec=ec: e.matmul(
                ps[7][:], ones1f[:], sd_row[0:1, ec * 512:(ec + 1) * 512],
                start=True, stop=True), waits=[tsd, ps_last[7]])
            tok = op("vector", lambda e, ec=ec: e.tensor_copy(
                sdownb[:, ec * 512:(ec + 1) * 512], ps[7][:]),
                waits=[mmb, nrm_final, chunk_a1[0]])
            ps_last[7] = tok
            sd_toks.append(tok)

        fuseT = sb("fuseT", [128, NII * 512], BF16, Z1)
        slab = [sb("slab0", [128, NCC * 512], BF16, Z2),
                sb("slab1", [128, NCC * 512], BF16, Z2 + 16384)]
        fuse_toks = [None] * NII
        st_g_last = None
        ds0 = ds1 = None
        gs_cons = [[], []]
        t3_cons = [None, None]
        for fc in range(2):
            gate_prev, up_prev = ds0, ds1
            for grp in range(NII // 4):
                extra_g = [xncv_cons[0], xncv_cons[1]] if (fc == 0 and grp == 0) else []
                ld_g = op("gpsimd", lambda e, grp=grp: e.dma_start(
                    out=slab[0][:].rearrange("p (cc i) -> p cc i", cc=NCC),
                    in_=i_wg.rearrange("(cc cin) i -> cin cc i", cin=128)
                    [:, :, grp * 512:(grp + 1) * 512]),
                    waits=[gate_prev] + extra_g)
                ld_u = op("gpsimd", lambda e, grp=grp: e.dma_start(
                    out=slab[1][:].rearrange("p (cc i) -> p cc i", cc=NCC),
                    in_=i_wu.rearrange("(cc cin) i -> cin cc i", cin=128)
                    [:, :, grp * 512:(grp + 1) * 512]),
                    waits=[up_prev] + extra_g)
                for isub in range(4):
                    ii = grp * 4 + isub
                    gp = ii % 2
                    bank = gp
                    bank2 = 2 + gp
                    mm = None
                    for cc in range(NCC):
                        mm = op("tensor",
                                lambda e, isub=isub, cc=cc, fc=fc, bank=bank:
                                e.matmul(
                                    ps[bank][:],
                                    slab[0][:, cc * 512 + isub * 128:
                                            cc * 512 + (isub + 1) * 128],
                                    xnT_own[:, cc * TOWN + fc * 512:
                                            cc * TOWN + (fc + 1) * 512],
                                    start=(cc == 0), stop=(cc == NCC - 1)),
                                waits=[ld_g, ps_last[bank], q_mm_last]
                                if cc == 0 else [])
                    gate_prev = mm
                    sil = op("scalar",
                             lambda e, ii=ii, gp=gp, bank=bank: e.activation(
                                 gsil[:, gp * 512:(gp + 1) * 512], ps[bank][:],
                                 AF.Silu, scale=sgpm[:, ii:ii + 1]),
                             waits=[mm, t_sg] + gs_cons[gp])
                    ps_last[bank] = sil
                    mmu = None
                    for cc in range(NCC):
                        mmu = op("tensor",
                                 lambda e, isub=isub, cc=cc, fc=fc, bank2=bank2:
                                 e.matmul(
                                     ps[bank2][:],
                                     slab[1][:, cc * 512 + isub * 128:
                                             cc * 512 + (isub + 1) * 128],
                                     xnT_own[:, cc * TOWN + fc * 512:
                                             cc * TOWN + (fc + 1) * 512],
                                     start=(cc == 0), stop=(cc == NCC - 1)),
                                 waits=[ld_u, ps_last[bank2]]
                                 if cc == 0 else [])
                    up_prev = mmu
                    fu = op("vector",
                            lambda e, ii=ii, gp=gp, bank2=bank2:
                            e.scalar_tensor_tensor(
                                fuseT[:, ii * 512:(ii + 1) * 512],
                                ps[bank2][:], supm[:, ii:ii + 1],
                                gsil[:, gp * 512:(gp + 1) * 512],
                                ALU.mult, ALU.mult),
                            waits=[mmu, sil, t_su, fuse_toks[ii]])
                    ps_last[bank2] = fu
                    gs_cons[gp] = [fu]
                    fuse_toks[ii] = fu
            ds = [gate_prev, up_prev]
            for ec in range(4):
                mm_last = None
                for half in range(4):
                    dp = half % 2
                    ldd = op("gpsimd",
                             lambda e, half=half, ec=ec, dp=dp: e.dma_start(
                                 out=slab[dp][:].rearrange(
                                     "p (k i) -> p k i", k=NCC),
                                 in_=i_wd.rearrange(
                                     "(ii iin) e -> iin ii e", iin=128)
                                 [:, half * 16:(half + 1) * 16,
                                  ec * 512:(ec + 1) * 512]),
                             waits=[ds[dp]])
                    for k in range(16):
                        ii = half * 16 + k
                        for tt in range(4):
                            bank = 4 + tt
                            w = []
                            if k == 0 and tt == 0:
                                w.append(ldd)
                            if tt == 0:
                                w.append(fuse_toks[ii])
                            if ii == 0:
                                w.append(ps_last[bank])
                            mm_last = op(
                                "tensor",
                                lambda e, ii=ii, k=k, tt=tt, dp=dp, bank=bank:
                                e.matmul(
                                    ps[bank][:],
                                    fuseT[:, ii * 512 + tt * 128:
                                          ii * 512 + (tt + 1) * 128],
                                    slab[dp][:, k * 512:(k + 1) * 512],
                                    start=(ii == 0), stop=(ii == NII - 1)),
                                waits=w)
                    ds[dp] = mm_last
                for tt in range(4):
                    bank = 4 + tt
                    tp = tt % 2
                    vm = op("vector",
                            lambda e, tt=tt, ec=ec, tp=tp, bank=bank:
                            e.tensor_mul(
                                tmp3[:, tp * 512:(tp + 1) * 512], ps[bank][:],
                                sdownb[:, ec * 512:(ec + 1) * 512]),
                            waits=[mm_last, t3_cons[tp]])
                    ps_last[bank] = vm
                    si = op("scalar", lambda e, tp=tp: e.activation(
                        gsil[:, tp * 512:(tp + 1) * 512],
                        tmp3[:, tp * 512:(tp + 1) * 512], AF.Silu),
                        waits=[vm] + gs_cons[tp])
                    stg = op("sync",
                             lambda e, fc=fc, tt=tt, ec=ec, tp=tp: e.dma_start(
                                 out=d_g[fc * 512 + tt * 128:
                                         fc * 512 + (tt + 1) * 128,
                                         ec * 512:(ec + 1) * 512],
                                 in_=gsil[:, tp * 512:(tp + 1) * 512]),
                             waits=[si])
                    st_g_last = stg
                    gs_cons[tp] = [stg]
                    t3_cons[tp] = si
            ds0, ds1 = ds[0], ds[1]

        Csb = sb("Csb", [128, 1024], F32, M0 + 0)
        tmp5 = sb("tmp5", [128, 3 * 512], BF16, M0 + 4096)
        pbuf = sb("pbuf", [128, 8 * 512], BF16, M0 + 10240)
        rlb = sb("rlb", [128, 512], F32, M0 + 18432)
        sqb5 = sb("sqb5", [128, 512], BF16, M0 + 20480)
        R1 = sb("R1", [128, 512], F32, M0 + 21504)
        R2 = sb("R2", [128, 512], F32, M0 + 23552)
        r_c = R1[32:33, :]
        r_ln = R1[64:65, :]
        r_c2 = R1[0:1, :]
        r_rl = R2[0:1, :]

        ktb = [sb("kt0", [128, TKV], BF16, Z1),
               sb("kt1", [128, TKV], BF16, Z1 + TKV * 2)]
        vtb = [sb("vt0", [128, TKV], BF16, Z1 + TKV * 4),
               sb("vt1", [128, TKV], BF16, Z1 + TKV * 6)]
        yT = sb("yT", [128, H * TOWN], BF16, Z3)

        wos = [sb("wos0", [128, NCC * 512], BF16, Z2),
               sb("wos1", [128, NCC * 512], BF16, Z2 + 16384)]

        def ld_wo_op(ec, waits):
            return op("gpsimd", lambda e, ec=ec: e.dma_start(
                out=wos[ec % 2][:].rearrange("p (dc e) -> p dc e", dc=NCC),
                in_=i_wo.rearrange("(dc din) e -> din dc e", din=128)
                [:, :, ec * 512:(ec + 1) * 512]), waits=waits)

        wo_tok = [None] * 4
        wo_tok[0] = ld_wo_op(0, [ds0, ds1])
        wo_tok[1] = ld_wo_op(1, [ds0, ds1])

        DELAY = 6
        NPB = 8
        NT5 = 3
        sbank = (0, 1, 4, 5)
        npair = NPOS * 4

        def c_chain(h, sl, lb, w_sqo, w_vc1, w_mmc):
            sqo = op("vector", lambda e, h=h, sl=sl: e.tensor_mul(
                sqb5[:],
                QT[:, h * TOWN + sl * 512: h * TOWN + (sl + 1) * 512],
                QT[:, h * TOWN + sl * 512: h * TOWN + (sl + 1) * 512]),
                waits=[q_ev_last, st_v_last, st_g_last] + w_sqo)
            mmc = op("tensor", lambda e, lb=lb: e.matmul(
                ps[lb][0:1, :], onesb[:], sqb5[:], start=True, stop=True),
                waits=[sqo] + w_mmc)
            vc1 = op("vector", lambda e, h=h, lb=lb: e.tensor_scalar_mul(
                r_c, ps[lb][0:1, :], maxk2[0:1, h:h + 1]),
                waits=[mmc, st_g_last] + w_vc1)
            acl = op("scalar", lambda e: e.activation(
                r_ln, r_c, AF.Ln), waits=[vc1])
            ac = op("scalar", lambda e: e.activation(
                r_c2, r_ln, AF.Exp, scale=0.5), waits=[acl])
            mcb = op("tensor", lambda e, lb=lb: e.matmul(
                ps[lb][:], ones1f[:], r_c2, start=True, stop=True),
                waits=[ac, vc1])
            vcb = op("vector", lambda e, sl=sl, lb=lb: e.tensor_copy(
                Csb[:, sl * 512:(sl + 1) * 512], ps[lb][:]),
                waits=[mcb])
            return {"sqo": sqo, "mmc": mmc, "vcb": vcb, "ac": ac}

        def pairs_of(a):
            A = [0] + list(range(2, 2 + a))
            B = [1] + list(range(2 + a, NPOS))
            lst = [(0, p) for p in A] + [(1, p) for p in B]
            return ([(sl, p, st) for (sl, p) in lst for st in range(4)],
                    4 * len(A))

        chain00 = c_chain(0, 0, 3, [], [], [ps_last[3]])
        first5 = [st_k_last, st_v_last, st_g_last, ds0, ds1]
        kt_cons = [None, None]
        vt_cons = [None, None]
        rlb_init = st_g_last
        prev_sid = None
        y_done = None

        for g in range(H // 2):
            ldx_toks = {}
            for h in (2 * g, 2 * g + 1):
                kp = h % 2
                jprev = ([em.join(prev_sid, "tensor")]
                         if prev_sid is not None else [])
                ldk = op("sync", lambda e, h=h, kp=kp: e.dma_start(
                    out=ktb[kp][:], in_=d_k[h * 128:(h + 1) * 128, :]),
                    waits=jprev + (first5 if g == 0 else []))
                ldvs = []
                for pos in range(NPOS):
                    ldvs.append(op(
                        "sync", lambda e, h=h, kp=kp, pos=pos: e.dma_start(
                            out=vtb[kp][:].rearrange(
                                "p (pos tl d) -> p pos tl d",
                                pos=NPOS, tl=4)[:, pos],
                            in_=d_v[pos * 512:(pos + 1) * 512,
                                    h * 128:(h + 1) * 128]
                            .rearrange("(tl s) d -> s tl d", s=128)),
                        waits=[]))
                ldx_toks[h] = (ldk, ldvs)
            sid = em.switch_begin(8)
            for a in range(8):
                em.arm(a)
                plist, nA = pairs_of(a)
                jt = (em.join(prev_sid, "tensor")
                      if prev_sid is not None else None)
                jv = (em.join(prev_sid, "vector")
                      if prev_sid is not None else None)
                js = (em.join(prev_sid, "scalar")
                      if prev_sid is not None else None)
                sba = {b: jv for b in sbank}
                yl_ps = {2: jv, 3: jv, 6: jv, 7: jv}
                mrb_prev = jt
                rlb_cons = rlb_init if g == 0 else jv
                pba = [jt] * NPB
                t5a = [js] * NT5
                chA = {2 * g: chain00 if g == 0 else None}
                if g > 0:
                    chA[2 * g] = {"vcb": jv, "mmc": jt, "ac": js}
                for h in (2 * g, 2 * g + 1):
                    kp = h % 2
                    ldk, ldvs = ldx_toks[h]
                    chB = c_chain(h, 1, 7, [chA[h]["mmc"]], [chA[h]["ac"]],
                                  [yl_ps[7]])
                    cbA = chA[h]["vcb"]
                    pt_toks = {}
                    mml_last = {0: None, 1: None}
                    mmy_last = {0: None, 1: None}

                    def consume(j):
                        sl, pos, st = plist[j]
                        lbank = 3 if sl == 0 else 7
                        ybank = 2 if sl == 0 else 6
                        first = (j == 0) or (j == nA)
                        last = (j == nA - 1) or (j == npair - 1)
                        pb = j % NPB
                        pt = pt_toks[j]
                        mml_last[sl] = op(
                            "tensor", lambda e, pb=pb, lbank=lbank,
                            first=first, last=last: e.matmul(
                                ps[lbank][0:1, :], onesb[:],
                                pbuf[:, pb * 512:(pb + 1) * 512],
                                start=first, stop=last),
                            waits=[pt] + ([yl_ps[lbank], chB["vcb"]]
                                          if first else []))
                        mmy_last[sl] = op(
                            "tensor", lambda e, pb=pb, kp=kp, ybank=ybank,
                            pos=pos, st=st, first=first, last=last: e.matmul(
                                ps[ybank][:],
                                vtb[kp][:, (pos * 4 + st) * 128:
                                        (pos * 4 + st + 1) * 128],
                                pbuf[:, pb * 512:(pb + 1) * 512],
                                start=first, stop=last),
                            waits=[ldvs[pos]] + ([yl_ps[ybank]]
                                                 if first else []))
                        pba[j % NPB] = mmy_last[sl]

                    for j in range(npair):
                        sl, pos, st = plist[j]
                        sp = j % NT5
                        pb = j % NPB
                        bank = sbank[j % 4]
                        cb = cbA if sl == 0 else chB["vcb"]
                        mms = op("tensor",
                                 lambda e, h=h, sl=sl, pos=pos, st=st, kp=kp,
                                 bank=bank: e.matmul(
                                     ps[bank][:],
                                     ktb[kp][:, pos * 512 + st * 128:
                                             pos * 512 + (st + 1) * 128],
                                     QT[:, h * TOWN + sl * 512:
                                        h * TOWN + (sl + 1) * 512],
                                     start=True, stop=True),
                                 waits=[ldk, sba[bank]])
                        stt = op("vector",
                                 lambda e, sp=sp, sl=sl, bank=bank:
                                 e.scalar_tensor_tensor(
                                     tmp5[:, sp * 512:(sp + 1) * 512],
                                     ps[bank][:], 1.0,
                                     Csb[:, sl * 512:(sl + 1) * 512],
                                     ALU.mult, ALU.subtract),
                                 waits=[mms, cb, t5a[sp]])
                        sba[bank] = stt
                        ex = op("scalar",
                                lambda e, sp=sp, pb=pb: e.activation(
                                    pbuf[:, pb * 512:(pb + 1) * 512],
                                    tmp5[:, sp * 512:(sp + 1) * 512],
                                    AF.Exp),
                                waits=[stt, pba[pb]])
                        t5a[sp] = ex
                        pt = ex
                        if pos == sl:
                            pt = op("vector",
                                    lambda e, pb=pb, st=st: e.tensor_mul(
                                        pbuf[:, pb * 512:(pb + 1) * 512],
                                        pbuf[:, pb * 512:(pb + 1) * 512],
                                        dmask[:, st * 512:(st + 1) * 512]),
                                    waits=[ex, t_dmask])
                        pt_toks[j] = pt
                        if j >= DELAY:
                            consume(j - DELAY)
                    for j in range(npair - DELAY, npair):
                        consume(j)
                    if h + 1 < H:
                        chA[h + 1] = c_chain(h + 1, 0, 0,
                                             [chB["mmc"]], [chB["ac"]],
                                             [sba[0]])
                        sba[0] = chA[h + 1]["vcb"]
                    vrecA = op("vector", lambda e: e.reciprocal(
                        r_rl, ps[3][0:1, :]),
                        waits=[mml_last[0], mrb_prev])
                    mrbA = op("tensor", lambda e: e.matmul(
                        ps[5][:], ones1f[:], r_rl, start=True, stop=True),
                        waits=[vrecA, sba[5]])
                    vrbA = op("vector", lambda e: e.tensor_copy(
                        rlb[:], ps[5][:]), waits=[mrbA, rlb_cons])
                    y_doneA = op("vector",
                                 lambda e, h=h: e.scalar_tensor_tensor(
                                     yT[:, h * TOWN: h * TOWN + 512],
                                     ps[2][:], 1.0, rlb[:],
                                     ALU.mult, ALU.mult),
                                 waits=[mmy_last[0], vrbA])
                    vrecB = op("vector", lambda e: e.reciprocal(
                        r_rl, ps[7][0:1, :]), waits=[mml_last[1], mrbA])
                    mrbB = op("tensor", lambda e: e.matmul(
                        ps[5][:], ones1f[:], r_rl, start=True, stop=True),
                        waits=[vrecB, vrbA])
                    vrbB = op("vector", lambda e: e.tensor_copy(
                        rlb[:], ps[5][:]), waits=[mrbB, y_doneA])
                    y_doneB = op("vector",
                                 lambda e, h=h: e.scalar_tensor_tensor(
                                     yT[:, h * TOWN + 512:
                                        h * TOWN + 1024],
                                     ps[6][:], 1.0, rlb[:],
                                     ALU.mult, ALU.mult),
                                 waits=[mmy_last[1], vrbB])
                    sba[5] = vrbB
                    yl_ps[2] = y_doneA
                    yl_ps[3] = vrecA
                    yl_ps[6] = y_doneB
                    yl_ps[7] = vrecB
                    mrb_prev = mrbB
                    rlb_cons = y_doneB
            em.switch_end()
            prev_sid = sid
        jt = em.join(prev_sid, "tensor")
        jv = em.join(prev_sid, "vector")
        js = em.join(prev_sid, "scalar")
        y_done = jv
        ex_last = js
        for b in range(8):
            ps_last[b] = jv if b in (2, 3, 6, 7) else jv
        ps_last[5] = jv

        tmp6 = sb("tmp6", [128, 1024], F32, M0 + 0)
        xtile = sb("xtile", [128, 1024], F32, M0 + 4096)
        gtile = sb("gtile", [128, 1024], F32, M0 + 8192)
        bob = sb("bob", [128, D], F32, M0 + 12288)
        bo_row = sb("bo_row", [1, D], F32, M0 + 20480)
        t5g = ex_last

        tbo = op("sync", lambda e: e.dma_start(out=bo_row[:], in_=i_bor),
                 waits=[y_done, t5g])
        bob_toks = []
        for ec in range(4):
            mmb = op("tensor", lambda e, ec=ec: e.matmul(
                ps[6][:], ones1f[:], bo_row[0:1, ec * 512:(ec + 1) * 512],
                start=True, stop=True), waits=[tbo, ps_last[6]])
            tok = op("vector", lambda e, ec=ec: e.tensor_copy(
                bob[:, ec * 512:(ec + 1) * 512], ps[6][:]),
                waits=[mmb, y_done, t5g])
            ps_last[6] = tok
            bob_toks.append(tok)

        t6_cons = [None, None]
        xt_cons = [t5g, t5g]
        pbg = em.join(prev_sid, "tensor")
        gt_cons = [pbg, pbg]
        wo_rd = [None, None]
        st_last = None
        for ec in range(4):
            if ec >= 2:
                wo_tok[ec] = ld_wo_op(ec, [wo_rd[ec % 2]])
            for tt in range(8):
                opar = tt % 2
                sl, tl = tt // 4, tt % 4
                bank = opar
                mm = None
                for hh in range(H):
                    mm = op("tensor",
                            lambda e, hh=hh, sl=sl, tl=tl, ec=ec, bank=bank:
                            e.matmul(
                                ps[bank][:],
                                yT[:, hh * TOWN + sl * 512 + tl * 128:
                                   hh * TOWN + sl * 512 + (tl + 1) * 128],
                                wos[ec % 2][:, hh * 512:(hh + 1) * 512],
                                start=(hh == 0), stop=(hh == H - 1)),
                            waits=[wo_tok[ec], y_done, ps_last[bank]]
                            if hh == 0 else [])
                wo_rd[ec % 2] = mm
                ldx = op("sync", lambda e, tt=tt, ec=ec, opar=opar: e.dma_start(
                    out=xtile[:, opar * 512:(opar + 1) * 512],
                    in_=i_xown[tt * 128:(tt + 1) * 128,
                               ec * 512:(ec + 1) * 512]),
                    waits=[xt_cons[opar]])
                ldg = op("gpsimd", lambda e, tt=tt, ec=ec, opar=opar: e.dma_start(
                    out=gtile[:, opar * 512:(opar + 1) * 512],
                    in_=d_g[tt * 128:(tt + 1) * 128,
                            ec * 512:(ec + 1) * 512]),
                    waits=[gt_cons[opar], st_g_last])
                e1 = op("vector", lambda e, opar=opar, ec=ec, bank=bank: e.tensor_add(
                    tmp6[:, opar * 512:(opar + 1) * 512], ps[bank][:],
                    bob[:, ec * 512:(ec + 1) * 512]),
                    waits=[mm, t6_cons[opar]] + (bob_toks if ec == 0 else []))
                ps_last[bank] = e1
                e2 = op("vector", lambda e, opar=opar: e.tensor_mul(
                    tmp6[:, opar * 512:(opar + 1) * 512],
                    tmp6[:, opar * 512:(opar + 1) * 512],
                    gtile[:, opar * 512:(opar + 1) * 512]), waits=[ldg, e1])
                e3 = op("vector", lambda e, opar=opar: e.tensor_add(
                    tmp6[:, opar * 512:(opar + 1) * 512],
                    tmp6[:, opar * 512:(opar + 1) * 512],
                    xtile[:, opar * 512:(opar + 1) * 512]), waits=[ldx, e2])
                xt_cons[opar] = e3
                gt_cons[opar] = e3
                st_last = op("sync",
                             lambda e, tt=tt, ec=ec, opar=opar: e.dma_start(
                                 out=o_out[tt * 128:(tt + 1) * 128,
                                           ec * 512:(ec + 1) * 512],
                                 in_=tmp6[:, opar * 512:(opar + 1) * 512]),
                             waits=[e3])
                t6_cons[opar] = st_last

        if DEBUG:
            op("sync", lambda e: e.dma_start(out=o_QT, in_=QT[:]),
               waits=[st_last])
            op("sync", lambda e: e.dma_start(out=o_yT, in_=yT[:]),
               waits=[st_last])
        em.emit(nc, block, sems)
    return nc


def kernel(L0_attn_x, L0_attn_Wq, L0_attn_bq, L0_attn_Wk, L0_attn_bk,
           L0_attn_Wv, L0_attn_bv, L0_attn_Wo, L0_attn_bo,
           L0_ffn_w_gate, L0_ffn_w_up, L0_ffn_w_down,
           L0_ffn_s_gate, L0_ffn_s_up, L0_ffn_s_down):
    f32 = np.float32
    x = np.asarray(L0_attn_x, dtype=f32)
    Bsz, T, _ = x.shape

    wvhd = np.ascontiguousarray(
        np.asarray(L0_attn_Wv, f32).transpose(1, 0, 2).reshape(D, D))
    bqT = np.ascontiguousarray(
        np.broadcast_to(np.asarray(L0_attn_bq, f32).T, (128, H)))
    bkT = np.ascontiguousarray(
        np.broadcast_to(np.asarray(L0_attn_bk, f32).T, (128, H)))
    bvr = np.ascontiguousarray(np.asarray(L0_attn_bv, f32).reshape(1, D))
    bor = np.ascontiguousarray(np.asarray(L0_attn_bo, f32).reshape(1, D))
    sgpm = np.ascontiguousarray(
        np.asarray(L0_ffn_s_gate, f32).reshape(NII, 128).T)
    supm = np.ascontiguousarray(
        np.asarray(L0_ffn_s_up, f32).reshape(NII, 128).T)
    sdr = np.ascontiguousarray(np.asarray(L0_ffn_s_down, f32).reshape(1, D))
    dm = np.zeros((128, 4, CH), dtype=ml_dtypes.bfloat16)
    srow = np.arange(128)[:, None]
    tcol = np.arange(CH)[None, :]
    for k in range(4):
        dm[:, k, :] = ((k * 128 + srow) <= tcol).astype(ml_dtypes.bfloat16)
    dm = np.ascontiguousarray(dm.reshape(128, 4 * CH))

    shared = dict(
        wq=np.ascontiguousarray(np.asarray(L0_attn_Wq, f32)),
        wk=np.ascontiguousarray(np.asarray(L0_attn_Wk, f32)),
        wvhd=wvhd,
        wo=np.ascontiguousarray(np.asarray(L0_attn_Wo, f32)),
        bqT=bqT, bkT=bkT, bvr=bvr, bor=bor,
        wg=np.ascontiguousarray(np.asarray(L0_ffn_w_gate, np.int8)),
        wu=np.ascontiguousarray(np.asarray(L0_ffn_w_up, np.int8)),
        wd=np.ascontiguousarray(np.asarray(L0_ffn_w_down, np.int8)),
        sgpm=sgpm, supm=supm, sdr=sdr, dmask=dm,
    )

    in_maps = []
    metas = []
    for i in range(8):
        a, b = i, 7 - i
        rows = [x[0, a * CH:(a + 1) * CH], x[1, b * CH:(b + 1) * CH]]
        rows += [x[0, j * CH:(j + 1) * CH] for j in range(a)]
        rows += [x[1, j * CH:(j + 1) * CH] for j in range(b)]
        xkv = np.concatenate(rows, axis=0)
        big = np.full((2 * NPOS,), -1e9, dtype=f32)
        big[0] = 0.0
        big[2:2 + a] = 0.0
        big[NPOS + 1] = 0.0
        big[NPOS + 2 + a: NPOS + 2 + a + b] = 0.0
        m = dict(shared)
        m["xkvT"] = np.ascontiguousarray(xkv.T)
        m["xown"] = np.ascontiguousarray(xkv[:TOWN])
        m["bigoff"] = np.ascontiguousarray(
            np.broadcast_to(big[None, :], (128, 2 * NPOS)))
        in_maps.append(m)
        metas.append((a, b))

    nc = build_nc()
    res = run_bass_kernel_spmd(nc, in_maps, core_ids=list(range(8)))

    out = np.empty((2, T, D), dtype=f32)
    for i, (a, b) in enumerate(metas):
        oo = np.asarray(res.results[i]["out"], dtype=f32)
        out[0, a * CH:(a + 1) * CH] = oo[:CH]
        out[1, b * CH:(b + 1) * CH] = oo[CH:]
    return out
